# version 02
# baseline (speedup 1.0000x reference)
import math
import numpy as np
import concourse.bass as bass
import concourse.mybir as mybir
from concourse.bass_utils import run_bass_kernel_spmd

F32 = mybir.dt.float32
BF16 = mybir.dt.bfloat16
AF = mybir.ActivationFunctionType
OP = mybir.AluOpType

T = 2048
C = 2048
FF = 5632
H = 8
NT = 16
NJ = 4
NG = 16
NF = 44
VROW = 528
EPS_LN = 1e-5
EPS_RMS = 1e-6

ALL_PHASES = ("P1", "A1", "A2", "C", "D", "E1", "E2")


def make_consts():
    import jax
    import jax.numpy as jnp
    with jax.default_device(jax.devices("cpu")[0]):
        inv = 1.0 / 10000.0 ** (jnp.arange(0, 256, 2, dtype=jnp.float32) / 256)
        f = jnp.arange(T, dtype=jnp.float32)[:, None] * inv[None, :]
        cos_hi = np.ascontiguousarray(np.asarray(jnp.cos(f)).T)
        sin_hi = np.ascontiguousarray(np.asarray(jnp.sin(f)).T)
    cos_lo = np.zeros_like(cos_hi)
    sin_lo = np.zeros_like(sin_hi)
    m = np.zeros((128, 4 * 512), dtype=np.float32)
    r = np.arange(128)[:, None]
    c = np.arange(512)[None, :]
    for k in range(4):
        m[:, k * 512:(k + 1) * 512] = (c >= k * 128 + r).astype(np.float32)
    trig = np.array([[sx, sy, sz] for sx in (1.0, -1.0) for sy in (1.0, -1.0)
                     for sz in (1.0, -1.0)], dtype=np.float64) / math.sqrt(3.0)
    dirs64 = np.ascontiguousarray(trig.T)
    return {"cos_hi": cos_hi, "cos_lo": cos_lo, "sin_hi": sin_hi, "sin_lo": sin_lo,
            "bandmask": m, "dirs64": dirs64.astype(np.float32),
            "dirs_lo64": (dirs64 - dirs64.astype(np.float32)).astype(np.float32)}


class Ops:
    def __init__(self):
        self.ops = []

    def em(self, eng, fn, inc=None, waits=()):
        self.ops.append((eng, fn, inc, tuple(w for w in waits if w is not None)))

    def emit(self, block):
        by_eng = {}
        for op in self.ops:
            by_eng.setdefault(op[0], []).append(op)
        for eng, items in by_eng.items():
            def body(e, items=items):
                seen = {}
                for _, fn, inc, waits in items:
                    for sem, val in waits:
                        if val > 0 and seen.get(id(sem), -1) < val:
                            e.wait_ge(sem, val)
                            seen[id(sem)] = val
                    inst = fn(e)
                    if inc is not None:
                        inst.then_inc(inc[0], inc[1])
            getattr(block, eng)(body)


class Env:
    DMA_POOL = {"L": 6, "G": 4, "X": 2}

    def __init__(self, nc):
        self.nc = nc
        self._ctx = []
        self.S = {}
        self.V = {}
        for nm in ("Tn", "Vv", "Sc"):
            cm = nc.semaphore(nm)
            self.S[nm] = cm.__enter__()
            self._ctx.append(cm)
            self.V[nm] = 0
        self.dpool = {}
        self.dcnt = {}
        self.didx = {}
        for nm, r in self.DMA_POOL.items():
            self.dpool[nm] = []
            self.dcnt[nm] = []
            self.didx[nm] = 0
            for i in range(r):
                cm = nc.semaphore(f"{nm}{i}")
                self.dpool[nm].append(cm.__enter__())
                self._ctx.append(cm)
                self.dcnt[nm].append(0)
        self.mile = {}

    def sb(self, name, shape, dtype, ctx=None):
        cm = self.nc.sbuf_tensor(name, shape, dtype)
        h = cm.__enter__()
        (ctx if ctx is not None else self._ctx).append(cm)
        return h

    def psum(self, name):
        cm = self.nc.psum_tensor(name, [128, 512], F32)
        h = cm.__enter__()
        self._ctx.append(cm)
        return h

    def tok(self, nm):
        return (self.S[nm], self.V[nm])

    def _dma_slot(self, nm):
        i = self.didx[nm] % len(self.dpool[nm])
        self.didx[nm] += 1
        sem = self.dpool[nm][i]
        prev = self.dcnt[nm][i]
        self.dcnt[nm][i] = prev + 16
        return sem, prev

    def dma(self, o, outap, inap, waits=(), eng="sync", slow=False):
        nm = "G" if eng == "gpsimd" else "L"
        sem, prev = self._dma_slot(nm)
        o.em(eng, lambda e, a=outap, b=inap, s=slow: e.dma_start(
            out=a, in_=b, allow_slow_non_contiguous=s),
             inc=(sem, 16), waits=list(waits) + [(sem, prev)])
        return (sem, prev + 16)

    def dmaT(self, o, outap, inap, waits=()):
        sem, prev = self._dma_slot("X")
        o.em("sync", lambda e, a=outap, b=inap: e.dma_start_transpose(a, b),
             inc=(sem, 16), waits=list(waits) + [(sem, prev)])
        return (sem, prev + 16)

    def ven(self, o, fn, waits=()):
        prev = self.V["Vv"]
        self.V["Vv"] = prev + 1
        o.em("vector", fn, inc=(self.S["Vv"], 1),
             waits=list(waits) + [(self.S["Vv"], prev)])
        return (self.S["Vv"], prev + 1)

    def scn(self, o, fn, waits=()):
        prev = self.V["Sc"]
        self.V["Sc"] = prev + 1
        o.em("scalar", fn, inc=(self.S["Sc"], 1),
             waits=list(waits) + [(self.S["Sc"], prev)])
        return (self.S["Sc"], prev + 1)

    def ten(self, o, fn, waits=()):
        self.V["Tn"] += 1
        o.em("tensor", fn, inc=(self.S["Tn"], 1), waits=waits)
        return (self.S["Tn"], self.V["Tn"])

    def mm3(self, o, pb, lh, ll, rh, rl, start, stop, waits=(), inc=False):
        o.em("tensor", lambda e, a=pb, b=lh, c=rh, s=start:
             e.matmul(a, b, c, start=s, stop=False), waits=waits)
        o.em("tensor", lambda e, a=pb, b=lh, c=rl:
             e.matmul(a, b, c, start=False, stop=False))
        token = None
        if inc:
            self.V["Tn"] += 1
            token = (self.S["Tn"], self.V["Tn"])
        o.em("tensor", lambda e, a=pb, b=ll, c=rh, s=stop:
             e.matmul(a, b, c, start=False, stop=s),
             inc=(self.S["Tn"], 1) if inc else None)
        return token

    def qtok(self, nm):
        return [(s, c) for s, c in zip(self.dpool[nm], self.dcnt[nm]) if c > 0]

    def fence(self, o):
        o.em("sync", lambda e: e.nop(),
             waits=self.qtok("G") + self.qtok("L") + self.qtok("X"))

    def split(self, o, hi, lo, src, waits=(), hi_on_sc=True):
        if hi_on_sc:
            w1 = self.scn(o, lambda e, a=hi, b=src: e.copy(a, b), waits=waits)
        else:
            w1 = self.ven(o, lambda e, a=hi, b=src: e.tensor_copy(a, b), waits=waits)
        w2 = self.ven(o, lambda e, a=lo, b=src, c=hi:
                      e.tensor_tensor(a, b, c, OP.subtract), waits=[w1])
        return w1, w2


def build_nc(phases=ALL_PHASES):
    nc = bass.Bass("TRN2", target_bir_lowering=False)
    dt = {}

    def din(name, shape):
        dt[name] = nc.dram_tensor(name, shape, F32, kind="ExternalInput").ap()
    din("x", [T, C])
    din("Wq", [C, C]); din("Wk", [C, 512]); din("Wv", [C, 512]); din("Wo", [C, C])
    din("head_scales", [1, H])
    din("w_router", [C]); din("b_router", [1, 1]); din("norm1_w", [C])
    din("wv", [C, C]); din("bv", [C]); din("wo", [C, C]); din("norm2_w", [C])
    din("w_gate", [C, FF]); din("w_up", [C, FF]); din("w_down", [FF, C])
    din("cos_hi", [128, T]); din("cos_lo", [128, T])
    din("sin_hi", [128, T]); din("sin_lo", [128, T])
    din("bandmask", [128, 4 * 512]); din("dirs64", [3, 8]); din("dirs_lo64", [3, 8])
    out = nc.dram_tensor("out", [T, C], F32, kind="ExternalOutput").ap()

    def scr(name, shape, dtype):
        dt[name] = nc.dram_tensor(name, shape, dtype, kind="Internal").ap()
    for nm in ("S_xn_hi", "S_xn_lo", "S_q_hi", "S_q_lo", "S_oh_hi", "S_oh_lo"):
        scr(nm, [128, NG * T], BF16)
    scr("S_wv_hi", [128, NG * C], BF16); scr("S_wv_lo", [128, NG * C], BF16)
    scr("S_wo_hi", [128, NG * C], BF16); scr("S_wo_lo", [128, NG * C], BF16)
    scr("S_pr_hi", [128, NF * T], BF16); scr("S_pr_lo", [128, NF * T], BF16)
    scr("S_za", [T, C], F32)
    scr("S_ao", [T, C], F32)

    env = Env(nc)
    try:
        _program(nc, env, dt, out, phases)
    finally:
        for cm in reversed(env._ctx):
            try:
                cm.__exit__(None, None, None)
            except Exception:
                pass
    return nc


def _free(scope):
    for cm in reversed(scope):
        cm.__exit__(None, None, None)
    scope.clear()


def _program(nc, env, dt, out, phases):
    S = env.S
    sb = env.sb

    st = {nm: sb("st_" + nm, [128, NT], F32) for nm in ("rs", "u", "w", "g1", "rt", "rs2")}
    ones_row = sb("ones_row", [1, 128], F32)
    hs_row = sb("hs_row", [1, H], F32)
    hs1m_row = sb("hs1m_row", [1, H], F32)
    brc = sb("brc", [128, 1], F32)
    brs = sb("brs", [1, 1], F32)
    n1w_fm = sb("n1w_fm", [128, NG], F32)
    n2w_fm = sb("n2w_fm", [128, NG], F32)
    bvf = sb("bvf", [128, NG], F32)
    sm = sb("smallA", [128, 16], F32)
    P = [env.psum(f"pb{i}") for i in range(8)]

    sp = []
    wrbc = sb("wrbc", [128, C], F32, sp)
    with nc.Block("p0") as blk:
        o = Ops()
        tmp = sb("p0tmp", [128, C], F32, sp)
        w1 = env.dma(o, hs_row[:], dt["head_scales"])
        env.ven(o, lambda e: e.tensor_scalar(hs1m_row[:], hs_row[:], -1.0, 1.0,
                                             OP.mult, OP.add), waits=[w1])
        env.ven(o, lambda e: e.memset(ones_row[:], 1.0))
        wr = env.dma(o, tmp[0:1, 0:C], dt["w_router"].rearrange("(r c) -> r c", r=1))
        wb = env.dma(o, brs[:], dt["b_router"])
        for j in range(4):
            wm = env.ten(o, lambda e, j=j: e.matmul(
                P[0][:], ones_row[:], tmp[0:1, j * 512:(j + 1) * 512],
                start=True, stop=True), waits=[wr, env.tok("Vv")])
            env.ven(o, lambda e, j=j: e.tensor_copy(wrbc[:, j * 512:(j + 1) * 512], P[0][:]),
                    waits=[wm])
        wm = env.ten(o, lambda e: e.matmul(P[1][:, 0:1], ones_row[:], brs[:],
                                           start=True, stop=True),
                     waits=[wb, env.tok("Vv")])
        env.ven(o, lambda e: e.tensor_copy(brc[:], P[1][:, 0:1]), waits=[wm])
        env.dma(o, n1w_fm[:], dt["norm1_w"].rearrange("(g p) -> p g", p=128), slow=True)
        env.dma(o, n2w_fm[:], dt["norm2_w"].rearrange("(g p) -> p g", p=128), slow=True)
        env.dma(o, bvf[:], dt["bv"].rearrange("(g p) -> p g", p=128), slow=True)
        env.fence(o)
        o.emit(blk)

    env.mile["S_xnJ"] = {}
    if "P1" in phases:
      with nc.Block("p1") as blk:
        o = Ops()
        xb = [sb(f"p1x{i}", [128, C], F32, sp) for i in range(2)]
        xnb = sb("p1xn", [128, C], F32, sp)
        hib = [sb(f"p1hi{i}", [128, C], BF16, sp) for i in range(2)]
        lob = [sb(f"p1lo{i}", [128, C], BF16, sp) for i in range(2)]
        dump = sb("p1dump", [128, C], F32, sp)
        acc = sb("p1acc", [128, 4], F32, sp)
        stg_h = [sb(f"p1sh{i}", [128, NG * 512], BF16, sp) for i in range(2)]
        stg_l = [sb(f"p1sl{i}", [128, NG * 512], BF16, sp) for i in range(2)]
        xn_done = []
        tr_done = []
        stg_free = [None, None]
        for tau in range(NT):
            J = tau // 4
            bx = xb[tau % 2]
            wx = env.dma(o, bx[:], dt["x"][tau * 128:(tau + 1) * 128, :],
                         waits=[xn_done[tau - 2]] if tau >= 2 else ())
            ws1 = env.scn(o, lambda e, b=bx: e.activation(
                out=dump[:], in_=b[:], func=AF.Copy, accum_out=acc[:, 0:1]),
                waits=[wx, env.tok("Vv")])
            ws2 = env.scn(o, lambda e, b=bx: e.activation(
                out=dump[:], in_=b[:], func=AF.Square, accum_out=acc[:, 1:2]),
                waits=[ws1])
            mc, vc, vpe, sd, r0, t1, t2, t3, dn, g0 = (sm[:, k:k + 1] for k in range(10))
            env.ven(o, lambda e, a=mc: e.tensor_scalar(a, acc[:, 0:1], 1.0 / C, None, OP.mult),
                    waits=[ws1, ws2, env.tok("Sc")])
            env.ven(o, lambda e, a=vc: e.tensor_scalar(a, acc[:, 1:2], 1.0 / C, None, OP.mult))
            env.ven(o, lambda e, a=t3, b=mc, c=vc: e.scalar_tensor_tensor(
                a, b, b[:, 0:1], c, OP.mult, OP.subtract))
            env.ven(o, lambda e, a=vc, b=t3: e.tensor_scalar(a, b, -1.0, None, OP.mult))
            wv_ = env.ven(o, lambda e, a=vpe, b=vc: e.tensor_scalar(a, b, EPS_LN, None, OP.add))
            wsd = env.scn(o, lambda e, a=sd, b=vpe: e.sqrt(a, b), waits=[wv_])
            wrc = env.ven(o, lambda e, a=r0, b=sd: e.reciprocal(a, b), waits=[wsd])
            env.ven(o, lambda e, a=t1, b=r0: e.tensor_tensor(a, b, b, OP.mult), waits=[wrc])
            env.ven(o, lambda e, a=t2, b=t1, c=vpe: e.tensor_tensor(a, b, c, OP.mult))
            env.ven(o, lambda e, a=t3, b=t2: e.tensor_scalar(a, b, -0.5, 1.5, OP.mult, OP.add))
            rsc = st["rs"][:, tau:tau + 1]
            env.ven(o, lambda e, a=rsc, b=r0, c=t3: e.tensor_tensor(a, b, c, OP.mult))
            env.ven(o, lambda e, a=rsc, tau=tau: e.tensor_scalar(
                st["u"][:, tau:tau + 1], a, 1.0, None, OP.add))
            env.ven(o, lambda e, a=rsc, b=mc, tau=tau: e.scalar_tensor_tensor(
                st["w"][:, tau:tau + 1], b, -1.0, a, OP.mult, OP.mult))
            env.ven(o, lambda e, a=dn, b=vpe: e.reciprocal(a, b))
            env.ven(o, lambda e, a=t1, b=vc, c=dn: e.tensor_tensor(a, b, c, OP.mult))
            wv_ = env.ven(o, lambda e, a=t2, b=t1: e.tensor_scalar(a, b, EPS_RMS, None, OP.add))
            wsd = env.scn(o, lambda e, a=sd, b=t2: e.sqrt(a, b), waits=[wv_])
            wrc = env.ven(o, lambda e, a=g0, b=sd: e.reciprocal(a, b), waits=[wsd])
            env.ven(o, lambda e, a=t1, b=g0: e.tensor_tensor(a, b, b, OP.mult), waits=[wrc])
            env.ven(o, lambda e, a=t1, b=t2: e.tensor_tensor(a, a, b, OP.mult))
            env.ven(o, lambda e, a=t1: e.tensor_scalar(a, a, -0.5, 1.5, OP.mult, OP.add))
            env.ven(o, lambda e, a=g0, b=t1, tau=tau: e.tensor_tensor(
                st["g1"][:, tau:tau + 1], a, b, OP.mult))
            wxn = env.ven(o, lambda e, b=bx, a=rsc, tau=tau: e.tensor_scalar(
                xnb[:], b[:], a, st["w"][:, tau:tau + 1], OP.mult, OP.add))
            xn_done.append(wxn)
            wrt = env.ven(o, lambda e: e.scalar_tensor_tensor(
                dump[:], xnb[:], 1.0, wrbc[:], OP.mult, OP.mult, accum_out=acc[:, 2:3]),
                waits=[wxn])
            env.ven(o, lambda e: e.tensor_scalar(acc[:, 3:4], acc[:, 2:3], brc[:, 0:1], 0.0,
                                                 OP.add, OP.is_gt), waits=[wrt])
            env.ven(o, lambda e, tau=tau: e.tensor_copy(st["rt"][:, tau:tau + 1], acc[:, 3:4]))
            bh, bl = hib[tau % 2], lob[tau % 2]
            wprev = tr_done[tau - 2] if tau >= 2 else []
            whh, wll = env.split(o, bh[:], bl[:], xnb[:], waits=[wxn] + wprev)
            sj = J % 2
            dst_h = stg_h[sj][:].rearrange("p (g t) -> p g t", g=NG)[
                :, :, (tau % 4) * 128:(tau % 4) * 128 + 128]
            dst_l = stg_l[sj][:].rearrange("p (g t) -> p g t", g=NG)[
                :, :, (tau % 4) * 128:(tau % 4) * 128 + 128]
            wtw = stg_free[sj] if (tau % 4 == 0 and stg_free[sj]) else []
            wt1 = env.dmaT(o, dst_h, bh[:], waits=[whh, wll] + wtw)
            wt2 = env.dmaT(o, dst_l, bl[:], waits=[wll] + wtw)
            tr_done.append([wt1, wt2])
            if tau % 4 == 3:
                wsh = env.dma(o, dt["S_xn_hi"][:].rearrange("p (g t) -> p g t", g=NG)
                              [:, :, J * 512:(J + 1) * 512],
                              stg_h[sj][:].rearrange("p (g t) -> p g t", g=NG),
                              waits=env.qtok("X"), eng="gpsimd")
                wst = env.dma(o, dt["S_xn_lo"][:].rearrange("p (g t) -> p g t", g=NG)
                              [:, :, J * 512:(J + 1) * 512],
                              stg_l[sj][:].rearrange("p (g t) -> p g t", g=NG),
                              eng="gpsimd")
                stg_free[sj] = [wsh, wst]
                env.mile["S_xnJ"][J] = [wsh, wst]
        env.fence(o)
        o.emit(blk)
    _free(sp)

    scA = []
    K_hi = sb("K_hi", [128, 4 * T], BF16, scA)
    K_lo = sb("K_lo", [128, 4 * T], BF16, scA)
    V_hi = sb("V_hi", [128, NT * VROW], BF16, scA)
    V_lo = sb("V_lo", [128, NT * VROW], BF16, scA)
    mask4 = sb("mask4", [128, 4 * 512], F32, scA)
    dirs_h = sb("dirs_h", [3, 8], BF16, scA)
    dirs_l = sb("dirs_l", [3, 8], BF16, scA)
    cosT = sb("cosT", [128, T], F32, scA)
    sinT = sb("sinT", [128, T], F32, scA)

    if "A1" in phases:
      scA1 = []
      with nc.Block("a1") as blk:
        o = Ops()
        xnJ_h = sb("a1xnh", [128, NG * 512], BF16, scA1)
        xnJ_l = sb("a1xnl", [128, NG * 512], BF16, scA1)
        wvh = sb("a1wvh", [128, NG * 512], BF16, scA1)
        wvl = sb("a1wvl", [128, NG * 512], BF16, scA1)
        wsg = [sb(f"a1w{i}", [128, 1024], F32, scA1) for i in range(3)]
        wbh = [sb(f"a1bh{i}", [128, 1024], BF16, scA1) for i in range(3)]
        wbl = [sb(f"a1bl{i}", [128, 1024], BF16, scA1) for i in range(3)]
        rop = [sb(f"a1rop{i}", [128, 512], F32, scA1) for i in range(4)]
        qsth = sb("a1qh", [128, 2 * 512], BF16, scA1)
        qstl = sb("a1ql", [128, 2 * 512], BF16, scA1)
        env.dma(o, mask4[:], dt["bandmask"])
        w1 = env.dma(o, rop[0][:], dt["cos_hi"][:, 0:512])
        w1b = env.dma(o, rop[1][:], dt["cos_hi"][:, 512:1024])
        w1c = env.dma(o, rop[2][:], dt["cos_hi"][:, 1024:1536])
        w1d = env.dma(o, rop[3][:], dt["cos_hi"][:, 1536:2048])
        w2 = env.dma(o, cosT[:], dt["cos_lo"])
        wc = env.ven(o, lambda e: e.tensor_tensor(cosT[:, 0:512], rop[0][:],
                                                  cosT[:, 0:512], OP.add), waits=[w1, w2])
        env.ven(o, lambda e: e.tensor_tensor(cosT[:, 512:1024], rop[1][:],
                                             cosT[:, 512:1024], OP.add), waits=[w1b])
        env.ven(o, lambda e: e.tensor_tensor(cosT[:, 1024:1536], rop[2][:],
                                             cosT[:, 1024:1536], OP.add), waits=[w1c])
        wc = env.ven(o, lambda e: e.tensor_tensor(cosT[:, 1536:2048], rop[3][:],
                                                  cosT[:, 1536:2048], OP.add), waits=[w1d])
        w1 = env.dma(o, rop[0][:], dt["sin_hi"][:, 0:512], waits=[wc])
        w1b = env.dma(o, rop[1][:], dt["sin_hi"][:, 512:1024])
        w1c = env.dma(o, rop[2][:], dt["sin_hi"][:, 1024:1536])
        w1d = env.dma(o, rop[3][:], dt["sin_hi"][:, 1536:2048])
        w2 = env.dma(o, sinT[:], dt["sin_lo"])
        env.ven(o, lambda e: e.tensor_tensor(sinT[:, 0:512], rop[0][:],
                                             sinT[:, 0:512], OP.add), waits=[w1, w2])
        env.ven(o, lambda e: e.tensor_tensor(sinT[:, 512:1024], rop[1][:],
                                             sinT[:, 512:1024], OP.add), waits=[w1b])
        env.ven(o, lambda e: e.tensor_tensor(sinT[:, 1024:1536], rop[2][:],
                                             sinT[:, 1024:1536], OP.add), waits=[w1c])
        wsin = env.ven(o, lambda e: e.tensor_tensor(sinT[:, 1536:2048], rop[3][:],
                                                    sinT[:, 1536:2048], OP.add), waits=[w1d])
        wd1 = env.dma(o, wsg[0][0:3, 0:8], dt["dirs64"], waits=[wsin])
        wd2 = env.dma(o, wsg[0][0:3, 8:16], dt["dirs_lo64"])
        wdh = env.scn(o, lambda e: e.copy(dirs_h[:], wsg[0][0:3, 0:8]), waits=[wd1, wd2])
        env.ven(o, lambda e: e.tensor_tensor(rop[1][0:3, 0:8], wsg[0][0:3, 0:8],
                                             dirs_h[:], OP.subtract), waits=[wdh])
        env.ven(o, lambda e: e.tensor_tensor(rop[1][0:3, 0:8], rop[1][0:3, 0:8],
                                             wsg[0][0:3, 8:16], OP.add))
        wdirs = env.ven(o, lambda e: e.tensor_copy(dirs_l[:], rop[1][0:3, 0:8]))
        wv_spl = [wdirs]
        for g in range(NG):
            slot = g % 3
            wl_ = env.dma(o, wsg[slot][:, 0:512], dt["Wv"][g * 128:(g + 1) * 128, :],
                          waits=[wv_spl[-3]] if len(wv_spl) >= 3 else [wdirs])
            _, w2_ = env.split(o, wvh[:, g * 512:(g + 1) * 512],
                               wvl[:, g * 512:(g + 1) * 512], wsg[slot][:, 0:512],
                               waits=[wl_])
            wv_spl.append(w2_)
        for tau in range(NT):
            env.ven(o, lambda e, tau=tau: e.memset(
                V_hi[:, tau * VROW + 512:tau * VROW + 513], 1.0))
            env.ven(o, lambda e, tau=tau: e.memset(
                V_lo[:, tau * VROW + 512:tau * VROW + 513], 0.0))
        wones = env.tok("Vv")
        pfree = wones
        wring = wv_spl[-1]
        qst_free = None
        for J in range(NJ):
            wxn1 = env.dma(o, xnJ_h[:].rearrange("p (g t) -> p g t", g=NG),
                           dt["S_xn_hi"][:].rearrange("p (g t) -> p g t", g=NG)
                           [:, :, J * 512:(J + 1) * 512],
                           waits=(env.mile["S_xnJ"].get(J) or []) + [env.tok("Tn")])
            wxn2 = env.dma(o, xnJ_l[:].rearrange("p (g t) -> p g t", g=NG),
                           dt["S_xn_lo"][:].rearrange("p (g t) -> p g t", g=NG)
                           [:, :, J * 512:(J + 1) * 512])
            for ti in range(4):
                tau = J * 4 + ti
                for g in range(NG):
                    env.mm3(o, P[ti][:],
                            xnJ_h[:, g * 512 + ti * 128: g * 512 + ti * 128 + 128],
                            xnJ_l[:, g * 512 + ti * 128: g * 512 + ti * 128 + 128],
                            wvh[:, g * 512:(g + 1) * 512], wvl[:, g * 512:(g + 1) * 512],
                            start=(g == 0), stop=(g == NG - 1),
                            waits=[wxn1, wxn2, wv_spl[-1], wones, pfree] if g == 0 else (),
                            inc=(g == NG - 1))
            wvm = env.tok("Tn")
            for ti in range(4):
                tau = J * 4 + ti
                env.split(o, V_hi[:, tau * VROW:tau * VROW + 512],
                          V_lo[:, tau * VROW:tau * VROW + 512], P[ti][:], waits=[wvm])
            kmm = []
            for g in range(NG):
                slot = g % 3
                wl_ = env.dma(o, wsg[slot][:, 0:512], dt["Wk"][g * 128:(g + 1) * 128, :],
                              waits=[wring] if g == 0 else
                              ([kmm[g - 3]] if g >= 3 else ()))
                whh, wll = env.split(o, wbh[slot][:, 0:512], wbl[slot][:, 0:512],
                                     wsg[slot][:, 0:512], waits=[wl_])
                for kap in range(4):
                    env.mm3(o, P[4 + kap][:],
                            wbh[slot][:, kap * 128:(kap + 1) * 128],
                            wbl[slot][:, kap * 128:(kap + 1) * 128],
                            xnJ_h[:, g * 512:(g + 1) * 512],
                            xnJ_l[:, g * 512:(g + 1) * 512],
                            start=(g == 0), stop=(g == NG - 1),
                            waits=([wll] + ([pfree] if g == 0 else []))
                            if kap == 0 else (),
                            inc=(kap == 3))
                kmm.append(env.tok("Tn"))
            wkm = env.tok("Tn")
            for g2 in range(2):
                pa, pb_ = P[4 + 2 * g2], P[4 + 2 * g2 + 1]
                cs = cosT[:, J * 512:(J + 1) * 512]
                sn = sinT[:, J * 512:(J + 1) * 512]
                env.ven(o, lambda e, p=pa, c=cs: e.tensor_tensor(rop[0][:], p[:], c, OP.mult),
                        waits=[wkm])
                env.ven(o, lambda e, p=pb_, s=sn: e.tensor_tensor(rop[1][:], p[:], s, OP.mult))
                env.ven(o, lambda e, p=pb_, c=cs: e.tensor_tensor(rop[2][:], p[:], c, OP.mult))
                env.ven(o, lambda e, p=pa, s=sn: e.tensor_tensor(rop[3][:], p[:], s, OP.mult))
                wA = env.ven(o, lambda e: e.tensor_tensor(rop[0][:], rop[0][:], rop[1][:],
                                                          OP.subtract))
                wB = env.ven(o, lambda e: e.tensor_tensor(rop[2][:], rop[2][:], rop[3][:],
                                                          OP.add))
                kA = (2 * g2) * T + J * 512
                kB = (2 * g2 + 1) * T + J * 512
                env.split(o, K_hi[:, kA:kA + 512], K_lo[:, kA:kA + 512], rop[0][:], waits=[wA])
                env.split(o, K_hi[:, kB:kB + 512], K_lo[:, kB:kB + 512], rop[2][:], waits=[wB])
            pfree = env.tok("Vv")
            wring = wkm
            for hq in range(2):
                qmm = []
                for g in range(NG):
                    slot = g % 3
                    wl_ = env.dma(o, wsg[slot][:],
                                  dt["Wq"][g * 128:(g + 1) * 128,
                                           hq * 1024:(hq + 1) * 1024],
                                  waits=[wring] if g == 0 else
                                  ([qmm[g - 3]] if g >= 3 else ()))
                    whh, wll = env.split(o, wbh[slot][:], wbl[slot][:], wsg[slot][:],
                                         waits=[wl_])
                    for hh in range(4):
                        for kk in range(2):
                            env.mm3(o, P[hh * 2 + kk][:],
                                    wbh[slot][:, hh * 256 + kk * 128:
                                             hh * 256 + kk * 128 + 128],
                                    wbl[slot][:, hh * 256 + kk * 128:
                                             hh * 256 + kk * 128 + 128],
                                    xnJ_h[:, g * 512:(g + 1) * 512],
                                    xnJ_l[:, g * 512:(g + 1) * 512],
                                    start=(g == 0), stop=(g == NG - 1),
                                    waits=([wll] + ([pfree] if g == 0 else []))
                                    if (hh == 0 and kk == 0) else (),
                                    inc=(hh == 3 and kk == 1))
                    qmm.append(env.tok("Tn"))
                wqm = env.tok("Tn")
                for hh in range(4):
                    h = hq * 4 + hh
                    pa, pb_ = P[hh * 2], P[hh * 2 + 1]
                    cs = cosT[:, J * 512:(J + 1) * 512]
                    sn = sinT[:, J * 512:(J + 1) * 512]
                    wqf = qst_free if qst_free else []
                    env.ven(o, lambda e, p=pa, c=cs: e.tensor_tensor(rop[0][:], p[:], c,
                                                                     OP.mult),
                            waits=[wqm] + wqf)
                    env.ven(o, lambda e, p=pb_, s=sn: e.tensor_tensor(rop[1][:], p[:], s,
                                                                      OP.mult))
                    env.ven(o, lambda e, p=pb_, c=cs: e.tensor_tensor(rop[2][:], p[:], c,
                                                                      OP.mult))
                    env.ven(o, lambda e, p=pa, s=sn: e.tensor_tensor(rop[3][:], p[:], s,
                                                                     OP.mult))
                    wA = env.ven(o, lambda e: e.tensor_tensor(rop[0][:], rop[0][:], rop[1][:],
                                                              OP.subtract))
                    wB = env.ven(o, lambda e: e.tensor_tensor(rop[2][:], rop[2][:], rop[3][:],
                                                              OP.add))
                    env.split(o, qsth[:, 0:512], qstl[:, 0:512], rop[0][:], waits=[wA])
                    _, wsp = env.split(o, qsth[:, 512:1024], qstl[:, 512:1024], rop[2][:],
                                       waits=[wB])
                    wq_a = env.dma(o, dt["S_q_hi"][:].rearrange("p (g t) -> p g t", g=NG)
                                   [:, 2 * h:2 * h + 2, J * 512:(J + 1) * 512],
                                   qsth[:].rearrange("p (g t) -> p g t", g=2),
                                   waits=[wsp], eng="gpsimd")
                    wq_b = env.dma(o, dt["S_q_lo"][:].rearrange("p (g t) -> p g t", g=NG)
                                   [:, 2 * h:2 * h + 2, J * 512:(J + 1) * 512],
                                   qstl[:].rearrange("p (g t) -> p g t", g=2),
                                   eng="gpsimd")
                    qst_free = [wq_a, wq_b]
                pfree = env.tok("Vv")
                wring = wqm
        env.mile["S_q"] = qst_free
        env.fence(o)
        o.emit(blk)
      _free(scA1)

    if "A2" in phases:
      scA2 = []
      with nc.Block("a2") as blk:
        o = Ops()
        qb_h = [sb(f"a2qh{i}", [128, 2 * T], BF16, scA2) for i in range(2)]
        qb_l = [sb(f"a2ql{i}", [128, 2 * T], BF16, scA2) for i in range(2)]
        eb = [sb(f"a2eb{i}", [128, 512], F32, scA2) for i in range(2)]
        ebm = [sb(f"a2ebm{i}", [128, 512], F32, scA2) for i in range(2)]
        e_hi = [sb(f"a2eh{i}", [128, 512], BF16, scA2) for i in range(2)]
        e_lo = [sb(f"a2el{i}", [128, 512], BF16, scA2) for i in range(2)]
        of32 = [sb(f"a2of{i}", [128, 512], F32, scA2) for i in range(2)]
        og32 = [sb(f"a2og{i}", [128, 512], F32, scA2) for i in range(2)]
        ohf = sb("a2ohf", [128, 512], F32, scA2)
        oh_hi = sb("a2ohh", [128, 512], BF16, scA2)
        oh_lo = sb("a2ohl", [128, 512], BF16, scA2)
        kp_row = sb("a2kp", [1, T], F32, scA2)
        qp_row = sb("a2qp", [1, T], F32, scA2)
        rr = sb("a2rr", [1, 512], F32, scA2)
        rrg = sb("a2rrg", [1, 512], F32, scA2)
        arow = sb("a2ar", [1, 512], F32, scA2)
        brow = sb("a2br", [1, 512], F32, scA2)
        pS = [P[0], P[1]]; pO = [P[2], P[3]]; pR = P[4]; pB = P[5]
        q_free = [None, None]
        comb_prev = None
        oh_store_prev = None
        for h in range(H):
            g2 = h // 4
            qh, ql = qb_h[h % 2], qb_l[h % 2]
            wq1 = env.dma(o, qh[:], dt["S_q_hi"][:, (2 * h) * T:(2 * h + 2) * T],
                          waits=(env.mile.get("S_q") or []) + (
                              [q_free[h % 2]] if q_free[h % 2] else []))
            wq2 = env.dma(o, ql[:], dt["S_q_lo"][:, (2 * h) * T:(2 * h + 2) * T])
            whd = env.tok("Vv")
            wkq = []
            for J in range(NJ):
                b0 = (2 * g2) * T + J * 512
                o.em("tensor", lambda e, b=b0, h=h: e.matmul(
                    pR[0:1, :], dirs_h[:, h:h + 1], K_hi[0:3, b:b + 512],
                    start=True, stop=False),
                    waits=[whd] + ([wkq[-1]] if wkq else []))
                o.em("tensor", lambda e, b=b0, h=h: e.matmul(
                    pR[0:1, :], dirs_h[:, h:h + 1], K_lo[0:3, b:b + 512],
                    start=False, stop=False))
                wm = env.ten(o, lambda e, b=b0, h=h: e.matmul(
                    pR[0:1, :], dirs_l[:, h:h + 1], K_hi[0:3, b:b + 512],
                    start=False, stop=True))
                wkq.append(env.ven(o, lambda e, J=J: e.tensor_copy(
                    kp_row[0:1, J * 512:(J + 1) * 512], pR[0:1, :]), waits=[wm]))
            for J in range(NJ):
                o.em("tensor", lambda e, J=J, q=qh, h=h: e.matmul(
                    pR[0:1, :], dirs_h[:, h:h + 1], q[0:3, J * 512:J * 512 + 512],
                    start=True, stop=False), waits=[wq1, wq2, wkq[-1]])
                o.em("tensor", lambda e, J=J, q=ql, h=h: e.matmul(
                    pR[0:1, :], dirs_h[:, h:h + 1], q[0:3, J * 512:J * 512 + 512],
                    start=False, stop=False))
                wm = env.ten(o, lambda e, J=J, q=qh, h=h: e.matmul(
                    pR[0:1, :], dirs_l[:, h:h + 1], q[0:3, J * 512:J * 512 + 512],
                    start=False, stop=True))
                wkq.append(env.ven(o, lambda e, J=J: e.tensor_copy(
                    qp_row[0:1, J * 512:(J + 1) * 512], pR[0:1, :]), waits=[wm]))
            w_rows = env.tok("Vv")
            for j in range(NJ):
                ni = 4 * j + 4
                evac_toks = []
                ev_sw = []
                for sweep in range(2):
                    exp_tok = []
                    split_tok = []
                    for i in range(ni):
                        ps = pS[i % 2]
                        wbank = [exp_tok[i - 2]] if i >= 2 else (
                            [comb_prev] if comb_prev else [])
                        if sweep == 0:
                            for dc in range(2):
                                env.mm3(o, ps[:],
                                        K_hi[:, (2 * g2 + dc) * T + i * 128:
                                             (2 * g2 + dc) * T + i * 128 + 128],
                                        K_lo[:, (2 * g2 + dc) * T + i * 128:
                                             (2 * g2 + dc) * T + i * 128 + 128],
                                        qh[:, dc * T + j * 512:dc * T + j * 512 + 512],
                                        ql[:, dc * T + j * 512:dc * T + j * 512 + 512],
                                        start=(dc == 0), stop=(dc == 1),
                                        waits=([wq1, wq2] + wbank) if dc == 0 else (),
                                        inc=(dc == 1))
                        else:
                            env.ten(o, lambda e, i=i, j=j, p=ps: e.matmul(
                                p[:], kp_row[0:1, i * 128:(i + 1) * 128],
                                qp_row[0:1, j * 512:(j + 1) * 512], start=True, stop=True),
                                waits=[w_rows] + wbank + (ev_sw if i < 2 else []))
                        wmm = env.tok("Tn")
                        scl = 0.0625 if sweep == 0 else 1.0
                        wex = env.scn(o, lambda e, p=ps, b=eb[i % 2], s=scl: e.activation(
                            out=b[:], in_=p[:], func=AF.Exp, scale=s),
                            waits=[wmm] + ([split_tok[i - 2]] if i >= 2 else []))
                        exp_tok.append(wex)
                        ksl = i - 4 * j
                        if 0 <= ksl <= 3:
                            wmk = env.ven(o, lambda e, k=ksl, a=ebm[i % 2], b=eb[i % 2]:
                                          e.tensor_tensor(a[:], b[:],
                                                          mask4[:, k * 512:(k + 1) * 512],
                                                          OP.mult), waits=[wex])
                            esrc = ebm[i % 2]
                        else:
                            wmk = wex
                            esrc = eb[i % 2]
                        slot = i % 2
                        whh, wll = env.split(o, e_hi[slot][:], e_lo[slot][:], esrc[:],
                                             waits=[wmk])
                        split_tok.append(wll)
                        base = i * VROW + g2 * 256
                        for dc in range(2):
                            env.mm3(o, pO[dc][:],
                                    V_hi[:, base + dc * 128: base + dc * 128 + 128],
                                    V_lo[:, base + dc * 128: base + dc * 128 + 128],
                                    e_hi[slot][:], e_lo[slot][:],
                                    start=(i == 0), stop=(i == ni - 1),
                                    waits=[whh, wll], inc=False)
                        o.em("tensor", lambda e, i=i, s=slot: e.matmul(
                            pR[0:1, :], V_hi[:, i * VROW + 512:i * VROW + 513],
                            e_hi[s][:], start=(i == 0), stop=False), waits=[whh])
                        env.ten(o, lambda e, i=i, s=slot, last=(i == ni - 1): e.matmul(
                            pR[0:1, :], V_hi[:, i * VROW + 512:i * VROW + 513],
                            e_lo[s][:], start=False, stop=last), waits=[wll])
                    wpv = env.tok("Tn")
                    dsts = of32 if sweep == 0 else og32
                    rdst = rr if sweep == 0 else rrg
                    wcb = [comb_prev] if comb_prev else []
                    ev0 = env.scn(o, lambda e, d=dsts: e.copy(d[0][:], pO[0][:]),
                                  waits=[wpv] + wcb)
                    ev1 = env.scn(o, lambda e, d=dsts: e.copy(d[1][:], pO[1][:]))
                    evr = env.ven(o, lambda e, a=rdst: e.reciprocal(a[:], pR[0:1, :]),
                                  waits=[wpv] + wcb)
                    evac_toks += [ev0, ev1, evr]
                    ev_sw = [ev0, ev1, evr]
                wa = env.ven(o, lambda e, h=h: e.tensor_scalar(
                    arow[:], rr[:], hs1m_row[0:1, h:h + 1], None, OP.mult), waits=evac_toks)
                wb = env.ven(o, lambda e, h=h: e.tensor_scalar(
                    brow[:], rrg[:], hs_row[0:1, h:h + 1], None, OP.mult))
                wba = env.ten(o, lambda e: e.matmul(pB[:], ones_row[:], arow[0:1, :],
                                                    start=True, stop=True), waits=[wa])
                wt1a = env.ven(o, lambda e: e.tensor_tensor(of32[0][:], of32[0][:], pB[:],
                                                            OP.mult), waits=[wba])
                wt1b = env.ven(o, lambda e: e.tensor_tensor(of32[1][:], of32[1][:], pB[:],
                                                            OP.mult))
                wbb = env.ten(o, lambda e: e.matmul(pB[:], ones_row[:], brow[0:1, :],
                                                    start=True, stop=True),
                              waits=[wb, wt1a, wt1b])
                for dc in range(2):
                    wt2 = env.ven(o, lambda e, dc=dc: e.tensor_tensor(
                        og32[dc][:], og32[dc][:], pB[:], OP.mult), waits=[wbb])
                    wt3 = env.ven(o, lambda e, dc=dc: e.tensor_tensor(
                        ohf[:], of32[dc][:], og32[dc][:], OP.add))
                    whh, wll = env.split(o, oh_hi[:], oh_lo[:], ohf[:],
                                         waits=[wt3] + (oh_store_prev or []))
                    go = (2 * h + dc) * T + j * 512
                    wo_a = env.dma(o, dt["S_oh_hi"][:, go:go + 512], oh_hi[:],
                                   waits=[whh, wll], eng="gpsimd")
                    wo_b = env.dma(o, dt["S_oh_lo"][:, go:go + 512], oh_lo[:],
                                   eng="gpsimd")
                    oh_store_prev = [wo_a, wo_b]
                comb_prev = env.tok("Vv")
            q_free[h % 2] = env.tok("Tn")
        env.mile["S_oh"] = env.qtok("G")
        env.fence(o)
        o.emit(blk)
      _free(scA2)
    _free(scA)

    scCD = []
    h2fm_hi = sb("h2fm_hi", [128, NG * T], BF16, scCD)
    h2fm_lo = sb("h2fm_lo", [128, NG * T], BF16, scCD)
    if "C" in phases:
      scC = []
      with nc.Block("c") as blk:
        o = Ops()
        wbh = [sb(f"cwbh{i}", [128, C], BF16, scC) for i in range(2)]
        wbl = [sb(f"cwbl{i}", [128, C], BF16, scC) for i in range(2)]
        xnP_h = sb("cxnh", [128, NG * 128], BF16, scC)
        xnP_l = sb("cxnl", [128, NG * 128], BF16, scC)
        vtm_hi = sb("cvth", [128, C], BF16, scC)
        vtm_lo = sb("cvtl", [128, C], BF16, scC)
        vfm_hi = sb("cvfh", [128, NG * 128], BF16, scC)
        vfm_lo = sb("cvfl", [128, NG * 128], BF16, scC)
        zaw = sb("czaw", [128, C], F32, scC)
        xcb = sb("cxb", [128, C], F32, scC)
        h2h = sb("ch2h", [128, C], BF16, scC)
        h2l = sb("ch2l", [128, C], BF16, scC)
        c0_bc = sb("c0_bc", [128, C], BF16, scC)
        acc = sb("cacc", [128, 2], F32, scC)
        prev = {"lo": None, "mm": None, "st": None}
        for g in range(NG):
            slot = g % 2
            wl_ = env.dma(o, zaw[:], dt["wo"][g * 128:(g + 1) * 128, :],
                          waits=[prev["lo"], prev["mm"]])
            whh, wll = env.split(o, wbh[slot][:], wbl[slot][:], zaw[:],
                                 waits=[wl_] + (prev["st"] or []))
            for dtl in range(4):
                env.ten(o, lambda e, g=g, d=dtl: e.matmul(
                    P[4 + d][0:1, :], bvf[:, g:g + 1], zaw[:, d * 512:(d + 1) * 512],
                    start=(g == 0), stop=(g == NG - 1)), waits=[wl_])
            prev["mm"] = env.tok("Tn")
            prev["lo"] = wll
            ws_a = env.dma(o, dt["S_wo_hi"][:, g * C:(g + 1) * C], wbh[slot][:],
                           waits=[whh, wll], eng="gpsimd")
            ws_b = env.dma(o, dt["S_wo_lo"][:, g * C:(g + 1) * C], wbl[slot][:],
                           eng="gpsimd")
            prev["st"] = [ws_a, ws_b]
        wc0 = env.tok("Tn")
        for dtl in range(4):
            env.ven(o, lambda e, d=dtl: e.tensor_copy(
                zaw[0:1, d * 512:(d + 1) * 512], P[4 + d][0:1, :]), waits=[wc0])
        wrow = env.tok("Vv")
        for dtl in range(4):
            wm = env.ten(o, lambda e, d=dtl: e.matmul(
                P[4 + d][:], ones_row[:], zaw[0:1, d * 512:(d + 1) * 512],
                start=True, stop=True), waits=[wrow])
            env.ven(o, lambda e, d=dtl: e.tensor_copy(c0_bc[:, d * 512:(d + 1) * 512],
                                                      P[4 + d][:]), waits=[wm])
        wc0bc = env.tok("Vv")
        for g in range(NG):
            slot = g % 2
            wl_ = env.dma(o, zaw[:], dt["wv"][g * 128:(g + 1) * 128, :],
                          waits=[prev["lo"], wc0bc if g == 0 else None])
            wsc = env.ven(o, lambda e, g=g: e.tensor_scalar(
                zaw[:], zaw[:], n1w_fm[:, g:g + 1], None, OP.mult), waits=[wl_])
            whh, wll = env.split(o, wbh[slot][:], wbl[slot][:], zaw[:],
                                 waits=[wsc] + (prev["st"] or []))
            prev["lo"] = wll
            ws_a = env.dma(o, dt["S_wv_hi"][:, g * C:(g + 1) * C], wbh[slot][:],
                           waits=[whh, wll], eng="gpsimd")
            ws_b = env.dma(o, dt["S_wv_lo"][:, g * C:(g + 1) * C], wbl[slot][:],
                           eng="gpsimd")
            prev["st"] = [ws_a, ws_b]
        env.mile["S_wv"] = list(prev["st"])
        h2_tr = {}
        pfree = wc0bc
        wb_free = prev["st"]
        za_store = {}
        for tau in range(NT):
            J = tau // 4
            pb0 = (tau % 2) * 4
            wxn1 = env.dma(o, xnP_h[:].rearrange("p (g t) -> p g t", g=NG),
                           dt["S_xn_hi"][:].rearrange("p (g t) -> p g t", g=NG)
                           [:, :, tau * 128:(tau + 1) * 128],
                           waits=(env.mile["S_xnJ"].get(J) or [])
                           + (env.mile.get("S_wv") or []) + [env.tok("Tn")])
            wxn2 = env.dma(o, xnP_l[:].rearrange("p (g t) -> p g t", g=NG),
                           dt["S_xn_lo"][:].rearrange("p (g t) -> p g t", g=NG)
                           [:, :, tau * 128:(tau + 1) * 128])
            mmg = []
            for g in range(NG):
                slot = g % 2
                wws = (env.mile.get("S_wv") or []) + ([mmg[g - 2]] if g >= 2 else list(wb_free))
                wl1 = env.dma(o, wbh[slot][:], dt["S_wv_hi"][:, g * C:(g + 1) * C],
                              waits=wws)
                wl2 = env.dma(o, wbl[slot][:], dt["S_wv_lo"][:, g * C:(g + 1) * C])
                for et in range(4):
                    env.mm3(o, P[pb0 + et][:],
                            xnP_h[:, g * 128:(g + 1) * 128],
                            xnP_l[:, g * 128:(g + 1) * 128],
                            wbh[slot][:, et * 512:(et + 1) * 512],
                            wbl[slot][:, et * 512:(et + 1) * 512],
                            start=(g == 0), stop=(g == NG - 1),
                            waits=([wl1, wl2] + ([wxn1, wxn2, pfree] if g == 0 else []))
                            if et == 0 else (),
                            inc=(et == 3))
                mmg.append(env.tok("Tn"))
            wvp = env.tok("Tn")
            for et in range(4):
                env.split(o, vtm_hi[:, et * 512:(et + 1) * 512],
                          vtm_lo[:, et * 512:(et + 1) * 512], P[pb0 + et][:],
                          waits=[wvp] if et == 0 else ())
            wsp = env.tok("Vv")
            dsth = vfm_hi[:].rearrange("p (g t) -> p g t", g=NG)
            dstl = vfm_lo[:].rearrange("p (g t) -> p g t", g=NG)
            wvt_a = env.dmaT(o, dsth, vtm_hi[:], waits=[wsp])
            wvt_b = env.dmaT(o, dstl, vtm_lo[:], waits=[wsp])
            wvt = [wvt_a, wvt_b]
            pfree2 = env.tok("Vv")
            mmg = []
            for eps in range(NG):
                slot = eps % 2
                wws = [mmg[eps - 2]] if eps >= 2 else [wvp]
                wl1 = env.dma(o, wbh[slot][:], dt["S_wo_hi"][:, eps * C:(eps + 1) * C],
                              waits=wws)
                wl2 = env.dma(o, wbl[slot][:], dt["S_wo_lo"][:, eps * C:(eps + 1) * C])
                for dtl in range(4):
                    env.mm3(o, P[pb0 + dtl][:],
                            vfm_hi[:, eps * 128:(eps + 1) * 128],
                            vfm_lo[:, eps * 128:(eps + 1) * 128],
                            wbh[slot][:, dtl * 512:(dtl + 1) * 512],
                            wbl[slot][:, dtl * 512:(dtl + 1) * 512],
                            start=(eps == 0), stop=(eps == NG - 1),
                            waits=([wl1, wl2] + ([pfree2] + wvt if eps == 0 else []))
                            if dtl == 0 else (),
                            inc=(dtl == 3))
                mmg.append(env.tok("Tn"))
            wop = env.tok("Tn")
            wxl = env.dma(o, xcb[:], dt["x"][tau * 128:(tau + 1) * 128, :],
                          waits=[pfree])
            for dtl in range(4):
                env.ven(o, lambda e, p=P[pb0 + dtl], d=dtl, tau=tau:
                        e.tensor_scalar(zaw[:, d * 512:(d + 1) * 512], p[:],
                                        st["g1"][:, tau:tau + 1], None, OP.mult),
                        waits=([wop, wc0bc] + (env.mile.get("S_wv") or [])
                               + ([za_store.get(tau - 1)] if tau else []))
                        if dtl == 0 else ())
            wza = env.ven(o, lambda e: e.tensor_tensor(zaw[:], zaw[:], c0_bc[:], OP.add))
            za_store[tau] = env.dma(o, dt["S_za"][tau * 128:(tau + 1) * 128, :], zaw[:],
                                    waits=[wza], eng="gpsimd")
            wxa = env.ven(o, lambda e, tau=tau: e.tensor_scalar(
                xcb[:], xcb[:], st["rs"][:, tau:tau + 1], st["w"][:, tau:tau + 1],
                OP.mult, OP.add), waits=[wxl])
            wh_ = env.ven(o, lambda e: e.tensor_tensor(xcb[:], zaw[:], xcb[:], OP.add),
                          waits=[wxa])
            wsq = env.scn(o, lambda e: e.activation(
                out=zaw[:], in_=xcb[:], func=AF.Square, accum_out=acc[:, 0:1]),
                waits=[wh_, za_store[tau]])
            vpe2, sd2, r02, t12, t22 = (sm[:, k:k + 1] for k in range(10, 15))
            env.ven(o, lambda e, a=vpe2: e.tensor_scalar(
                a, acc[:, 0:1], 1.0 / C, EPS_RMS, OP.mult, OP.add), waits=[wsq])
            wv_ = env.tok("Vv")
            wsd = env.scn(o, lambda e, a=sd2, b=vpe2: e.sqrt(a, b), waits=[wv_])
            wrc = env.ven(o, lambda e, a=r02, b=sd2: e.reciprocal(a, b), waits=[wsd])
            env.ven(o, lambda e, a=t12, b=r02: e.tensor_tensor(a, b, b, OP.mult),
                    waits=[wrc])
            env.ven(o, lambda e, a=t22, b=t12, c=vpe2: e.tensor_tensor(a, b, c, OP.mult))
            env.ven(o, lambda e, a=t22: e.tensor_scalar(a, a, -0.5, 1.5, OP.mult, OP.add))
            env.ven(o, lambda e, a=r02, b=t22, tau=tau: e.tensor_tensor(
                st["rs2"][:, tau:tau + 1], a, b, OP.mult))
            wh2 = env.ven(o, lambda e, tau=tau: e.tensor_scalar(
                xcb[:], xcb[:], st["rs2"][:, tau:tau + 1], None, OP.mult))
            wtp = h2_tr[tau - 1] if tau >= 1 else []
            whh, wll = env.split(o, h2h[:], h2l[:], xcb[:], waits=[wh2] + wtp)
            dsth = h2fm_hi[:].rearrange("p (g t) -> p g t", g=NG)[
                :, :, tau * 128:(tau + 1) * 128]
            dstl = h2fm_lo[:].rearrange("p (g t) -> p g t", g=NG)[
                :, :, tau * 128:(tau + 1) * 128]
            wht_a = env.dmaT(o, dsth, h2h[:], waits=[whh, wll])
            wht_b = env.dmaT(o, dstl, h2l[:], waits=[wll])
            h2_tr[tau] = [wht_a, wht_b]
            pfree = env.tok("Vv")
            wb_free = [wop]
        env.mile["S_za"] = env.qtok("G")
        env.mile["h2fm"] = list(h2_tr[NT - 1])
        env.fence(o)
        o.emit(blk)
      _free(scC)

    if "D" in phases:
      scD = []
      with nc.Block("d") as blk:
        o = Ops()
        wgf = [sb(f"dwg{i}", [128, NG * 128], F32, scD) for i in range(2)]
        wuf = [sb(f"dwu{i}", [128, NG * 128], F32, scD) for i in range(2)]
        wgh = [sb(f"dwgh{i}", [128, NG * 128], BF16, scD) for i in range(2)]
        wgl = [sb(f"dwgl{i}", [128, NG * 128], BF16, scD) for i in range(2)]
        wuh = [sb(f"dwuh{i}", [128, NG * 128], BF16, scD) for i in range(2)]
        wul = [sb(f"dwul{i}", [128, NG * 128], BF16, scD) for i in range(2)]
        sg = sb("dsg", [128, 512], F32, scD)
        pr = sb("dpr", [128, 512], F32, scD)
        prh = [sb(f"dprh{i}", [128, 512], BF16, scD) for i in range(2)]
        prl = [sb(f"dprl{i}", [128, 512], BF16, scD) for i in range(2)]
        mm_tok = []
        spl_toks = []
        ev_tok = None
        st_tok = {}
        for phi in range(NF):
            slot = phi % 2
            wprev = ([mm_tok[phi - 2], spl_toks[phi - 2]] if phi >= 2
                     else (env.mile.get("h2fm") or []))
            wg_l = []
            for q4 in range(4):
                ap_g = dt["w_gate"].rearrange("(g p) f -> p g f", p=128)[
                    :, q4 * 4:(q4 + 1) * 4, phi * 128:(phi + 1) * 128]
                ap_u = dt["w_up"].rearrange("(g p) f -> p g f", p=128)[
                    :, q4 * 4:(q4 + 1) * 4, phi * 128:(phi + 1) * 128]
                wg_l.append(env.dma(
                    o, wgf[slot][:].rearrange("p (g f) -> p g f", g=NG)
                    [:, q4 * 4:(q4 + 1) * 4, :], ap_g,
                    waits=wprev if q4 == 0 else ()))
                wg_l.append(env.dma(
                    o, wuf[slot][:].rearrange("p (g f) -> p g f", g=NG)
                    [:, q4 * 4:(q4 + 1) * 4, :], ap_u))
            for g in range(NG):
                env.ven(o, lambda e, s=slot, g=g: e.tensor_scalar(
                    wgf[s][:, g * 128:(g + 1) * 128], wgf[s][:, g * 128:(g + 1) * 128],
                    n2w_fm[:, g:g + 1], None, OP.mult), waits=wg_l if g == 0 else ())
                env.ven(o, lambda e, s=slot, g=g: e.tensor_scalar(
                    wuf[s][:, g * 128:(g + 1) * 128], wuf[s][:, g * 128:(g + 1) * 128],
                    n2w_fm[:, g:g + 1], None, OP.mult))
            wts = env.tok("Vv")
            _, wgs = env.split(o, wgh[slot][:], wgl[slot][:], wgf[slot][:], waits=[wts])
            _, wus = env.split(o, wuh[slot][:], wul[slot][:], wuf[slot][:], waits=[wts])
            spl_toks.append(wus)
            w0 = [wgs, wus] + ([ev_tok] if ev_tok else [])
            for g in range(NG):
                for J in range(NJ):
                    env.mm3(o, P[J][:],
                            wgh[slot][:, g * 128:(g + 1) * 128],
                            wgl[slot][:, g * 128:(g + 1) * 128],
                            h2fm_hi[:, g * T + J * 512: g * T + J * 512 + 512],
                            h2fm_lo[:, g * T + J * 512: g * T + J * 512 + 512],
                            start=(g == 0), stop=(g == NG - 1),
                            waits=w0 if (g == 0 and J == 0) else (), inc=False)
                for J in range(NJ):
                    env.mm3(o, P[4 + J][:],
                            wuh[slot][:, g * 128:(g + 1) * 128],
                            wul[slot][:, g * 128:(g + 1) * 128],
                            h2fm_hi[:, g * T + J * 512: g * T + J * 512 + 512],
                            h2fm_lo[:, g * T + J * 512: g * T + J * 512 + 512],
                            start=(g == 0), stop=(g == NG - 1),
                            inc=(g == NG - 1 and J == NJ - 1))
            wmm = env.tok("Tn")
            mm_tok.append(wmm)
            for J in range(NJ):
                wsl = env.scn(o, lambda e, J=J: e.activation(out=sg[:], in_=P[J][:],
                                                             func=AF.Silu), waits=[wmm])
                wpr = env.ven(o, lambda e, J=J: e.tensor_tensor(pr[:], sg[:], P[4 + J][:],
                                                                OP.mult), waits=[wsl])
                sl2 = J % 2
                wst = (st_tok[(phi, J - 2)] if (phi, J - 2) in st_tok
                       else (st_tok[(phi - 1, J + 2)] if (phi - 1, J + 2) in st_tok
                             else []))
                whh, wll = env.split(o, prh[sl2][:], prl[sl2][:], pr[:], waits=[wpr] + wst)
                go = phi * T + J * 512
                wp_a = env.dma(o, dt["S_pr_hi"][:, go:go + 512], prh[sl2][:],
                               waits=[whh, wll], eng="gpsimd")
                wp_b = env.dma(o, dt["S_pr_lo"][:, go:go + 512], prl[sl2][:],
                               eng="gpsimd")
                st_tok[(phi, J)] = [wp_a, wp_b]
            ev_tok = env.tok("Vv")
        env.mile["S_pr"] = env.qtok("G")
        env.fence(o)
        o.emit(blk)
      _free(scD)
    _free(scCD)

    if "E1" in phases:
      scE = []
      woh = sb("e1woh", [128, 8 * C], BF16, scE)
      wol = sb("e1wol", [128, 8 * C], BF16, scE)
      wstg = sb("e1w", [128, C], F32, scE)
      ohJ_h = sb("e1ohh", [128, 8 * 512], BF16, scE)
      ohJ_l = sb("e1ohl", [128, 8 * 512], BF16, scE)
      aow = sb("e1aow", [128, C], F32, scE)
      aop = sb("e1aop", [128, C], F32, scE)
      with nc.Block("e1") as blk:
        o = Ops()
        ev_prev = None
        for half in range(2):
            spl_tok = []
            for gi in range(8):
                g = half * 8 + gi
                wws = [spl_tok[gi - 1]] if gi >= 1 else (
                    [env.tok("Tn")] if half == 1 else [])
                wl_ = env.dma(o, wstg[:], dt["Wo"][g * 128:(g + 1) * 128, :], waits=wws)
                _, wll = env.split(o, woh[:, gi * C:(gi + 1) * C],
                                   wol[:, gi * C:(gi + 1) * C], wstg[:], waits=[wl_])
                spl_tok.append(wll)
            wsplit = spl_tok[-1]
            for J in range(NJ):
                w1 = env.dma(o, ohJ_h[:].rearrange("p (g t) -> p g t", g=8),
                             dt["S_oh_hi"][:].rearrange("p (g t) -> p g t", g=NG)
                             [:, half * 8:half * 8 + 8, J * 512:(J + 1) * 512],
                             waits=env.mile.get("S_oh", []) + [env.tok("Tn")])
                w2 = env.dma(o, ohJ_l[:].rearrange("p (g t) -> p g t", g=8),
                             dt["S_oh_lo"][:].rearrange("p (g t) -> p g t", g=NG)
                             [:, half * 8:half * 8 + 8, J * 512:(J + 1) * 512])
                for ti in range(4):
                    tau = J * 4 + ti
                    for gi in range(8):
                        for et in range(4):
                            env.mm3(o, P[et][:],
                                    ohJ_h[:, gi * 512 + ti * 128: gi * 512 + ti * 128 + 128],
                                    ohJ_l[:, gi * 512 + ti * 128: gi * 512 + ti * 128 + 128],
                                    woh[:, gi * C + et * 512: gi * C + et * 512 + 512],
                                    wol[:, gi * C + et * 512: gi * C + et * 512 + 512],
                                    start=(gi == 0), stop=(gi == 7),
                                    waits=([w1, w2, wsplit] + ([ev_prev] if ev_prev else []))
                                    if (gi == 0 and et == 0) else (),
                                    inc=(gi == 7 and et == 3))
                    wmm = env.tok("Tn")
                    if half == 0:
                        for et in range(4):
                            env.ven(o, lambda e, et=et: e.tensor_copy(
                                aow[:, et * 512:(et + 1) * 512], P[et][:]),
                                waits=([wmm] + env.qtok("G")) if et == 0 else ())
                        wev = env.tok("Vv")
                        env.dma(o, dt["S_ao"][tau * 128:(tau + 1) * 128, :], aow[:],
                                waits=[wev], eng="gpsimd")
                    else:
                        wld = env.dma(o, aop[:], dt["S_ao"][tau * 128:(tau + 1) * 128, :],
                                      waits=env.mile.get("e1_half0", []) + [ev_prev])
                        for et in range(4):
                            env.ven(o, lambda e, et=et: e.tensor_tensor(
                                aow[:, et * 512:(et + 1) * 512], P[et][:],
                                aop[:, et * 512:(et + 1) * 512], OP.add),
                                waits=([wmm, wld] + env.qtok("G")) if et == 0 else ())
                        wev = env.tok("Vv")
                        env.dma(o, dt["S_ao"][tau * 128:(tau + 1) * 128, :], aow[:],
                                waits=[wev], eng="gpsimd")
                    ev_prev = env.tok("Vv")
            if half == 0:
                env.mile["e1_half0"] = env.qtok("G")
        env.mile["S_ao"] = env.qtok("G")
        env.fence(o)
        o.emit(blk)
      _free(scE)

    if "E2" in phases:
      scE = []
      NH3 = (15, 15, 14)
      wdh = sb("e2wdh", [128, 15 * C], BF16, scE)
      wdl = sb("e2wdl", [128, 15 * C], BF16, scE)
      wstg = sb("e2w", [128, C], F32, scE)
      prJ_h = sb("e2prh", [128, 15 * 512], BF16, scE)
      prJ_l = sb("e2prl", [128, 15 * 512], BF16, scE)
      zw = sb("e2zw", [128, C], F32, scE)
      zp = sb("e2zp", [128, C], F32, scE)
      aot = sb("e2ao", [128, C], F32, scE)
      xt_ = sb("e2x", [128, C], F32, scE)
      ow = sb("e2ow", [128, C], F32, scE)
      with nc.Block("e2") as blk:
        o = Ops()
        ev_prev = None
        fbase = 0
        for third in range(3):
            nf3 = NH3[third]
            spl_tok = []
            for fi in range(nf3):
                f = fbase + fi
                wws = [spl_tok[fi - 1]] if fi >= 1 else (
                    [env.tok("Tn")] if third > 0 else [])
                wl_ = env.dma(o, wstg[:], dt["w_down"][f * 128:(f + 1) * 128, :], waits=wws)
                _, wll = env.split(o, wdh[:, fi * C:(fi + 1) * C],
                                   wdl[:, fi * C:(fi + 1) * C], wstg[:], waits=[wl_])
                spl_tok.append(wll)
            wsplit = spl_tok[-1]
            for J in range(NJ):
                w1 = env.dma(o, prJ_h[:, 0:nf3 * 512].rearrange("p (g t) -> p g t", g=nf3),
                             dt["S_pr_hi"][:].rearrange("p (g t) -> p g t", g=NF)
                             [:, fbase:fbase + nf3, J * 512:(J + 1) * 512],
                             waits=env.mile.get("S_pr", []) + [env.tok("Tn")])
                w2 = env.dma(o, prJ_l[:, 0:nf3 * 512].rearrange("p (g t) -> p g t", g=nf3),
                             dt["S_pr_lo"][:].rearrange("p (g t) -> p g t", g=NF)
                             [:, fbase:fbase + nf3, J * 512:(J + 1) * 512])
                for ti in range(4):
                    tau = J * 4 + ti
                    for fi in range(nf3):
                        for et in range(4):
                            env.mm3(o, P[et][:],
                                    prJ_h[:, fi * 512 + ti * 128: fi * 512 + ti * 128 + 128],
                                    prJ_l[:, fi * 512 + ti * 128: fi * 512 + ti * 128 + 128],
                                    wdh[:, fi * C + et * 512: fi * C + et * 512 + 512],
                                    wdl[:, fi * C + et * 512: fi * C + et * 512 + 512],
                                    start=(fi == 0), stop=(fi == nf3 - 1),
                                    waits=([w1, w2, wsplit] + ([ev_prev] if ev_prev else []))
                                    if (fi == 0 and et == 0) else (),
                                    inc=(fi == nf3 - 1 and et == 3))
                    wmm = env.tok("Tn")
                    if third < 2:
                        wza = env.dma(o, zp[:], dt["S_za"][tau * 128:(tau + 1) * 128, :],
                                      waits=(env.mile.get("S_za", []) if third == 0
                                             else env.mile.get("e2_prev", [])) + [ev_prev])
                        for et in range(4):
                            env.ven(o, lambda e, et=et: e.tensor_tensor(
                                zw[:, et * 512:(et + 1) * 512], P[et][:],
                                zp[:, et * 512:(et + 1) * 512], OP.add),
                                waits=([wmm, wza] + env.qtok("G")) if et == 0 else ())
                        wev = env.tok("Vv")
                        env.dma(o, dt["S_za"][tau * 128:(tau + 1) * 128, :], zw[:],
                                waits=[wev], eng="gpsimd")
                    else:
                        wza = env.dma(o, zp[:], dt["S_za"][tau * 128:(tau + 1) * 128, :],
                                      waits=env.mile.get("e2_prev", []) + [ev_prev])
                        wao = env.dma(o, aot[:], dt["S_ao"][tau * 128:(tau + 1) * 128, :],
                                      waits=env.mile.get("S_ao", []))
                        wxl = env.dma(o, xt_[:], dt["x"][tau * 128:(tau + 1) * 128, :])
                        for et in range(4):
                            env.ven(o, lambda e, et=et: e.tensor_tensor(
                                zw[:, et * 512:(et + 1) * 512], P[et][:],
                                zp[:, et * 512:(et + 1) * 512], OP.add),
                                waits=([wmm, wza] + env.qtok("G")) if et == 0 else ())
                        env.ven(o, lambda e, tau=tau: e.tensor_scalar(
                            zw[:], zw[:], st["rt"][:, tau:tau + 1], None, OP.mult))
                        env.ven(o, lambda e: e.tensor_tensor(zw[:], zw[:], aot[:], OP.add),
                                waits=[wao])
                        env.ven(o, lambda e, tau=tau: e.tensor_scalar(
                            xt_[:], xt_[:], st["u"][:, tau:tau + 1], st["w"][:, tau:tau + 1],
                            OP.mult, OP.add), waits=[wxl])
                        wfin = env.ven(o, lambda e: e.tensor_tensor(ow[:], zw[:], xt_[:],
                                                                    OP.add))
                        env.dma(o, out[tau * 128:(tau + 1) * 128, :], ow[:],
                                waits=[wfin], eng="gpsimd")
                    ev_prev = env.tok("Vv")
            env.mile["e2_prev"] = env.qtok("G")
            fbase += nf3
        o.em("sync", lambda e: e.nop(), waits=env.qtok("G") + env.qtok("L") + env.qtok("X"))
        env.fence(o)
        o.emit(blk)
      _free(scE)


def kernel(**inputs):
    consts = make_consts()
    nc = build_nc()
    def arr(k):
        return np.ascontiguousarray(np.asarray(inputs[k], dtype=np.float32))
    x = arr("L0_attn_x")
    base = {
        "Wq": arr("L0_attn_Wq"), "Wk": arr("L0_attn_Wk"), "Wv": arr("L0_attn_Wv"),
        "Wo": arr("L0_attn_Wo"),
        "head_scales": arr("L0_attn_head_scales").reshape(1, H),
        "w_router": arr("L0_ffn_w_router"),
        "b_router": arr("L0_ffn_b_router").reshape(1, 1),
        "norm1_w": arr("L0_ffn_norm1_w"), "wv": arr("L0_ffn_wv"), "bv": arr("L0_ffn_bv"),
        "wo": arr("L0_ffn_wo"), "norm2_w": arr("L0_ffn_norm2_w"),
        "w_gate": arr("L0_ffn_w_gate"), "w_up": arr("L0_ffn_w_up"),
        "w_down": arr("L0_ffn_w_down"),
    }
    base.update(consts)
    in_maps = [dict(base, x=np.ascontiguousarray(x[i])) for i in range(8)]
    res = run_bass_kernel_spmd(nc, in_maps, core_ids=list(range(8)))
    return np.stack([r["out"] for r in res.results], axis=0)


# version 03
# speedup vs baseline: 1.1839x; 1.1839x over previous
import math
import numpy as np
import concourse.bass as bass
import concourse.mybir as mybir
from concourse.bass_utils import run_bass_kernel_spmd

F32 = mybir.dt.float32
BF16 = mybir.dt.bfloat16
AF = mybir.ActivationFunctionType
OP = mybir.AluOpType

T = 2048
C = 2048
FF = 5632
H = 8
NT = 16
NJ = 4
NG = 16
NF = 44
VROW = 528
EPS_LN = 1e-5
EPS_RMS = 1e-6

ALL_PHASES = ("P1", "A1", "A2", "C", "D", "E1", "E2")


def make_consts():
    import jax
    import jax.numpy as jnp
    with jax.default_device(jax.devices("cpu")[0]):
        inv = 1.0 / 10000.0 ** (jnp.arange(0, 256, 2, dtype=jnp.float32) / 256)
        f = jnp.arange(T, dtype=jnp.float32)[:, None] * inv[None, :]
        cos_hi = np.ascontiguousarray(np.asarray(jnp.cos(f)).T)
        sin_hi = np.ascontiguousarray(np.asarray(jnp.sin(f)).T)
    cos_lo = np.zeros_like(cos_hi)
    sin_lo = np.zeros_like(sin_hi)
    m = np.zeros((128, 4 * 512), dtype=np.float32)
    r = np.arange(128)[:, None]
    c = np.arange(512)[None, :]
    for k in range(4):
        m[:, k * 512:(k + 1) * 512] = (c >= k * 128 + r).astype(np.float32)
    trig = np.array([[sx, sy, sz] for sx in (1.0, -1.0) for sy in (1.0, -1.0)
                     for sz in (1.0, -1.0)], dtype=np.float64) / math.sqrt(3.0)
    dirs64 = np.ascontiguousarray(trig.T)
    return {"cos_hi": cos_hi, "cos_lo": cos_lo, "sin_hi": sin_hi, "sin_lo": sin_lo,
            "bandmask": m, "dirs64": dirs64.astype(np.float32),
            "dirs_lo64": (dirs64 - dirs64.astype(np.float32)).astype(np.float32)}


class Ops:
    def __init__(self):
        self.ops = []

    def em(self, eng, fn, inc=None, waits=()):
        self.ops.append((eng, fn, inc, tuple(w for w in waits if w is not None)))

    def emit(self, block):
        by_eng = {}
        for op in self.ops:
            by_eng.setdefault(op[0], []).append(op)
        for eng, items in by_eng.items():
            def body(e, items=items):
                seen = {}
                for _, fn, inc, waits in items:
                    for sem, val in waits:
                        if val > 0 and seen.get(id(sem), -1) < val:
                            e.wait_ge(sem, val)
                            seen[id(sem)] = val
                    inst = fn(e)
                    if inc is not None:
                        inst.then_inc(inc[0], inc[1])
            getattr(block, eng)(body)


class Env:
    DMA_POOL = {"L": 6, "A": 4, "G": 4, "X": 2}

    def __init__(self, nc):
        self.nc = nc
        self._ctx = []
        self.S = {}
        self.V = {}
        for nm in ("Tn", "Vv", "Sc"):
            cm = nc.semaphore(nm)
            self.S[nm] = cm.__enter__()
            self._ctx.append(cm)
            self.V[nm] = 0
        self.dpool = {}
        self.dcnt = {}
        self.didx = {}
        for nm, r in self.DMA_POOL.items():
            self.dpool[nm] = []
            self.dcnt[nm] = []
            self.didx[nm] = 0
            for i in range(r):
                cm = nc.semaphore(f"{nm}{i}")
                self.dpool[nm].append(cm.__enter__())
                self._ctx.append(cm)
                self.dcnt[nm].append(0)
        self.mile = {}

    def sb(self, name, shape, dtype, ctx=None):
        cm = self.nc.sbuf_tensor(name, shape, dtype)
        h = cm.__enter__()
        (ctx if ctx is not None else self._ctx).append(cm)
        return h

    def psum(self, name):
        cm = self.nc.psum_tensor(name, [128, 512], F32)
        h = cm.__enter__()
        self._ctx.append(cm)
        return h

    def tok(self, nm):
        return (self.S[nm], self.V[nm])

    def _dma_slot(self, nm):
        i = self.didx[nm] % len(self.dpool[nm])
        self.didx[nm] += 1
        sem = self.dpool[nm][i]
        prev = self.dcnt[nm][i]
        self.dcnt[nm][i] = prev + 16
        return sem, prev

    def dma(self, o, outap, inap, waits=(), eng="sync", slow=False):
        nm = {"gpsimd": "G", "scalar": "A"}.get(eng, "L")
        sem, prev = self._dma_slot(nm)
        o.em(eng, lambda e, a=outap, b=inap, s=slow: e.dma_start(
            out=a, in_=b, allow_slow_non_contiguous=s),
             inc=(sem, 16), waits=list(waits) + [(sem, prev)])
        return (sem, prev + 16)

    def dmaT(self, o, outap, inap, waits=()):
        sem, prev = self._dma_slot("X")
        o.em("sync", lambda e, a=outap, b=inap: e.dma_start_transpose(a, b),
             inc=(sem, 16), waits=list(waits) + [(sem, prev)])
        return (sem, prev + 16)

    def ven(self, o, fn, waits=()):
        prev = self.V["Vv"]
        self.V["Vv"] = prev + 1
        o.em("vector", fn, inc=(self.S["Vv"], 1),
             waits=list(waits) + [(self.S["Vv"], prev)])
        return (self.S["Vv"], prev + 1)

    def scn(self, o, fn, waits=()):
        prev = self.V["Sc"]
        self.V["Sc"] = prev + 1
        o.em("scalar", fn, inc=(self.S["Sc"], 1),
             waits=list(waits) + [(self.S["Sc"], prev)])
        return (self.S["Sc"], prev + 1)

    def ten(self, o, fn, waits=()):
        self.V["Tn"] += 1
        o.em("tensor", fn, inc=(self.S["Tn"], 1), waits=waits)
        return (self.S["Tn"], self.V["Tn"])

    def mm3(self, o, pb, lh, ll, rh, rl, start, stop, waits=(), inc=False):
        o.em("tensor", lambda e, a=pb, b=lh, c=rh, s=start:
             e.matmul(a, b, c, start=s, stop=False), waits=waits)
        o.em("tensor", lambda e, a=pb, b=lh, c=rl:
             e.matmul(a, b, c, start=False, stop=False))
        token = None
        if inc:
            self.V["Tn"] += 1
            token = (self.S["Tn"], self.V["Tn"])
        o.em("tensor", lambda e, a=pb, b=ll, c=rh, s=stop:
             e.matmul(a, b, c, start=False, stop=s),
             inc=(self.S["Tn"], 1) if inc else None)
        return token

    def qtok(self, nm):
        return [(s, c) for s, c in zip(self.dpool[nm], self.dcnt[nm]) if c > 0]

    def fence(self, o):
        o.em("sync", lambda e: e.nop(),
             waits=self.qtok("G") + self.qtok("L") + self.qtok("A") + self.qtok("X"))

    def split(self, o, hi, lo, src, waits=(), hi_on_sc=True):
        if hi_on_sc:
            w1 = self.scn(o, lambda e, a=hi, b=src: e.copy(a, b), waits=waits)
        else:
            w1 = self.ven(o, lambda e, a=hi, b=src: e.tensor_copy(a, b), waits=waits)
        w2 = self.ven(o, lambda e, a=lo, b=src, c=hi:
                      e.tensor_tensor(a, b, c, OP.subtract), waits=[w1])
        return w1, w2


def build_nc(phases=ALL_PHASES):
    nc = bass.Bass("TRN2", target_bir_lowering=False)
    dt = {}

    def din(name, shape):
        dt[name] = nc.dram_tensor(name, shape, F32, kind="ExternalInput").ap()
    din("x", [T, C])
    din("Wq", [C, C]); din("Wk", [C, 512]); din("Wv", [C, 512]); din("Wo", [C, C])
    din("head_scales", [1, H])
    din("w_router", [C]); din("b_router", [1, 1]); din("norm1_w", [C])
    din("wv", [C, C]); din("bv", [C]); din("wo", [C, C]); din("norm2_w", [C])
    din("w_gate", [C, FF]); din("w_up", [C, FF]); din("w_down", [FF, C])
    din("cos_hi", [128, T]); din("cos_lo", [128, T])
    din("sin_hi", [128, T]); din("sin_lo", [128, T])
    din("bandmask", [128, 4 * 512]); din("dirs64", [3, 8]); din("dirs_lo64", [3, 8])
    out = nc.dram_tensor("out", [T, C], F32, kind="ExternalOutput").ap()

    def scr(name, shape, dtype):
        dt[name] = nc.dram_tensor(name, shape, dtype, kind="Internal").ap()
    for nm in ("S_xn_hi", "S_xn_lo", "S_q_hi", "S_q_lo", "S_oh_hi", "S_oh_lo"):
        scr(nm, [128, NG * T], BF16)
    scr("S_wv_hi", [128, NG * C], BF16); scr("S_wv_lo", [128, NG * C], BF16)
    scr("S_wo_hi", [128, NG * C], BF16); scr("S_wo_lo", [128, NG * C], BF16)
    scr("S_pr_hi", [128, NF * T], BF16); scr("S_pr_lo", [128, NF * T], BF16)
    scr("S_za", [T, C], F32)
    scr("S_ao", [T, C], F32)

    env = Env(nc)
    try:
        _program(nc, env, dt, out, phases)
    finally:
        for cm in reversed(env._ctx):
            try:
                cm.__exit__(None, None, None)
            except Exception:
                pass
    return nc


def _free(scope):
    for cm in reversed(scope):
        cm.__exit__(None, None, None)
    scope.clear()


def _program(nc, env, dt, out, phases):
    S = env.S
    sb = env.sb

    st = {nm: sb("st_" + nm, [128, NT], F32) for nm in ("rs", "u", "w", "g1", "rt", "rs2")}
    ones_row = sb("ones_row", [1, 128], F32)
    hs_row = sb("hs_row", [1, H], F32)
    hs1m_row = sb("hs1m_row", [1, H], F32)
    brc = sb("brc", [128, 1], F32)
    brs = sb("brs", [1, 1], F32)
    n1w_fm = sb("n1w_fm", [128, NG], F32)
    n2w_fm = sb("n2w_fm", [128, NG], F32)
    bvf = sb("bvf", [128, NG], F32)
    sm = sb("smallA", [128, 16], F32)
    P = [env.psum(f"pb{i}") for i in range(8)]

    sp = []
    wrbc = sb("wrbc", [128, C], F32, sp)
    with nc.Block("p0") as blk:
        o = Ops()
        tmp = sb("p0tmp", [128, C], F32, sp)
        w1 = env.dma(o, hs_row[:], dt["head_scales"])
        env.ven(o, lambda e: e.tensor_scalar(hs1m_row[:], hs_row[:], -1.0, 1.0,
                                             OP.mult, OP.add), waits=[w1])
        env.ven(o, lambda e: e.memset(ones_row[:], 1.0))
        wr = env.dma(o, tmp[0:1, 0:C], dt["w_router"].rearrange("(r c) -> r c", r=1))
        wb = env.dma(o, brs[:], dt["b_router"])
        for j in range(4):
            wm = env.ten(o, lambda e, j=j: e.matmul(
                P[0][:], ones_row[:], tmp[0:1, j * 512:(j + 1) * 512],
                start=True, stop=True), waits=[wr, env.tok("Vv")])
            env.ven(o, lambda e, j=j: e.tensor_copy(wrbc[:, j * 512:(j + 1) * 512], P[0][:]),
                    waits=[wm])
        wm = env.ten(o, lambda e: e.matmul(P[1][:, 0:1], ones_row[:], brs[:],
                                           start=True, stop=True),
                     waits=[wb, env.tok("Vv")])
        env.ven(o, lambda e: e.tensor_copy(brc[:], P[1][:, 0:1]), waits=[wm])
        env.dma(o, n1w_fm[:], dt["norm1_w"].rearrange("(g p) -> p g", p=128), slow=True)
        env.dma(o, n2w_fm[:], dt["norm2_w"].rearrange("(g p) -> p g", p=128), slow=True)
        env.dma(o, bvf[:], dt["bv"].rearrange("(g p) -> p g", p=128), slow=True)
        env.fence(o)
        o.emit(blk)

    env.mile["S_xnJ"] = {}
    if "P1" in phases:
      with nc.Block("p1") as blk:
        o = Ops()
        xb = [sb(f"p1x{i}", [128, C], F32, sp) for i in range(2)]
        xnb = sb("p1xn", [128, C], F32, sp)
        hib = [sb(f"p1hi{i}", [128, C], BF16, sp) for i in range(2)]
        lob = [sb(f"p1lo{i}", [128, C], BF16, sp) for i in range(2)]
        dump = sb("p1dump", [128, C], F32, sp)
        acc = sb("p1acc", [128, 4], F32, sp)
        stg_h = [sb(f"p1sh{i}", [128, NG * 512], BF16, sp) for i in range(2)]
        stg_l = [sb(f"p1sl{i}", [128, NG * 512], BF16, sp) for i in range(2)]
        xn_done = []
        tr_done = []
        stg_free = [None, None]
        for tau in range(NT):
            J = tau // 4
            bx = xb[tau % 2]
            wx = env.dma(o, bx[:], dt["x"][tau * 128:(tau + 1) * 128, :],
                         waits=[xn_done[tau - 2]] if tau >= 2 else ())
            ws1 = env.scn(o, lambda e, b=bx: e.activation(
                out=dump[:], in_=b[:], func=AF.Copy, accum_out=acc[:, 0:1]),
                waits=[wx, env.tok("Vv")])
            ws2 = env.scn(o, lambda e, b=bx: e.activation(
                out=dump[:], in_=b[:], func=AF.Square, accum_out=acc[:, 1:2]),
                waits=[ws1])
            mc, vc, vpe, sd, r0, t1, t2, t3, dn, g0 = (sm[:, k:k + 1] for k in range(10))
            env.ven(o, lambda e, a=mc: e.tensor_scalar(a, acc[:, 0:1], 1.0 / C, None, OP.mult),
                    waits=[ws1, ws2, env.tok("Sc")])
            env.ven(o, lambda e, a=vc: e.tensor_scalar(a, acc[:, 1:2], 1.0 / C, None, OP.mult))
            env.ven(o, lambda e, a=t3, b=mc, c=vc: e.scalar_tensor_tensor(
                a, b, b[:, 0:1], c, OP.mult, OP.subtract))
            env.ven(o, lambda e, a=vc, b=t3: e.tensor_scalar(a, b, -1.0, None, OP.mult))
            wv_ = env.ven(o, lambda e, a=vpe, b=vc: e.tensor_scalar(a, b, EPS_LN, None, OP.add))
            wsd = env.scn(o, lambda e, a=sd, b=vpe: e.sqrt(a, b), waits=[wv_])
            wrc = env.ven(o, lambda e, a=r0, b=sd: e.reciprocal(a, b), waits=[wsd])
            env.ven(o, lambda e, a=t1, b=r0: e.tensor_tensor(a, b, b, OP.mult), waits=[wrc])
            env.ven(o, lambda e, a=t2, b=t1, c=vpe: e.tensor_tensor(a, b, c, OP.mult))
            env.ven(o, lambda e, a=t3, b=t2: e.tensor_scalar(a, b, -0.5, 1.5, OP.mult, OP.add))
            rsc = st["rs"][:, tau:tau + 1]
            env.ven(o, lambda e, a=rsc, b=r0, c=t3: e.tensor_tensor(a, b, c, OP.mult))
            env.ven(o, lambda e, a=rsc, tau=tau: e.tensor_scalar(
                st["u"][:, tau:tau + 1], a, 1.0, None, OP.add))
            env.ven(o, lambda e, a=rsc, b=mc, tau=tau: e.scalar_tensor_tensor(
                st["w"][:, tau:tau + 1], b, -1.0, a, OP.mult, OP.mult))
            env.ven(o, lambda e, a=dn, b=vpe: e.reciprocal(a, b))
            env.ven(o, lambda e, a=t1, b=vc, c=dn: e.tensor_tensor(a, b, c, OP.mult))
            wv_ = env.ven(o, lambda e, a=t2, b=t1: e.tensor_scalar(a, b, EPS_RMS, None, OP.add))
            wsd = env.scn(o, lambda e, a=sd, b=t2: e.sqrt(a, b), waits=[wv_])
            wrc = env.ven(o, lambda e, a=g0, b=sd: e.reciprocal(a, b), waits=[wsd])
            env.ven(o, lambda e, a=t1, b=g0: e.tensor_tensor(a, b, b, OP.mult), waits=[wrc])
            env.ven(o, lambda e, a=t1, b=t2: e.tensor_tensor(a, a, b, OP.mult))
            env.ven(o, lambda e, a=t1: e.tensor_scalar(a, a, -0.5, 1.5, OP.mult, OP.add))
            env.ven(o, lambda e, a=g0, b=t1, tau=tau: e.tensor_tensor(
                st["g1"][:, tau:tau + 1], a, b, OP.mult))
            wxn = env.ven(o, lambda e, b=bx, a=rsc, tau=tau: e.tensor_scalar(
                xnb[:], b[:], a, st["w"][:, tau:tau + 1], OP.mult, OP.add))
            xn_done.append(wxn)
            wrt = env.ven(o, lambda e: e.scalar_tensor_tensor(
                dump[:], xnb[:], 1.0, wrbc[:], OP.mult, OP.mult, accum_out=acc[:, 2:3]),
                waits=[wxn])
            env.ven(o, lambda e: e.tensor_scalar(acc[:, 3:4], acc[:, 2:3], brc[:, 0:1], 0.0,
                                                 OP.add, OP.is_gt), waits=[wrt])
            env.ven(o, lambda e, tau=tau: e.tensor_copy(st["rt"][:, tau:tau + 1], acc[:, 3:4]))
            bh, bl = hib[tau % 2], lob[tau % 2]
            wprev = tr_done[tau - 2] if tau >= 2 else []
            whh, wll = env.split(o, bh[:], bl[:], xnb[:], waits=[wxn] + wprev)
            sj = J % 2
            dst_h = stg_h[sj][:].rearrange("p (g t) -> p g t", g=NG)[
                :, :, (tau % 4) * 128:(tau % 4) * 128 + 128]
            dst_l = stg_l[sj][:].rearrange("p (g t) -> p g t", g=NG)[
                :, :, (tau % 4) * 128:(tau % 4) * 128 + 128]
            wtw = stg_free[sj] if (tau % 4 == 0 and stg_free[sj]) else []
            wt1 = env.dmaT(o, dst_h, bh[:], waits=[whh, wll] + wtw)
            wt2 = env.dmaT(o, dst_l, bl[:], waits=[wll] + wtw)
            tr_done.append([wt1, wt2])
            if tau % 4 == 3:
                wsh = env.dma(o, dt["S_xn_hi"][:].rearrange("p (g t) -> p g t", g=NG)
                              [:, :, J * 512:(J + 1) * 512],
                              stg_h[sj][:].rearrange("p (g t) -> p g t", g=NG),
                              waits=env.qtok("X"), eng="gpsimd")
                wst = env.dma(o, dt["S_xn_lo"][:].rearrange("p (g t) -> p g t", g=NG)
                              [:, :, J * 512:(J + 1) * 512],
                              stg_l[sj][:].rearrange("p (g t) -> p g t", g=NG),
                              eng="gpsimd")
                stg_free[sj] = [wsh, wst]
                env.mile["S_xnJ"][J] = [wsh, wst]
        env.fence(o)
        o.emit(blk)
    _free(sp)

    scA = []
    K_hi = sb("K_hi", [128, 4 * T], BF16, scA)
    K_lo = sb("K_lo", [128, 4 * T], BF16, scA)
    V_hi = sb("V_hi", [128, NT * VROW], BF16, scA)
    V_lo = sb("V_lo", [128, NT * VROW], BF16, scA)
    mask4 = sb("mask4", [128, 4 * 512], F32, scA)
    dirs_h = sb("dirs_h", [3, 8], BF16, scA)
    dirs_l = sb("dirs_l", [3, 8], BF16, scA)
    cosT = sb("cosT", [128, T], F32, scA)
    sinT = sb("sinT", [128, T], F32, scA)

    if "A1" in phases:
      scA1 = []
      with nc.Block("a1") as blk:
        o = Ops()
        xnJ_h = sb("a1xnh", [128, NG * 512], BF16, scA1)
        xnJ_l = sb("a1xnl", [128, NG * 512], BF16, scA1)
        wvh = sb("a1wvh", [128, NG * 512], BF16, scA1)
        wvl = sb("a1wvl", [128, NG * 512], BF16, scA1)
        wsg = [sb(f"a1w{i}", [128, 1024], F32, scA1) for i in range(3)]
        wbh = [sb(f"a1bh{i}", [128, 1024], BF16, scA1) for i in range(3)]
        wbl = [sb(f"a1bl{i}", [128, 1024], BF16, scA1) for i in range(3)]
        rop = [sb(f"a1rop{i}", [128, 512], F32, scA1) for i in range(4)]
        qsth = sb("a1qh", [128, 2 * 512], BF16, scA1)
        qstl = sb("a1ql", [128, 2 * 512], BF16, scA1)
        env.dma(o, mask4[:], dt["bandmask"])
        w1 = env.dma(o, rop[0][:], dt["cos_hi"][:, 0:512])
        w1b = env.dma(o, rop[1][:], dt["cos_hi"][:, 512:1024])
        w1c = env.dma(o, rop[2][:], dt["cos_hi"][:, 1024:1536])
        w1d = env.dma(o, rop[3][:], dt["cos_hi"][:, 1536:2048])
        w2 = env.dma(o, cosT[:], dt["cos_lo"])
        wc = env.ven(o, lambda e: e.tensor_tensor(cosT[:, 0:512], rop[0][:],
                                                  cosT[:, 0:512], OP.add), waits=[w1, w2])
        env.ven(o, lambda e: e.tensor_tensor(cosT[:, 512:1024], rop[1][:],
                                             cosT[:, 512:1024], OP.add), waits=[w1b])
        env.ven(o, lambda e: e.tensor_tensor(cosT[:, 1024:1536], rop[2][:],
                                             cosT[:, 1024:1536], OP.add), waits=[w1c])
        wc = env.ven(o, lambda e: e.tensor_tensor(cosT[:, 1536:2048], rop[3][:],
                                                  cosT[:, 1536:2048], OP.add), waits=[w1d])
        w1 = env.dma(o, rop[0][:], dt["sin_hi"][:, 0:512], waits=[wc])
        w1b = env.dma(o, rop[1][:], dt["sin_hi"][:, 512:1024])
        w1c = env.dma(o, rop[2][:], dt["sin_hi"][:, 1024:1536])
        w1d = env.dma(o, rop[3][:], dt["sin_hi"][:, 1536:2048])
        w2 = env.dma(o, sinT[:], dt["sin_lo"])
        env.ven(o, lambda e: e.tensor_tensor(sinT[:, 0:512], rop[0][:],
                                             sinT[:, 0:512], OP.add), waits=[w1, w2])
        env.ven(o, lambda e: e.tensor_tensor(sinT[:, 512:1024], rop[1][:],
                                             sinT[:, 512:1024], OP.add), waits=[w1b])
        env.ven(o, lambda e: e.tensor_tensor(sinT[:, 1024:1536], rop[2][:],
                                             sinT[:, 1024:1536], OP.add), waits=[w1c])
        wsin = env.ven(o, lambda e: e.tensor_tensor(sinT[:, 1536:2048], rop[3][:],
                                                    sinT[:, 1536:2048], OP.add), waits=[w1d])
        wd1 = env.dma(o, wsg[0][0:3, 0:8], dt["dirs64"], waits=[wsin])
        wd2 = env.dma(o, wsg[0][0:3, 8:16], dt["dirs_lo64"])
        wdh = env.scn(o, lambda e: e.copy(dirs_h[:], wsg[0][0:3, 0:8]), waits=[wd1, wd2])
        env.ven(o, lambda e: e.tensor_tensor(rop[1][0:3, 0:8], wsg[0][0:3, 0:8],
                                             dirs_h[:], OP.subtract), waits=[wdh])
        env.ven(o, lambda e: e.tensor_tensor(rop[1][0:3, 0:8], rop[1][0:3, 0:8],
                                             wsg[0][0:3, 8:16], OP.add))
        wdirs = env.ven(o, lambda e: e.tensor_copy(dirs_l[:], rop[1][0:3, 0:8]))
        wv_spl = [wdirs]
        for g in range(NG):
            slot = g % 3
            wl_ = env.dma(o, wsg[slot][:, 0:512], dt["Wv"][g * 128:(g + 1) * 128, :],
                          waits=[wv_spl[-3]] if len(wv_spl) >= 3 else [wdirs])
            _, w2_ = env.split(o, wvh[:, g * 512:(g + 1) * 512],
                               wvl[:, g * 512:(g + 1) * 512], wsg[slot][:, 0:512],
                               waits=[wl_])
            wv_spl.append(w2_)
        for tau in range(NT):
            env.ven(o, lambda e, tau=tau: e.memset(
                V_hi[:, tau * VROW + 512:tau * VROW + 513], 1.0))
            env.ven(o, lambda e, tau=tau: e.memset(
                V_lo[:, tau * VROW + 512:tau * VROW + 513], 0.0))
        wones = env.tok("Vv")
        pfree = wones
        wring = wv_spl[-1]
        qst_free = None
        for J in range(NJ):
            wxn1 = env.dma(o, xnJ_h[:].rearrange("p (g t) -> p g t", g=NG),
                           dt["S_xn_hi"][:].rearrange("p (g t) -> p g t", g=NG)
                           [:, :, J * 512:(J + 1) * 512],
                           waits=(env.mile["S_xnJ"].get(J) or []) + [env.tok("Tn")])
            wxn2 = env.dma(o, xnJ_l[:].rearrange("p (g t) -> p g t", g=NG),
                           dt["S_xn_lo"][:].rearrange("p (g t) -> p g t", g=NG)
                           [:, :, J * 512:(J + 1) * 512])
            for ti in range(4):
                tau = J * 4 + ti
                for g in range(NG):
                    env.mm3(o, P[ti][:],
                            xnJ_h[:, g * 512 + ti * 128: g * 512 + ti * 128 + 128],
                            xnJ_l[:, g * 512 + ti * 128: g * 512 + ti * 128 + 128],
                            wvh[:, g * 512:(g + 1) * 512], wvl[:, g * 512:(g + 1) * 512],
                            start=(g == 0), stop=(g == NG - 1),
                            waits=[wxn1, wxn2, wv_spl[-1], wones, pfree] if g == 0 else (),
                            inc=(g == NG - 1))
            wvm = env.tok("Tn")
            for ti in range(4):
                tau = J * 4 + ti
                env.split(o, V_hi[:, tau * VROW:tau * VROW + 512],
                          V_lo[:, tau * VROW:tau * VROW + 512], P[ti][:], waits=[wvm])
            kmm = []
            for g in range(NG):
                slot = g % 3
                wl_ = env.dma(o, wsg[slot][:, 0:512], dt["Wk"][g * 128:(g + 1) * 128, :],
                              waits=[wring] if g == 0 else
                              ([kmm[g - 3]] if g >= 3 else ()))
                whh, wll = env.split(o, wbh[slot][:, 0:512], wbl[slot][:, 0:512],
                                     wsg[slot][:, 0:512], waits=[wl_])
                for kap in range(4):
                    env.mm3(o, P[4 + kap][:],
                            wbh[slot][:, kap * 128:(kap + 1) * 128],
                            wbl[slot][:, kap * 128:(kap + 1) * 128],
                            xnJ_h[:, g * 512:(g + 1) * 512],
                            xnJ_l[:, g * 512:(g + 1) * 512],
                            start=(g == 0), stop=(g == NG - 1),
                            waits=([wll] + ([pfree] if g == 0 else []))
                            if kap == 0 else (),
                            inc=(kap == 3))
                kmm.append(env.tok("Tn"))
            wkm = env.tok("Tn")
            for g2 in range(2):
                pa, pb_ = P[4 + 2 * g2], P[4 + 2 * g2 + 1]
                cs = cosT[:, J * 512:(J + 1) * 512]
                sn = sinT[:, J * 512:(J + 1) * 512]
                env.ven(o, lambda e, p=pa, c=cs: e.tensor_tensor(rop[0][:], p[:], c, OP.mult),
                        waits=[wkm])
                env.ven(o, lambda e, p=pb_, s=sn: e.tensor_tensor(rop[1][:], p[:], s, OP.mult))
                env.ven(o, lambda e, p=pb_, c=cs: e.tensor_tensor(rop[2][:], p[:], c, OP.mult))
                env.ven(o, lambda e, p=pa, s=sn: e.tensor_tensor(rop[3][:], p[:], s, OP.mult))
                wA = env.ven(o, lambda e: e.tensor_tensor(rop[0][:], rop[0][:], rop[1][:],
                                                          OP.subtract))
                wB = env.ven(o, lambda e: e.tensor_tensor(rop[2][:], rop[2][:], rop[3][:],
                                                          OP.add))
                kA = (2 * g2) * T + J * 512
                kB = (2 * g2 + 1) * T + J * 512
                env.split(o, K_hi[:, kA:kA + 512], K_lo[:, kA:kA + 512], rop[0][:], waits=[wA])
                env.split(o, K_hi[:, kB:kB + 512], K_lo[:, kB:kB + 512], rop[2][:], waits=[wB])
            pfree = env.tok("Vv")
            wring = wkm
            for hq in range(2):
                qmm = []
                for g in range(NG):
                    slot = g % 3
                    wl_ = env.dma(o, wsg[slot][:],
                                  dt["Wq"][g * 128:(g + 1) * 128,
                                           hq * 1024:(hq + 1) * 1024],
                                  waits=[wring] if g == 0 else
                                  ([qmm[g - 3]] if g >= 3 else ()))
                    whh, wll = env.split(o, wbh[slot][:], wbl[slot][:], wsg[slot][:],
                                         waits=[wl_])
                    for hh in range(4):
                        for kk in range(2):
                            env.mm3(o, P[hh * 2 + kk][:],
                                    wbh[slot][:, hh * 256 + kk * 128:
                                             hh * 256 + kk * 128 + 128],
                                    wbl[slot][:, hh * 256 + kk * 128:
                                             hh * 256 + kk * 128 + 128],
                                    xnJ_h[:, g * 512:(g + 1) * 512],
                                    xnJ_l[:, g * 512:(g + 1) * 512],
                                    start=(g == 0), stop=(g == NG - 1),
                                    waits=([wll] + ([pfree] if g == 0 else []))
                                    if (hh == 0 and kk == 0) else (),
                                    inc=(hh == 3 and kk == 1))
                    qmm.append(env.tok("Tn"))
                wqm = env.tok("Tn")
                for hh in range(4):
                    h = hq * 4 + hh
                    pa, pb_ = P[hh * 2], P[hh * 2 + 1]
                    cs = cosT[:, J * 512:(J + 1) * 512]
                    sn = sinT[:, J * 512:(J + 1) * 512]
                    wqf = qst_free if qst_free else []
                    env.ven(o, lambda e, p=pa, c=cs: e.tensor_tensor(rop[0][:], p[:], c,
                                                                     OP.mult),
                            waits=[wqm] + wqf)
                    env.ven(o, lambda e, p=pb_, s=sn: e.tensor_tensor(rop[1][:], p[:], s,
                                                                      OP.mult))
                    env.ven(o, lambda e, p=pb_, c=cs: e.tensor_tensor(rop[2][:], p[:], c,
                                                                      OP.mult))
                    env.ven(o, lambda e, p=pa, s=sn: e.tensor_tensor(rop[3][:], p[:], s,
                                                                     OP.mult))
                    wA = env.ven(o, lambda e: e.tensor_tensor(rop[0][:], rop[0][:], rop[1][:],
                                                              OP.subtract))
                    wB = env.ven(o, lambda e: e.tensor_tensor(rop[2][:], rop[2][:], rop[3][:],
                                                              OP.add))
                    env.split(o, qsth[:, 0:512], qstl[:, 0:512], rop[0][:], waits=[wA])
                    _, wsp = env.split(o, qsth[:, 512:1024], qstl[:, 512:1024], rop[2][:],
                                       waits=[wB])
                    wq_a = env.dma(o, dt["S_q_hi"][:].rearrange("p (g t) -> p g t", g=NG)
                                   [:, 2 * h:2 * h + 2, J * 512:(J + 1) * 512],
                                   qsth[:].rearrange("p (g t) -> p g t", g=2),
                                   waits=[wsp], eng="gpsimd")
                    wq_b = env.dma(o, dt["S_q_lo"][:].rearrange("p (g t) -> p g t", g=NG)
                                   [:, 2 * h:2 * h + 2, J * 512:(J + 1) * 512],
                                   qstl[:].rearrange("p (g t) -> p g t", g=2),
                                   eng="gpsimd")
                    qst_free = [wq_a, wq_b]
                pfree = env.tok("Vv")
                wring = wqm
        env.mile["S_q"] = qst_free
        env.fence(o)
        o.emit(blk)
      _free(scA1)

    if "A2" in phases:
      scA2 = []
      with nc.Block("a2") as blk:
        o = Ops()
        qb_h = [sb(f"a2qh{i}", [128, 2 * T], BF16, scA2) for i in range(2)]
        qb_l = [sb(f"a2ql{i}", [128, 2 * T], BF16, scA2) for i in range(2)]
        eb = [sb(f"a2eb{i}", [128, 512], F32, scA2) for i in range(4)]
        ebm = [sb(f"a2ebm{i}", [128, 512], F32, scA2) for i in range(4)]
        e_hi = [sb(f"a2eh{i}", [128, 512], BF16, scA2) for i in range(4)]
        e_lo = [sb(f"a2el{i}", [128, 512], BF16, scA2) for i in range(4)]
        of32 = [sb(f"a2of{i}", [128, 512], F32, scA2) for i in range(2)]
        og32 = [sb(f"a2og{i}", [128, 512], F32, scA2) for i in range(2)]
        ohf = sb("a2ohf", [128, 512], F32, scA2)
        oh_hi = sb("a2ohh", [128, 512], BF16, scA2)
        oh_lo = sb("a2ohl", [128, 512], BF16, scA2)
        kp_row = sb("a2kp", [1, T], F32, scA2)
        qp_row = sb("a2qp", [1, T], F32, scA2)
        rr = sb("a2rr", [1, 512], F32, scA2)
        rrg = sb("a2rrg", [1, 512], F32, scA2)
        arow = sb("a2ar", [1, 512], F32, scA2)
        brow = sb("a2br", [1, 512], F32, scA2)
        pS = [P[0], P[1], P[2], P[3]]; pO = [P[4], P[5]]; pR = P[6]; pB = P[7]
        q_free = [None, None]
        comb_prev = None
        oh_store_prev = None
        prev_exps = []
        for h in range(H):
            g2 = h // 4
            qh, ql = qb_h[h % 2], qb_l[h % 2]
            wq1 = env.dma(o, qh[:], dt["S_q_hi"][:, (2 * h) * T:(2 * h + 2) * T],
                          waits=(env.mile.get("S_q") or []) + (
                              [q_free[h % 2]] if q_free[h % 2] else []))
            wq2 = env.dma(o, ql[:], dt["S_q_lo"][:, (2 * h) * T:(2 * h + 2) * T])
            whd = env.tok("Vv")
            wkq = []
            for J in range(NJ):
                b0 = (2 * g2) * T + J * 512
                o.em("tensor", lambda e, b=b0, h=h: e.matmul(
                    pR[0:1, :], dirs_h[:, h:h + 1], K_hi[0:3, b:b + 512],
                    start=True, stop=False),
                    waits=[whd] + ([wkq[-1]] if wkq else []))
                o.em("tensor", lambda e, b=b0, h=h: e.matmul(
                    pR[0:1, :], dirs_h[:, h:h + 1], K_lo[0:3, b:b + 512],
                    start=False, stop=False))
                wm = env.ten(o, lambda e, b=b0, h=h: e.matmul(
                    pR[0:1, :], dirs_l[:, h:h + 1], K_hi[0:3, b:b + 512],
                    start=False, stop=True))
                wkq.append(env.ven(o, lambda e, J=J: e.tensor_copy(
                    kp_row[0:1, J * 512:(J + 1) * 512], pR[0:1, :]), waits=[wm]))
            for J in range(NJ):
                o.em("tensor", lambda e, J=J, q=qh, h=h: e.matmul(
                    pR[0:1, :], dirs_h[:, h:h + 1], q[0:3, J * 512:J * 512 + 512],
                    start=True, stop=False), waits=[wq1, wq2, wkq[-1]])
                o.em("tensor", lambda e, J=J, q=ql, h=h: e.matmul(
                    pR[0:1, :], dirs_h[:, h:h + 1], q[0:3, J * 512:J * 512 + 512],
                    start=False, stop=False))
                wm = env.ten(o, lambda e, J=J, q=qh, h=h: e.matmul(
                    pR[0:1, :], dirs_l[:, h:h + 1], q[0:3, J * 512:J * 512 + 512],
                    start=False, stop=True))
                wkq.append(env.ven(o, lambda e, J=J: e.tensor_copy(
                    qp_row[0:1, J * 512:(J + 1) * 512], pR[0:1, :]), waits=[wm]))
            w_rows = env.tok("Vv")
            pv_gate = [comb_prev] if comb_prev else []
            for j in range(NJ):
                ni = 4 * j + 4
                evac_toks = []
                for sweep in range(2):
                    exp_tok = []
                    split_tok = []
                    for i in range(ni):
                        ps = pS[i % 4]
                        wbank = [exp_tok[i - 4]] if i >= 4 else list(prev_exps)
                        if sweep == 0:
                            for dc in range(2):
                                env.mm3(o, ps[:],
                                        K_hi[:, (2 * g2 + dc) * T + i * 128:
                                             (2 * g2 + dc) * T + i * 128 + 128],
                                        K_lo[:, (2 * g2 + dc) * T + i * 128:
                                             (2 * g2 + dc) * T + i * 128 + 128],
                                        qh[:, dc * T + j * 512:dc * T + j * 512 + 512],
                                        ql[:, dc * T + j * 512:dc * T + j * 512 + 512],
                                        start=(dc == 0), stop=(dc == 1),
                                        waits=([wq1, wq2] + wbank) if dc == 0 else (),
                                        inc=(dc == 1))
                        else:
                            env.ten(o, lambda e, i=i, j=j, p=ps: e.matmul(
                                p[:], kp_row[0:1, i * 128:(i + 1) * 128],
                                qp_row[0:1, j * 512:(j + 1) * 512], start=True, stop=True),
                                waits=[w_rows] + wbank + (pv_gate if i < 4 else []))
                        wmm = env.tok("Tn")
                        scl = 0.0625 if sweep == 0 else 1.0
                        wex = env.scn(o, lambda e, p=ps, b=eb[i % 4], s=scl: e.activation(
                            out=b[:], in_=p[:], func=AF.Exp, scale=s),
                            waits=[wmm] + ([split_tok[i - 4]] if i >= 4 else []))
                        exp_tok.append(wex)
                        ksl = i - 4 * j
                        if 0 <= ksl <= 3:
                            wmk = env.ven(o, lambda e, k=ksl, a=ebm[i % 4], b=eb[i % 4]:
                                          e.tensor_tensor(a[:], b[:],
                                                          mask4[:, k * 512:(k + 1) * 512],
                                                          OP.mult), waits=[wex])
                            esrc = ebm[i % 4]
                        else:
                            wmk = wex
                            esrc = eb[i % 4]
                        slot = i % 4
                        whh, wll = env.split(o, e_hi[slot][:], e_lo[slot][:], esrc[:],
                                             waits=[wmk])
                        split_tok.append(wll)
                        base = i * VROW + g2 * 256
                        for dc in range(2):
                            env.mm3(o, pO[dc][:],
                                    V_hi[:, base + dc * 128: base + dc * 128 + 128],
                                    V_lo[:, base + dc * 128: base + dc * 128 + 128],
                                    e_hi[slot][:], e_lo[slot][:],
                                    start=(i == 0), stop=(i == ni - 1),
                                    waits=([whh, wll] + pv_gate) if i == 0 else [whh, wll],
                                    inc=False)
                        o.em("tensor", lambda e, i=i, s=slot: e.matmul(
                            pR[0:1, :], V_hi[:, i * VROW + 512:i * VROW + 513],
                            e_hi[s][:], start=(i == 0), stop=False),
                            waits=([whh] + pv_gate) if i == 0 else [whh])
                        env.ten(o, lambda e, i=i, s=slot, last=(i == ni - 1): e.matmul(
                            pR[0:1, :], V_hi[:, i * VROW + 512:i * VROW + 513],
                            e_lo[s][:], start=False, stop=last), waits=[wll])
                    wpv = env.tok("Tn")
                    dsts = of32 if sweep == 0 else og32
                    rdst = rr if sweep == 0 else rrg
                    wcb = [comb_prev] if comb_prev else []
                    ev0 = env.scn(o, lambda e, d=dsts: e.copy(d[0][:], pO[0][:]),
                                  waits=[wpv] + wcb)
                    ev1 = env.scn(o, lambda e, d=dsts: e.copy(d[1][:], pO[1][:]))
                    evr = env.ven(o, lambda e, a=rdst: e.reciprocal(a[:], pR[0:1, :]),
                                  waits=[wpv] + wcb)
                    evac_toks += [ev0, ev1, evr]
                    pv_gate = [ev0, ev1, evr]
                    prev_exps = exp_tok[-4:]
                wa = env.ven(o, lambda e, h=h: e.tensor_scalar(
                    arow[:], rr[:], hs1m_row[0:1, h:h + 1], None, OP.mult), waits=evac_toks)
                wb = env.ven(o, lambda e, h=h: e.tensor_scalar(
                    brow[:], rrg[:], hs_row[0:1, h:h + 1], None, OP.mult))
                wba = env.ten(o, lambda e: e.matmul(pB[:], ones_row[:], arow[0:1, :],
                                                    start=True, stop=True), waits=[wa])
                wt1a = env.ven(o, lambda e: e.tensor_tensor(of32[0][:], of32[0][:], pB[:],
                                                            OP.mult), waits=[wba])
                wt1b = env.ven(o, lambda e: e.tensor_tensor(of32[1][:], of32[1][:], pB[:],
                                                            OP.mult))
                wbb = env.ten(o, lambda e: e.matmul(pB[:], ones_row[:], brow[0:1, :],
                                                    start=True, stop=True),
                              waits=[wb, wt1a, wt1b])
                for dc in range(2):
                    wt2 = env.ven(o, lambda e, dc=dc: e.tensor_tensor(
                        og32[dc][:], og32[dc][:], pB[:], OP.mult), waits=[wbb])
                    wt3 = env.ven(o, lambda e, dc=dc: e.tensor_tensor(
                        ohf[:], of32[dc][:], og32[dc][:], OP.add))
                    whh, wll = env.split(o, oh_hi[:], oh_lo[:], ohf[:],
                                         waits=[wt3] + (oh_store_prev or []))
                    go = (2 * h + dc) * T + j * 512
                    wo_a = env.dma(o, dt["S_oh_hi"][:, go:go + 512], oh_hi[:],
                                   waits=[whh, wll], eng="gpsimd")
                    wo_b = env.dma(o, dt["S_oh_lo"][:, go:go + 512], oh_lo[:],
                                   eng="gpsimd")
                    oh_store_prev = [wo_a, wo_b]
                comb_prev = env.tok("Vv")
            q_free[h % 2] = env.tok("Tn")
        env.mile["S_oh"] = env.qtok("G")
        env.fence(o)
        o.emit(blk)
      _free(scA2)
    _free(scA)

    scCD = []
    h2fm_hi = sb("h2fm_hi", [128, NG * T], BF16, scCD)
    h2fm_lo = sb("h2fm_lo", [128, NG * T], BF16, scCD)
    if "C" in phases:
      scC = []
      with nc.Block("c") as blk:
        o = Ops()
        wbh = [sb(f"cwbh{i}", [128, C], BF16, scC) for i in range(2)]
        wbl = [sb(f"cwbl{i}", [128, C], BF16, scC) for i in range(2)]
        xnP_h = sb("cxnh", [128, NG * 256], BF16, scC)
        xnP_l = sb("cxnl", [128, NG * 256], BF16, scC)
        vfm_hi = sb("cvfh", [128, NG * 256], BF16, scC)
        vfm_lo = sb("cvfl", [128, NG * 256], BF16, scC)
        zaw = [sb(f"czaw{i}", [128, C], F32, scC) for i in range(2)]
        xcb = sb("cxb", [128, C], F32, scC)
        c0_bc = sb("c0_bc", [128, C], BF16, scC)
        acc = sb("cacc", [128, 2], F32, scC)
        prev = {"lo": None, "mm": None, "st": None}
        for g in range(NG):
            slot = g % 2
            wl_ = env.dma(o, zaw[0][:], dt["wo"][g * 128:(g + 1) * 128, :],
                          waits=[prev["lo"], prev["mm"]])
            whh, wll = env.split(o, wbh[slot][:], wbl[slot][:], zaw[0][:],
                                 waits=[wl_] + (prev["st"] or []))
            for dtl in range(4):
                env.ten(o, lambda e, g=g, d=dtl: e.matmul(
                    P[4 + d][0:1, :], bvf[:, g:g + 1], zaw[0][:, d * 512:(d + 1) * 512],
                    start=(g == 0), stop=(g == NG - 1)), waits=[wl_])
            prev["mm"] = env.tok("Tn")
            prev["lo"] = wll
            ws_a = env.dma(o, dt["S_wo_hi"][:, g * C:(g + 1) * C], wbh[slot][:],
                           waits=[whh, wll], eng="gpsimd")
            ws_b = env.dma(o, dt["S_wo_lo"][:, g * C:(g + 1) * C], wbl[slot][:],
                           eng="gpsimd")
            prev["st"] = [ws_a, ws_b]
        wc0 = env.tok("Tn")
        for dtl in range(4):
            env.ven(o, lambda e, d=dtl: e.tensor_copy(
                zaw[0][0:1, d * 512:(d + 1) * 512], P[4 + d][0:1, :]), waits=[wc0])
        wrow = env.tok("Vv")
        for dtl in range(4):
            wm = env.ten(o, lambda e, d=dtl: e.matmul(
                P[4 + d][:], ones_row[:], zaw[0][0:1, d * 512:(d + 1) * 512],
                start=True, stop=True), waits=[wrow])
            env.ven(o, lambda e, d=dtl: e.tensor_copy(c0_bc[:, d * 512:(d + 1) * 512],
                                                      P[4 + d][:]), waits=[wm])
        wc0bc = env.tok("Vv")
        for g in range(NG):
            slot = g % 2
            wl_ = env.dma(o, zaw[0][:], dt["wv"][g * 128:(g + 1) * 128, :],
                          waits=[prev["lo"], wc0bc if g == 0 else None])
            wsc = env.ven(o, lambda e, g=g: e.tensor_scalar(
                zaw[0][:], zaw[0][:], n1w_fm[:, g:g + 1], None, OP.mult), waits=[wl_])
            whh, wll = env.split(o, wbh[slot][:], wbl[slot][:], zaw[0][:],
                                 waits=[wsc] + (prev["st"] or []))
            prev["lo"] = wll
            ws_a = env.dma(o, dt["S_wv_hi"][:, g * C:(g + 1) * C], wbh[slot][:],
                           waits=[whh, wll], eng="gpsimd")
            ws_b = env.dma(o, dt["S_wv_lo"][:, g * C:(g + 1) * C], wbl[slot][:],
                           eng="gpsimd")
            prev["st"] = [ws_a, ws_b]
        env.mile["S_wv"] = list(prev["st"])
        h2_tr = {}
        pfree = wc0bc
        za_store = {}
        vload_tok = {}
        woload_tok = {}
        for tp in range(8):
            t0 = 2 * tp
            wxn1 = env.dma(o, xnP_h[:].rearrange("p (g t) -> p g t", g=NG),
                           dt["S_xn_hi"][:].rearrange("p (g t) -> p g t", g=NG)
                           [:, :, t0 * 128:t0 * 128 + 256],
                           waits=(env.mile["S_xnJ"].get(t0 // 4) or [])
                           + (env.mile.get("S_wv") or []) + [env.tok("Tn")]
                           + (h2_tr.get(t0 - 1) or []))
            wxn2 = env.dma(o, xnP_l[:].rearrange("p (g t) -> p g t", g=NG),
                           dt["S_xn_lo"][:].rearrange("p (g t) -> p g t", g=NG)
                           [:, :, t0 * 128:t0 * 128 + 256])
            vsp = []
            mmg = {}
            for rnd in range(2):
                for g in range(NG):
                    k = rnd * NG + g
                    slot = k % 2
                    ww = [mmg[k - 2]] if k >= 2 else [pfree]
                    co = g * C + rnd * 1024
                    wl1 = env.dma(o, wbh[slot][:, 0:1024], dt["S_wv_hi"][:, co:co + 1024],
                                  waits=ww + (env.mile.get("S_wv") or []))
                    wl2 = env.dma(o, wbl[slot][:, 0:1024], dt["S_wv_lo"][:, co:co + 1024],
                                  waits=ww + (env.mile.get("S_wv") or []), eng="scalar")
                    for e8 in range(8):
                        env.mm3(o, P[e8][:, 0:256],
                                wbh[slot][:, e8 * 128:(e8 + 1) * 128],
                                wbl[slot][:, e8 * 128:(e8 + 1) * 128],
                                xnP_h[:, g * 256:(g + 1) * 256],
                                xnP_l[:, g * 256:(g + 1) * 256],
                                start=(g == 0), stop=(g == NG - 1),
                                waits=([wl1, wl2]
                                       + ([wxn1, wxn2] if g == 0 else [])
                                       + ([vsp[-1]] if (g == 0 and rnd == 1) else []))
                                if e8 == 0 else (),
                                inc=(e8 == 7))
                    mmg[k] = env.tok("Tn")
                wvm = env.tok("Tn")
                for e8 in range(8):
                    eps = rnd * 8 + e8
                    _, wsp = env.split(o, vfm_hi[:, eps * 256:(eps + 1) * 256],
                                       vfm_lo[:, eps * 256:(eps + 1) * 256],
                                       P[e8][:, 0:256],
                                       waits=[wvm] if e8 == 0 else ())
                    vsp.append(wsp)
            vdone = vsp[-1]
            mmg = {}
            for eps in range(NG):
                slot = eps % 2
                ww = [mmg[eps - 2]] if eps >= 2 else [env.tok("Tn")]
                wl1 = env.dma(o, wbh[slot][:], dt["S_wo_hi"][:, eps * C:(eps + 1) * C],
                              waits=ww)
                wl2 = env.dma(o, wbl[slot][:], dt["S_wo_lo"][:, eps * C:(eps + 1) * C],
                              waits=ww, eng="scalar")
                for ti in range(2):
                    for dtl in range(4):
                        env.mm3(o, P[ti * 4 + dtl][:],
                                vfm_hi[:, eps * 256 + ti * 128: eps * 256 + ti * 128 + 128],
                                vfm_lo[:, eps * 256 + ti * 128: eps * 256 + ti * 128 + 128],
                                wbh[slot][:, dtl * 512:(dtl + 1) * 512],
                                wbl[slot][:, dtl * 512:(dtl + 1) * 512],
                                start=(eps == 0), stop=(eps == NG - 1),
                                waits=([wl1, wl2] + ([vdone] if eps == 0 else []))
                                if (ti == 0 and dtl == 0) else (),
                                inc=(ti == 1 and dtl == 3))
                mmg[eps] = env.tok("Tn")
            wop = env.tok("Tn")
            wza_tok = {}
            for ti in range(2):
                tau = t0 + ti
                for dtl in range(4):
                    env.ven(o, lambda e, p=P[ti * 4 + dtl], d=dtl, tau=tau, ti=ti:
                            e.scalar_tensor_tensor(zaw[ti][:, d * 512:(d + 1) * 512], p[:],
                                                   st["g1"][:, tau:tau + 1],
                                                   c0_bc[:, d * 512:(d + 1) * 512],
                                                   OP.mult, OP.add),
                            waits=([wop, wc0bc] + (env.mile.get("S_wv") or [])
                                   + ([za_store.get(tau - 1 - ti)] if tau - 1 - ti >= 0
                                      else []))
                            if dtl == 0 else ())
                wza_tok[ti] = env.tok("Vv")
                za_store[tau] = env.dma(o, dt["S_za"][tau * 128:(tau + 1) * 128, :],
                                        zaw[ti][:], waits=[wza_tok[ti]], eng="gpsimd")
            pfree = wza_tok[1]
            prev_split = None
            for ti in range(2):
                tau = t0 + ti
                wxl = env.dma(o, xcb[:], dt["x"][tau * 128:(tau + 1) * 128, :],
                              waits=[wza_tok[1]] if ti == 0 else [prev_split])
                wxa = env.ven(o, lambda e, tau=tau: e.tensor_scalar(
                    xcb[:], xcb[:], st["rs"][:, tau:tau + 1], st["w"][:, tau:tau + 1],
                    OP.mult, OP.add), waits=[wxl])
                wh_ = env.ven(o, lambda e, ti=ti: e.tensor_tensor(xcb[:], zaw[ti][:], xcb[:],
                                                                  OP.add), waits=[wxa])
                wsq = env.scn(o, lambda e, ti=ti: e.activation(
                    out=zaw[ti][:], in_=xcb[:], func=AF.Square, accum_out=acc[:, 0:1]),
                    waits=[wh_, za_store[tau]])
                vpe2, sd2, r02, t12, t22 = (sm[:, k:k + 1] for k in range(10, 15))
                env.ven(o, lambda e, a=vpe2: e.tensor_scalar(
                    a, acc[:, 0:1], 1.0 / C, EPS_RMS, OP.mult, OP.add), waits=[wsq])
                wv_ = env.tok("Vv")
                wsd = env.scn(o, lambda e, a=sd2, b=vpe2: e.sqrt(a, b), waits=[wv_])
                wrc = env.ven(o, lambda e, a=r02, b=sd2: e.reciprocal(a, b), waits=[wsd])
                env.ven(o, lambda e, a=t12, b=r02: e.tensor_tensor(a, b, b, OP.mult),
                        waits=[wrc])
                env.ven(o, lambda e, a=t22, b=t12, c=vpe2: e.tensor_tensor(a, b, c, OP.mult))
                env.ven(o, lambda e, a=t22: e.tensor_scalar(a, a, -0.5, 1.5, OP.mult, OP.add))
                env.ven(o, lambda e, a=r02, b=t22, tau=tau: e.tensor_tensor(
                    st["rs2"][:, tau:tau + 1], a, b, OP.mult))
                wh2 = env.ven(o, lambda e, tau=tau: e.tensor_scalar(
                    xcb[:], xcb[:], st["rs2"][:, tau:tau + 1], None, OP.mult))
                wtp = h2_tr[tau - 1] if tau >= 1 else []
                whh, wll = env.split(o, xnP_h[:, 0:C], xnP_l[:, 0:C], xcb[:],
                                     waits=[wh2] + wtp)
                prev_split = wll
                dsth = h2fm_hi[:].rearrange("p (g t) -> p g t", g=NG)[
                    :, :, tau * 128:(tau + 1) * 128]
                dstl = h2fm_lo[:].rearrange("p (g t) -> p g t", g=NG)[
                    :, :, tau * 128:(tau + 1) * 128]
                wht_a = env.dmaT(o, dsth, xnP_h[:, 0:C], waits=[whh, wll])
                wht_b = env.dmaT(o, dstl, xnP_l[:, 0:C], waits=[wll])
                h2_tr[tau] = [wht_a, wht_b]
        env.mile["S_za"] = env.qtok("G")
        env.mile["h2fm"] = list(h2_tr[NT - 1])
        env.fence(o)
        o.emit(blk)
      _free(scC)

    if "D" in phases:
      scD = []
      with nc.Block("d") as blk:
        o = Ops()
        wgf = [sb(f"dwg{i}", [128, NG * 128], F32, scD) for i in range(2)]
        wuf = [sb(f"dwu{i}", [128, NG * 128], F32, scD) for i in range(2)]
        wgh = [sb(f"dwgh{i}", [128, NG * 128], BF16, scD) for i in range(2)]
        wgl = [sb(f"dwgl{i}", [128, NG * 128], BF16, scD) for i in range(2)]
        wuh = [sb(f"dwuh{i}", [128, NG * 128], BF16, scD) for i in range(2)]
        wul = [sb(f"dwul{i}", [128, NG * 128], BF16, scD) for i in range(2)]
        sg = sb("dsg", [128, 512], F32, scD)
        pr = sb("dpr", [128, 512], F32, scD)
        prh = [sb(f"dprh{i}", [128, 512], BF16, scD) for i in range(2)]
        prl = [sb(f"dprl{i}", [128, 512], BF16, scD) for i in range(2)]
        mm_tok = []
        spl_toks = []
        ev_tok = None
        st_tok = {}
        for phi in range(NF):
            slot = phi % 2
            wprev = ([mm_tok[phi - 2], spl_toks[phi - 2]] if phi >= 2
                     else (env.mile.get("h2fm") or []))
            wg_l = []
            for q4 in range(4):
                ap_g = dt["w_gate"].rearrange("(g p) f -> p g f", p=128)[
                    :, q4 * 4:(q4 + 1) * 4, phi * 128:(phi + 1) * 128]
                ap_u = dt["w_up"].rearrange("(g p) f -> p g f", p=128)[
                    :, q4 * 4:(q4 + 1) * 4, phi * 128:(phi + 1) * 128]
                wg_l.append(env.dma(
                    o, wgf[slot][:].rearrange("p (g f) -> p g f", g=NG)
                    [:, q4 * 4:(q4 + 1) * 4, :], ap_g,
                    waits=wprev if q4 == 0 else ()))
                wg_l.append(env.dma(
                    o, wuf[slot][:].rearrange("p (g f) -> p g f", g=NG)
                    [:, q4 * 4:(q4 + 1) * 4, :], ap_u))
            for g in range(NG):
                env.ven(o, lambda e, s=slot, g=g: e.tensor_scalar(
                    wgf[s][:, g * 128:(g + 1) * 128], wgf[s][:, g * 128:(g + 1) * 128],
                    n2w_fm[:, g:g + 1], None, OP.mult), waits=wg_l if g == 0 else ())
                env.ven(o, lambda e, s=slot, g=g: e.tensor_scalar(
                    wuf[s][:, g * 128:(g + 1) * 128], wuf[s][:, g * 128:(g + 1) * 128],
                    n2w_fm[:, g:g + 1], None, OP.mult))
            wts = env.tok("Vv")
            _, wgs = env.split(o, wgh[slot][:], wgl[slot][:], wgf[slot][:], waits=[wts])
            _, wus = env.split(o, wuh[slot][:], wul[slot][:], wuf[slot][:], waits=[wts])
            spl_toks.append(wus)
            w0 = [wgs, wus] + ([ev_tok] if ev_tok else [])
            for g in range(NG):
                for J in range(NJ):
                    env.mm3(o, P[J][:],
                            wgh[slot][:, g * 128:(g + 1) * 128],
                            wgl[slot][:, g * 128:(g + 1) * 128],
                            h2fm_hi[:, g * T + J * 512: g * T + J * 512 + 512],
                            h2fm_lo[:, g * T + J * 512: g * T + J * 512 + 512],
                            start=(g == 0), stop=(g == NG - 1),
                            waits=w0 if (g == 0 and J == 0) else (), inc=False)
                for J in range(NJ):
                    env.mm3(o, P[4 + J][:],
                            wuh[slot][:, g * 128:(g + 1) * 128],
                            wul[slot][:, g * 128:(g + 1) * 128],
                            h2fm_hi[:, g * T + J * 512: g * T + J * 512 + 512],
                            h2fm_lo[:, g * T + J * 512: g * T + J * 512 + 512],
                            start=(g == 0), stop=(g == NG - 1),
                            inc=(g == NG - 1 and J == NJ - 1))
            wmm = env.tok("Tn")
            mm_tok.append(wmm)
            for J in range(NJ):
                wsl = env.scn(o, lambda e, J=J: e.activation(out=sg[:], in_=P[J][:],
                                                             func=AF.Silu), waits=[wmm])
                wpr = env.ven(o, lambda e, J=J: e.tensor_tensor(pr[:], sg[:], P[4 + J][:],
                                                                OP.mult), waits=[wsl])
                sl2 = J % 2
                wst = (st_tok[(phi, J - 2)] if (phi, J - 2) in st_tok
                       else (st_tok[(phi - 1, J + 2)] if (phi - 1, J + 2) in st_tok
                             else []))
                whh, wll = env.split(o, prh[sl2][:], prl[sl2][:], pr[:], waits=[wpr] + wst)
                go = phi * T + J * 512
                wp_a = env.dma(o, dt["S_pr_hi"][:, go:go + 512], prh[sl2][:],
                               waits=[whh, wll], eng="gpsimd")
                wp_b = env.dma(o, dt["S_pr_lo"][:, go:go + 512], prl[sl2][:],
                               eng="gpsimd")
                st_tok[(phi, J)] = [wp_a, wp_b]
            ev_tok = env.tok("Vv")
        env.mile["S_pr"] = env.qtok("G")
        env.fence(o)
        o.emit(blk)
      _free(scD)
    _free(scCD)

    if "E1" in phases:
      scE = []
      woh = sb("e1woh", [128, 8 * C], BF16, scE)
      wol = sb("e1wol", [128, 8 * C], BF16, scE)
      wstg = sb("e1w", [128, C], F32, scE)
      ohJ_h = sb("e1ohh", [128, 8 * 512], BF16, scE)
      ohJ_l = sb("e1ohl", [128, 8 * 512], BF16, scE)
      aow = sb("e1aow", [128, C], F32, scE)
      aop = sb("e1aop", [128, C], F32, scE)
      with nc.Block("e1") as blk:
        o = Ops()
        ev_prev = None
        for half in range(2):
            spl_tok = []
            for gi in range(8):
                g = half * 8 + gi
                wws = [spl_tok[gi - 1]] if gi >= 1 else (
                    [env.tok("Tn")] if half == 1 else [])
                wl_ = env.dma(o, wstg[:], dt["Wo"][g * 128:(g + 1) * 128, :], waits=wws)
                _, wll = env.split(o, woh[:, gi * C:(gi + 1) * C],
                                   wol[:, gi * C:(gi + 1) * C], wstg[:], waits=[wl_])
                spl_tok.append(wll)
            wsplit = spl_tok[-1]
            for J in range(NJ):
                w1 = env.dma(o, ohJ_h[:].rearrange("p (g t) -> p g t", g=8),
                             dt["S_oh_hi"][:].rearrange("p (g t) -> p g t", g=NG)
                             [:, half * 8:half * 8 + 8, J * 512:(J + 1) * 512],
                             waits=env.mile.get("S_oh", []) + [env.tok("Tn")])
                w2 = env.dma(o, ohJ_l[:].rearrange("p (g t) -> p g t", g=8),
                             dt["S_oh_lo"][:].rearrange("p (g t) -> p g t", g=NG)
                             [:, half * 8:half * 8 + 8, J * 512:(J + 1) * 512])
                for ti in range(4):
                    tau = J * 4 + ti
                    for gi in range(8):
                        for et in range(4):
                            env.mm3(o, P[et][:],
                                    ohJ_h[:, gi * 512 + ti * 128: gi * 512 + ti * 128 + 128],
                                    ohJ_l[:, gi * 512 + ti * 128: gi * 512 + ti * 128 + 128],
                                    woh[:, gi * C + et * 512: gi * C + et * 512 + 512],
                                    wol[:, gi * C + et * 512: gi * C + et * 512 + 512],
                                    start=(gi == 0), stop=(gi == 7),
                                    waits=([w1, w2, wsplit] + ([ev_prev] if ev_prev else []))
                                    if (gi == 0 and et == 0) else (),
                                    inc=(gi == 7 and et == 3))
                    wmm = env.tok("Tn")
                    if half == 0:
                        for et in range(4):
                            env.ven(o, lambda e, et=et: e.tensor_copy(
                                aow[:, et * 512:(et + 1) * 512], P[et][:]),
                                waits=([wmm] + env.qtok("G")) if et == 0 else ())
                        wev = env.tok("Vv")
                        env.dma(o, dt["S_ao"][tau * 128:(tau + 1) * 128, :], aow[:],
                                waits=[wev], eng="gpsimd")
                    else:
                        wld = env.dma(o, aop[:], dt["S_ao"][tau * 128:(tau + 1) * 128, :],
                                      waits=env.mile.get("e1_half0", []) + [ev_prev])
                        for et in range(4):
                            env.ven(o, lambda e, et=et: e.tensor_tensor(
                                aow[:, et * 512:(et + 1) * 512], P[et][:],
                                aop[:, et * 512:(et + 1) * 512], OP.add),
                                waits=([wmm, wld] + env.qtok("G")) if et == 0 else ())
                        wev = env.tok("Vv")
                        env.dma(o, dt["S_ao"][tau * 128:(tau + 1) * 128, :], aow[:],
                                waits=[wev], eng="gpsimd")
                    ev_prev = env.tok("Vv")
            if half == 0:
                env.mile["e1_half0"] = env.qtok("G")
        env.mile["S_ao"] = env.qtok("G")
        env.fence(o)
        o.emit(blk)
      _free(scE)

    if "E2" in phases:
      scE = []
      NH3 = (15, 15, 14)
      wdh = sb("e2wdh", [128, 15 * C], BF16, scE)
      wdl = sb("e2wdl", [128, 15 * C], BF16, scE)
      wstg = sb("e2w", [128, C], F32, scE)
      prJ_h = sb("e2prh", [128, 15 * 512], BF16, scE)
      prJ_l = sb("e2prl", [128, 15 * 512], BF16, scE)
      zw = sb("e2zw", [128, C], F32, scE)
      zp = sb("e2zp", [128, C], F32, scE)
      aot = sb("e2ao", [128, C], F32, scE)
      xt_ = sb("e2x", [128, C], F32, scE)
      ow = sb("e2ow", [128, C], F32, scE)
      with nc.Block("e2") as blk:
        o = Ops()
        ev_prev = None
        fbase = 0
        for third in range(3):
            nf3 = NH3[third]
            spl_tok = []
            for fi in range(nf3):
                f = fbase + fi
                wws = [spl_tok[fi - 1]] if fi >= 1 else (
                    [env.tok("Tn")] if third > 0 else [])
                wl_ = env.dma(o, wstg[:], dt["w_down"][f * 128:(f + 1) * 128, :], waits=wws)
                _, wll = env.split(o, wdh[:, fi * C:(fi + 1) * C],
                                   wdl[:, fi * C:(fi + 1) * C], wstg[:], waits=[wl_])
                spl_tok.append(wll)
            wsplit = spl_tok[-1]
            for J in range(NJ):
                w1 = env.dma(o, prJ_h[:, 0:nf3 * 512].rearrange("p (g t) -> p g t", g=nf3),
                             dt["S_pr_hi"][:].rearrange("p (g t) -> p g t", g=NF)
                             [:, fbase:fbase + nf3, J * 512:(J + 1) * 512],
                             waits=env.mile.get("S_pr", []) + [env.tok("Tn")])
                w2 = env.dma(o, prJ_l[:, 0:nf3 * 512].rearrange("p (g t) -> p g t", g=nf3),
                             dt["S_pr_lo"][:].rearrange("p (g t) -> p g t", g=NF)
                             [:, fbase:fbase + nf3, J * 512:(J + 1) * 512])
                for ti in range(4):
                    tau = J * 4 + ti
                    for fi in range(nf3):
                        for et in range(4):
                            env.mm3(o, P[et][:],
                                    prJ_h[:, fi * 512 + ti * 128: fi * 512 + ti * 128 + 128],
                                    prJ_l[:, fi * 512 + ti * 128: fi * 512 + ti * 128 + 128],
                                    wdh[:, fi * C + et * 512: fi * C + et * 512 + 512],
                                    wdl[:, fi * C + et * 512: fi * C + et * 512 + 512],
                                    start=(fi == 0), stop=(fi == nf3 - 1),
                                    waits=([w1, w2, wsplit] + ([ev_prev] if ev_prev else []))
                                    if (fi == 0 and et == 0) else (),
                                    inc=(fi == nf3 - 1 and et == 3))
                    wmm = env.tok("Tn")
                    if third < 2:
                        wza = env.dma(o, zp[:], dt["S_za"][tau * 128:(tau + 1) * 128, :],
                                      waits=(env.mile.get("S_za", []) if third == 0
                                             else env.mile.get("e2_prev", [])) + [ev_prev])
                        for et in range(4):
                            env.ven(o, lambda e, et=et: e.tensor_tensor(
                                zw[:, et * 512:(et + 1) * 512], P[et][:],
                                zp[:, et * 512:(et + 1) * 512], OP.add),
                                waits=([wmm, wza] + env.qtok("G")) if et == 0 else ())
                        wev = env.tok("Vv")
                        env.dma(o, dt["S_za"][tau * 128:(tau + 1) * 128, :], zw[:],
                                waits=[wev], eng="gpsimd")
                    else:
                        wza = env.dma(o, zp[:], dt["S_za"][tau * 128:(tau + 1) * 128, :],
                                      waits=env.mile.get("e2_prev", []) + [ev_prev])
                        wao = env.dma(o, aot[:], dt["S_ao"][tau * 128:(tau + 1) * 128, :],
                                      waits=env.mile.get("S_ao", []))
                        wxl = env.dma(o, xt_[:], dt["x"][tau * 128:(tau + 1) * 128, :])
                        for et in range(4):
                            env.ven(o, lambda e, et=et: e.tensor_tensor(
                                zw[:, et * 512:(et + 1) * 512], P[et][:],
                                zp[:, et * 512:(et + 1) * 512], OP.add),
                                waits=([wmm, wza] + env.qtok("G")) if et == 0 else ())
                        env.ven(o, lambda e, tau=tau: e.tensor_scalar(
                            zw[:], zw[:], st["rt"][:, tau:tau + 1], None, OP.mult))
                        env.ven(o, lambda e: e.tensor_tensor(zw[:], zw[:], aot[:], OP.add),
                                waits=[wao])
                        env.ven(o, lambda e, tau=tau: e.tensor_scalar(
                            xt_[:], xt_[:], st["u"][:, tau:tau + 1], st["w"][:, tau:tau + 1],
                            OP.mult, OP.add), waits=[wxl])
                        wfin = env.ven(o, lambda e: e.tensor_tensor(ow[:], zw[:], xt_[:],
                                                                    OP.add))
                        env.dma(o, out[tau * 128:(tau + 1) * 128, :], ow[:],
                                waits=[wfin], eng="gpsimd")
                    ev_prev = env.tok("Vv")
            env.mile["e2_prev"] = env.qtok("G")
            fbase += nf3
        o.em("sync", lambda e: e.nop(), waits=env.qtok("G") + env.qtok("L") + env.qtok("A") + env.qtok("X"))
        env.fence(o)
        o.emit(blk)
      _free(scE)


def kernel(**inputs):
    consts = make_consts()
    nc = build_nc()
    def arr(k):
        return np.ascontiguousarray(np.asarray(inputs[k], dtype=np.float32))
    x = arr("L0_attn_x")
    base = {
        "Wq": arr("L0_attn_Wq"), "Wk": arr("L0_attn_Wk"), "Wv": arr("L0_attn_Wv"),
        "Wo": arr("L0_attn_Wo"),
        "head_scales": arr("L0_attn_head_scales").reshape(1, H),
        "w_router": arr("L0_ffn_w_router"),
        "b_router": arr("L0_ffn_b_router").reshape(1, 1),
        "norm1_w": arr("L0_ffn_norm1_w"), "wv": arr("L0_ffn_wv"), "bv": arr("L0_ffn_bv"),
        "wo": arr("L0_ffn_wo"), "norm2_w": arr("L0_ffn_norm2_w"),
        "w_gate": arr("L0_ffn_w_gate"), "w_up": arr("L0_ffn_w_up"),
        "w_down": arr("L0_ffn_w_down"),
    }
    base.update(consts)
    in_maps = [dict(base, x=np.ascontiguousarray(x[i])) for i in range(8)]
    res = run_bass_kernel_spmd(nc, in_maps, core_ids=list(range(8)))
    return np.stack([r["out"] for r in res.results], axis=0)


# version 05
# speedup vs baseline: 1.1967x; 1.0108x over previous
import math
import numpy as np
import concourse.bass as bass
import concourse.mybir as mybir
from concourse import library_config
from concourse.bass_utils import run_bass_kernel_spmd

F32 = mybir.dt.float32
BF16 = mybir.dt.bfloat16
AF = mybir.ActivationFunctionType
OP = mybir.AluOpType

T = 2048
C = 2048
FF = 5632
H = 8
NT = 16
NJ = 4
NG = 16
NF = 44
VROW = 528
EPS_LN = 1e-5
EPS_RMS = 1e-6

ALL_PHASES = ("P1", "A1", "A2", "C", "D", "E1", "E2")


def make_consts():
    import jax
    import jax.numpy as jnp
    with jax.default_device(jax.devices("cpu")[0]):
        inv = 1.0 / 10000.0 ** (jnp.arange(0, 256, 2, dtype=jnp.float32) / 256)
        f = jnp.arange(T, dtype=jnp.float32)[:, None] * inv[None, :]
        cos_hi = np.ascontiguousarray(np.asarray(jnp.cos(f)).T)
        sin_hi = np.ascontiguousarray(np.asarray(jnp.sin(f)).T)
    cos_lo = np.zeros_like(cos_hi)
    sin_lo = np.zeros_like(sin_hi)
    m = np.zeros((128, 4 * 512), dtype=np.float32)
    r = np.arange(128)[:, None]
    c = np.arange(512)[None, :]
    for k in range(4):
        m[:, k * 512:(k + 1) * 512] = (c >= k * 128 + r).astype(np.float32)
    trig = np.array([[sx, sy, sz] for sx in (1.0, -1.0) for sy in (1.0, -1.0)
                     for sz in (1.0, -1.0)], dtype=np.float64) / math.sqrt(3.0)
    dirs64 = np.ascontiguousarray(trig.T)
    return {"cos_hi": cos_hi, "cos_lo": cos_lo, "sin_hi": sin_hi, "sin_lo": sin_lo,
            "bandmask": m, "dirs64": dirs64.astype(np.float32),
            "dirs_lo64": (dirs64 - dirs64.astype(np.float32)).astype(np.float32)}


class Ops:
    def __init__(self):
        self.ops = []

    def em(self, eng, fn, inc=None, waits=()):
        self.ops.append((eng, fn, inc, tuple(w for w in waits if w is not None)))

    def emit(self, block):
        by_eng = {}
        for op in self.ops:
            by_eng.setdefault(op[0], []).append(op)
        for eng, items in by_eng.items():
            def body(e, items=items):
                seen = {}
                for _, fn, inc, waits in items:
                    for sem, val in waits:
                        if val > 0 and seen.get(id(sem), -1) < val:
                            e.wait_ge(sem, val)
                            seen[id(sem)] = val
                    inst = fn(e)
                    if inc is not None:
                        inst.then_inc(inc[0], inc[1])
            getattr(block, eng)(body)


class Env:
    DMA_POOL = {"L": 6, "A": 4, "G": 4, "X": 2}

    def __init__(self, nc):
        self.nc = nc
        self._ctx = []
        self.S = {}
        self.V = {}
        for nm in ("Tn", "Vv", "Sc"):
            cm = nc.semaphore(nm)
            self.S[nm] = cm.__enter__()
            self._ctx.append(cm)
            self.V[nm] = 0
        self.dpool = {}
        self.dcnt = {}
        self.didx = {}
        for nm, r in self.DMA_POOL.items():
            self.dpool[nm] = []
            self.dcnt[nm] = []
            self.didx[nm] = 0
            for i in range(r):
                cm = nc.semaphore(f"{nm}{i}")
                self.dpool[nm].append(cm.__enter__())
                self._ctx.append(cm)
                self.dcnt[nm].append(0)
        self.mile = {}

    def sb(self, name, shape, dtype, ctx=None):
        cm = self.nc.sbuf_tensor(name, shape, dtype)
        h = cm.__enter__()
        (ctx if ctx is not None else self._ctx).append(cm)
        return h

    def psum(self, name):
        cm = self.nc.psum_tensor(name, [128, 512], F32)
        h = cm.__enter__()
        self._ctx.append(cm)
        return h

    def tok(self, nm):
        return (self.S[nm], self.V[nm])

    def _dma_slot(self, nm):
        i = self.didx[nm] % len(self.dpool[nm])
        self.didx[nm] += 1
        sem = self.dpool[nm][i]
        prev = self.dcnt[nm][i]
        self.dcnt[nm][i] = prev + 16
        return sem, prev

    def dma(self, o, outap, inap, waits=(), eng="sync", slow=False):
        nm = {"gpsimd": "G", "scalar": "A"}.get(eng, "L")
        sem, prev = self._dma_slot(nm)
        o.em(eng, lambda e, a=outap, b=inap, s=slow: e.dma_start(
            out=a, in_=b, allow_slow_non_contiguous=s),
             inc=(sem, 16), waits=list(waits) + [(sem, prev)])
        return (sem, prev + 16)

    def dmaT(self, o, outap, inap, waits=()):
        sem, prev = self._dma_slot("X")
        o.em("sync", lambda e, a=outap, b=inap: e.dma_start_transpose(a, b),
             inc=(sem, 16), waits=list(waits) + [(sem, prev)])
        return (sem, prev + 16)

    def ven(self, o, fn, waits=()):
        prev = self.V["Vv"]
        self.V["Vv"] = prev + 1
        o.em("vector", fn, inc=(self.S["Vv"], 1),
             waits=list(waits) + [(self.S["Vv"], prev)])
        return (self.S["Vv"], prev + 1)

    def scn(self, o, fn, waits=()):
        prev = self.V["Sc"]
        self.V["Sc"] = prev + 1
        o.em("scalar", fn, inc=(self.S["Sc"], 1),
             waits=list(waits) + [(self.S["Sc"], prev)])
        return (self.S["Sc"], prev + 1)

    def gpn(self, o, fn, waits=()):
        if "Gp" not in self.S:
            cm = self.nc.semaphore("Gp")
            self.S["Gp"] = cm.__enter__()
            self._ctx.append(cm)
            self.V["Gp"] = 0
        prev = self.V["Gp"]
        self.V["Gp"] = prev + 1
        o.em("gpsimd", fn, inc=(self.S["Gp"], 1),
             waits=list(waits) + [(self.S["Gp"], prev)])
        return (self.S["Gp"], prev + 1)

    def ten(self, o, fn, waits=()):
        self.V["Tn"] += 1
        o.em("tensor", fn, inc=(self.S["Tn"], 1), waits=waits)
        return (self.S["Tn"], self.V["Tn"])

    def mm3(self, o, pb, lh, ll, rh, rl, start, stop, waits=(), inc=False):
        o.em("tensor", lambda e, a=pb, b=lh, c=rh, s=start:
             e.matmul(a, b, c, start=s, stop=False), waits=waits)
        o.em("tensor", lambda e, a=pb, b=lh, c=rl:
             e.matmul(a, b, c, start=False, stop=False))
        token = None
        if inc:
            self.V["Tn"] += 1
            token = (self.S["Tn"], self.V["Tn"])
        o.em("tensor", lambda e, a=pb, b=ll, c=rh, s=stop:
             e.matmul(a, b, c, start=False, stop=s),
             inc=(self.S["Tn"], 1) if inc else None)
        return token

    def qtok(self, nm):
        return [(s, c) for s, c in zip(self.dpool[nm], self.dcnt[nm]) if c > 0]

    def fence(self, o):
        o.em("sync", lambda e: e.nop(),
             waits=self.qtok("G") + self.qtok("L") + self.qtok("A") + self.qtok("X"))

    def split(self, o, hi, lo, src, waits=(), hi_on_sc=True):
        if hi_on_sc:
            w1 = self.scn(o, lambda e, a=hi, b=src: e.copy(a, b), waits=waits)
        else:
            w1 = self.ven(o, lambda e, a=hi, b=src: e.tensor_copy(a, b), waits=waits)
        w2 = self.ven(o, lambda e, a=lo, b=src, c=hi:
                      e.tensor_tensor(a, b, c, OP.subtract), waits=[w1])
        return w1, w2


def build_nc(phases=ALL_PHASES):
    nc = bass.Bass("TRN2", target_bir_lowering=False)
    dt = {}

    def din(name, shape):
        dt[name] = nc.dram_tensor(name, shape, F32, kind="ExternalInput").ap()
    din("x", [T, C])
    din("Wq", [C, C]); din("Wk", [C, 512]); din("Wv", [C, 512]); din("Wo", [C, C])
    din("head_scales", [1, H])
    din("w_router", [C]); din("b_router", [1, 1]); din("norm1_w", [C])
    din("wv", [C, C]); din("bv", [C]); din("wo", [C, C]); din("norm2_w", [C])
    din("w_gate", [C, FF]); din("w_up", [C, FF]); din("w_down", [FF, C])
    din("cos_hi", [128, T]); din("cos_lo", [128, T])
    din("sin_hi", [128, T]); din("sin_lo", [128, T])
    din("bandmask", [128, 4 * 512]); din("dirs64", [3, 8]); din("dirs_lo64", [3, 8])
    out = nc.dram_tensor("out", [T, C], F32, kind="ExternalOutput").ap()

    def scr(name, shape, dtype):
        dt[name] = nc.dram_tensor(name, shape, dtype, kind="Internal").ap()
    for nm in ("S_xn_hi", "S_xn_lo", "S_q_hi", "S_q_lo", "S_oh_hi", "S_oh_lo"):
        scr(nm, [128, NG * T], BF16)
    scr("S_wv_hi", [128, NG * C], BF16); scr("S_wv_lo", [128, NG * C], BF16)
    scr("S_wo_hi", [128, NG * C], BF16); scr("S_wo_lo", [128, NG * C], BF16)
    scr("S_pr_hi", [128, NF * T], BF16); scr("S_pr_lo", [128, NF * T], BF16)
    scr("S_wg_hi", [128, NF * C], BF16); scr("S_wg_lo", [128, NF * C], BF16)
    scr("S_wu_hi", [128, NF * C], BF16); scr("S_wu_lo", [128, NF * C], BF16)
    scr("S_za", [T, C], F32)
    scr("S_ao", [T, C], F32)

    env = Env(nc)
    try:
        _program(nc, env, dt, out, phases)
    finally:
        for cm in reversed(env._ctx):
            try:
                cm.__exit__(None, None, None)
            except Exception:
                pass
    return nc


def _free(scope):
    for cm in reversed(scope):
        cm.__exit__(None, None, None)
    scope.clear()


def _program(nc, env, dt, out, phases):
    S = env.S
    sb = env.sb

    st = {nm: sb("st_" + nm, [128, NT], F32) for nm in ("rs", "u", "w", "g1", "rt", "rs2")}
    ones_row = sb("ones_row", [1, 128], F32)
    hs_row = sb("hs_row", [1, H], F32)
    hs1m_row = sb("hs1m_row", [1, H], F32)
    hs_col = sb("hs_col", [128, H], F32)
    hs1m_col = sb("hs1m_col", [128, H], F32)
    brc = sb("brc", [128, 1], F32)
    brs = sb("brs", [1, 1], F32)
    n1w_fm = sb("n1w_fm", [128, NG], F32)
    n2w_fm = sb("n2w_fm", [128, NG], F32)
    bvf = sb("bvf", [128, NG], F32)
    sm = sb("smallA", [128, 16], F32)
    P = [env.psum(f"pb{i}") for i in range(8)]

    sp = []
    wrbc = sb("wrbc", [128, C], F32, sp)
    with nc.Block("p0") as blk:
        o = Ops()
        tmp = sb("p0tmp", [128, C], F32, sp)
        w1 = env.dma(o, hs_row[:], dt["head_scales"])
        wh1 = env.ven(o, lambda e: e.tensor_scalar(hs1m_row[:], hs_row[:], -1.0, 1.0,
                                                   OP.mult, OP.add), waits=[w1])
        won = env.ven(o, lambda e: e.memset(ones_row[:], 1.0))
        for h in range(H):
            wm = env.ten(o, lambda e, h=h: e.matmul(
                P[2][:, 0:1], ones_row[:], hs_row[0:1, h:h + 1],
                start=True, stop=True), waits=[wh1, won, env.tok("Vv")])
            env.ven(o, lambda e, h=h: e.tensor_copy(hs_col[:, h:h + 1], P[2][:, 0:1]),
                    waits=[wm])
            wm = env.ten(o, lambda e, h=h: e.matmul(
                P[3][:, 0:1], ones_row[:], hs1m_row[0:1, h:h + 1],
                start=True, stop=True), waits=[env.tok("Vv")])
            env.ven(o, lambda e, h=h: e.tensor_copy(hs1m_col[:, h:h + 1], P[3][:, 0:1]),
                    waits=[wm])
        wr = env.dma(o, tmp[0:1, 0:C], dt["w_router"].rearrange("(r c) -> r c", r=1))
        wb = env.dma(o, brs[:], dt["b_router"])
        for j in range(4):
            wm = env.ten(o, lambda e, j=j: e.matmul(
                P[0][:], ones_row[:], tmp[0:1, j * 512:(j + 1) * 512],
                start=True, stop=True), waits=[wr, env.tok("Vv")])
            env.ven(o, lambda e, j=j: e.tensor_copy(wrbc[:, j * 512:(j + 1) * 512], P[0][:]),
                    waits=[wm])
        wm = env.ten(o, lambda e: e.matmul(P[1][:, 0:1], ones_row[:], brs[:],
                                           start=True, stop=True),
                     waits=[wb, env.tok("Vv")])
        env.ven(o, lambda e: e.tensor_copy(brc[:], P[1][:, 0:1]), waits=[wm])
        env.dma(o, n1w_fm[:], dt["norm1_w"].rearrange("(g p) -> p g", p=128), slow=True)
        env.dma(o, n2w_fm[:], dt["norm2_w"].rearrange("(g p) -> p g", p=128), slow=True)
        env.dma(o, bvf[:], dt["bv"].rearrange("(g p) -> p g", p=128), slow=True)
        env.fence(o)
        o.emit(blk)

    env.mile["S_xnJ"] = {}
    if "P1" in phases:
      with nc.Block("p1") as blk:
        o = Ops()
        xb = [sb(f"p1x{i}", [128, C], F32, sp) for i in range(2)]
        xnb = sb("p1xn", [128, C], F32, sp)
        hib = [sb(f"p1hi{i}", [128, C], BF16, sp) for i in range(2)]
        lob = [sb(f"p1lo{i}", [128, C], BF16, sp) for i in range(2)]
        dump = sb("p1dump", [128, C], F32, sp)
        acc = sb("p1acc", [128, 4], F32, sp)
        stg_h = [sb(f"p1sh{i}", [128, NG * 512], BF16, sp) for i in range(2)]
        stg_l = [sb(f"p1sl{i}", [128, NG * 512], BF16, sp) for i in range(2)]
        xn_done = []
        tr_done = []
        stg_free = [None, None]
        for tau in range(NT):
            J = tau // 4
            bx = xb[tau % 2]
            wx = env.dma(o, bx[:], dt["x"][tau * 128:(tau + 1) * 128, :],
                         waits=[xn_done[tau - 2]] if tau >= 2 else ())
            ws1 = env.scn(o, lambda e, b=bx: e.activation(
                out=dump[:], in_=b[:], func=AF.Copy, accum_out=acc[:, 0:1]),
                waits=[wx, env.tok("Vv")])
            ws2 = env.scn(o, lambda e, b=bx: e.activation(
                out=dump[:], in_=b[:], func=AF.Square, accum_out=acc[:, 1:2]),
                waits=[ws1])
            mc, vc, vpe, sd, r0, t1, t2, t3, dn, g0 = (sm[:, k:k + 1] for k in range(10))
            env.ven(o, lambda e, a=mc: e.tensor_scalar(a, acc[:, 0:1], 1.0 / C, None, OP.mult),
                    waits=[ws1, ws2, env.tok("Sc")])
            env.ven(o, lambda e, a=vc: e.tensor_scalar(a, acc[:, 1:2], 1.0 / C, None, OP.mult))
            env.ven(o, lambda e, a=t3, b=mc, c=vc: e.scalar_tensor_tensor(
                a, b, b[:, 0:1], c, OP.mult, OP.subtract))
            env.ven(o, lambda e, a=vc, b=t3: e.tensor_scalar(a, b, -1.0, None, OP.mult))
            wv_ = env.ven(o, lambda e, a=vpe, b=vc: e.tensor_scalar(a, b, EPS_LN, None, OP.add))
            wsd = env.scn(o, lambda e, a=sd, b=vpe: e.sqrt(a, b), waits=[wv_])
            wrc = env.ven(o, lambda e, a=r0, b=sd: e.reciprocal(a, b), waits=[wsd])
            env.ven(o, lambda e, a=t1, b=r0: e.tensor_tensor(a, b, b, OP.mult), waits=[wrc])
            env.ven(o, lambda e, a=t2, b=t1, c=vpe: e.tensor_tensor(a, b, c, OP.mult))
            env.ven(o, lambda e, a=t3, b=t2: e.tensor_scalar(a, b, -0.5, 1.5, OP.mult, OP.add))
            rsc = st["rs"][:, tau:tau + 1]
            env.ven(o, lambda e, a=rsc, b=r0, c=t3: e.tensor_tensor(a, b, c, OP.mult))
            env.ven(o, lambda e, a=rsc, tau=tau: e.tensor_scalar(
                st["u"][:, tau:tau + 1], a, 1.0, None, OP.add))
            env.ven(o, lambda e, a=rsc, b=mc, tau=tau: e.scalar_tensor_tensor(
                st["w"][:, tau:tau + 1], b, -1.0, a, OP.mult, OP.mult))
            env.ven(o, lambda e, a=dn, b=vpe: e.reciprocal(a, b))
            env.ven(o, lambda e, a=t1, b=vc, c=dn: e.tensor_tensor(a, b, c, OP.mult))
            wv_ = env.ven(o, lambda e, a=t2, b=t1: e.tensor_scalar(a, b, EPS_RMS, None, OP.add))
            wsd = env.scn(o, lambda e, a=sd, b=t2: e.sqrt(a, b), waits=[wv_])
            wrc = env.ven(o, lambda e, a=g0, b=sd: e.reciprocal(a, b), waits=[wsd])
            env.ven(o, lambda e, a=t1, b=g0: e.tensor_tensor(a, b, b, OP.mult), waits=[wrc])
            env.ven(o, lambda e, a=t1, b=t2: e.tensor_tensor(a, a, b, OP.mult))
            env.ven(o, lambda e, a=t1: e.tensor_scalar(a, a, -0.5, 1.5, OP.mult, OP.add))
            env.ven(o, lambda e, a=g0, b=t1, tau=tau: e.tensor_tensor(
                st["g1"][:, tau:tau + 1], a, b, OP.mult))
            wxn = env.ven(o, lambda e, b=bx, a=rsc, tau=tau: e.tensor_scalar(
                xnb[:], b[:], a, st["w"][:, tau:tau + 1], OP.mult, OP.add))
            xn_done.append(wxn)
            wrt = env.ven(o, lambda e: e.scalar_tensor_tensor(
                dump[:], xnb[:], 1.0, wrbc[:], OP.mult, OP.mult, accum_out=acc[:, 2:3]),
                waits=[wxn])
            env.ven(o, lambda e: e.tensor_scalar(acc[:, 3:4], acc[:, 2:3], brc[:, 0:1], 0.0,
                                                 OP.add, OP.is_gt), waits=[wrt])
            env.ven(o, lambda e, tau=tau: e.tensor_copy(st["rt"][:, tau:tau + 1], acc[:, 3:4]))
            bh, bl = hib[tau % 2], lob[tau % 2]
            wprev = tr_done[tau - 2] if tau >= 2 else []
            whh, wll = env.split(o, bh[:], bl[:], xnb[:], waits=[wxn] + wprev)
            sj = J % 2
            dst_h = stg_h[sj][:].rearrange("p (g t) -> p g t", g=NG)[
                :, :, (tau % 4) * 128:(tau % 4) * 128 + 128]
            dst_l = stg_l[sj][:].rearrange("p (g t) -> p g t", g=NG)[
                :, :, (tau % 4) * 128:(tau % 4) * 128 + 128]
            wtw = stg_free[sj] if (tau % 4 == 0 and stg_free[sj]) else []
            wt1 = env.dmaT(o, dst_h, bh[:], waits=[whh, wll] + wtw)
            wt2 = env.dmaT(o, dst_l, bl[:], waits=[wll] + wtw)
            tr_done.append([wt1, wt2])
            if tau % 4 == 3:
                wsh = env.dma(o, dt["S_xn_hi"][:].rearrange("p (g t) -> p g t", g=NG)
                              [:, :, J * 512:(J + 1) * 512],
                              stg_h[sj][:].rearrange("p (g t) -> p g t", g=NG),
                              waits=env.qtok("X"), eng="gpsimd")
                wst = env.dma(o, dt["S_xn_lo"][:].rearrange("p (g t) -> p g t", g=NG)
                              [:, :, J * 512:(J + 1) * 512],
                              stg_l[sj][:].rearrange("p (g t) -> p g t", g=NG),
                              eng="gpsimd")
                stg_free[sj] = [wsh, wst]
                env.mile["S_xnJ"][J] = [wsh, wst]
        env.fence(o)
        o.emit(blk)
    _free(sp)

    scA = []
    K_hi = sb("K_hi", [128, 4 * T], BF16, scA)
    K_lo = sb("K_lo", [128, 4 * T], BF16, scA)
    V_hi = sb("V_hi", [128, NT * VROW], BF16, scA)
    V_lo = sb("V_lo", [128, NT * VROW], BF16, scA)
    mask4 = sb("mask4", [128, 4 * 512], F32, scA)
    dirs_h = sb("dirs_h", [3, 8], BF16, scA)
    dirs_l = sb("dirs_l", [3, 8], BF16, scA)
    cosT = sb("cosT", [128, T], F32, scA)
    sinT = sb("sinT", [128, T], F32, scA)

    if "A1" in phases:
      scA1 = []
      with nc.Block("a1") as blk:
        o = Ops()
        xnJ_h = sb("a1xnh", [128, NG * 512], BF16, scA1)
        xnJ_l = sb("a1xnl", [128, NG * 512], BF16, scA1)
        wvh = sb("a1wvh", [128, NG * 512], BF16, scA1)
        wvl = sb("a1wvl", [128, NG * 512], BF16, scA1)
        wsg = [sb(f"a1w{i}", [128, 1024], F32, scA1) for i in range(3)]
        wbh = [sb(f"a1bh{i}", [128, 1024], BF16, scA1) for i in range(3)]
        wbl = [sb(f"a1bl{i}", [128, 1024], BF16, scA1) for i in range(3)]
        rop = [sb(f"a1rop{i}", [128, 512], F32, scA1) for i in range(4)]
        qsth = sb("a1qh", [128, 2 * 512], BF16, scA1)
        qstl = sb("a1ql", [128, 2 * 512], BF16, scA1)
        env.dma(o, mask4[:], dt["bandmask"])
        w1 = env.dma(o, rop[0][:], dt["cos_hi"][:, 0:512])
        w1b = env.dma(o, rop[1][:], dt["cos_hi"][:, 512:1024])
        w1c = env.dma(o, rop[2][:], dt["cos_hi"][:, 1024:1536])
        w1d = env.dma(o, rop[3][:], dt["cos_hi"][:, 1536:2048])
        w2 = env.dma(o, cosT[:], dt["cos_lo"])
        wc = env.ven(o, lambda e: e.tensor_tensor(cosT[:, 0:512], rop[0][:],
                                                  cosT[:, 0:512], OP.add), waits=[w1, w2])
        env.ven(o, lambda e: e.tensor_tensor(cosT[:, 512:1024], rop[1][:],
                                             cosT[:, 512:1024], OP.add), waits=[w1b])
        env.ven(o, lambda e: e.tensor_tensor(cosT[:, 1024:1536], rop[2][:],
                                             cosT[:, 1024:1536], OP.add), waits=[w1c])
        wc = env.ven(o, lambda e: e.tensor_tensor(cosT[:, 1536:2048], rop[3][:],
                                                  cosT[:, 1536:2048], OP.add), waits=[w1d])
        w1 = env.dma(o, rop[0][:], dt["sin_hi"][:, 0:512], waits=[wc])
        w1b = env.dma(o, rop[1][:], dt["sin_hi"][:, 512:1024])
        w1c = env.dma(o, rop[2][:], dt["sin_hi"][:, 1024:1536])
        w1d = env.dma(o, rop[3][:], dt["sin_hi"][:, 1536:2048])
        w2 = env.dma(o, sinT[:], dt["sin_lo"])
        env.ven(o, lambda e: e.tensor_tensor(sinT[:, 0:512], rop[0][:],
                                             sinT[:, 0:512], OP.add), waits=[w1, w2])
        env.ven(o, lambda e: e.tensor_tensor(sinT[:, 512:1024], rop[1][:],
                                             sinT[:, 512:1024], OP.add), waits=[w1b])
        env.ven(o, lambda e: e.tensor_tensor(sinT[:, 1024:1536], rop[2][:],
                                             sinT[:, 1024:1536], OP.add), waits=[w1c])
        wsin = env.ven(o, lambda e: e.tensor_tensor(sinT[:, 1536:2048], rop[3][:],
                                                    sinT[:, 1536:2048], OP.add), waits=[w1d])
        wd1 = env.dma(o, wsg[0][0:3, 0:8], dt["dirs64"], waits=[wsin])
        wd2 = env.dma(o, wsg[0][0:3, 8:16], dt["dirs_lo64"])
        wdh = env.scn(o, lambda e: e.copy(dirs_h[:], wsg[0][0:3, 0:8]), waits=[wd1, wd2])
        env.ven(o, lambda e: e.tensor_tensor(rop[1][0:3, 0:8], wsg[0][0:3, 0:8],
                                             dirs_h[:], OP.subtract), waits=[wdh])
        env.ven(o, lambda e: e.tensor_tensor(rop[1][0:3, 0:8], rop[1][0:3, 0:8],
                                             wsg[0][0:3, 8:16], OP.add))
        wdirs = env.ven(o, lambda e: e.tensor_copy(dirs_l[:], rop[1][0:3, 0:8]))
        wv_spl = [wdirs]
        for g in range(NG):
            slot = g % 3
            wl_ = env.dma(o, wsg[slot][:, 0:512], dt["Wv"][g * 128:(g + 1) * 128, :],
                          waits=[wv_spl[-3]] if len(wv_spl) >= 3 else [wdirs])
            _, w2_ = env.split(o, wvh[:, g * 512:(g + 1) * 512],
                               wvl[:, g * 512:(g + 1) * 512], wsg[slot][:, 0:512],
                               waits=[wl_])
            wv_spl.append(w2_)
        for tau in range(NT):
            env.ven(o, lambda e, tau=tau: e.memset(
                V_hi[:, tau * VROW + 512:tau * VROW + 513], 1.0))
            env.ven(o, lambda e, tau=tau: e.memset(
                V_lo[:, tau * VROW + 512:tau * VROW + 513], 0.0))
        wones = env.tok("Vv")
        pfree = wones
        wring = wv_spl[-1]
        qst_free = None
        for J in range(NJ):
            wxn1 = env.dma(o, xnJ_h[:].rearrange("p (g t) -> p g t", g=NG),
                           dt["S_xn_hi"][:].rearrange("p (g t) -> p g t", g=NG)
                           [:, :, J * 512:(J + 1) * 512],
                           waits=(env.mile["S_xnJ"].get(J) or []) + [env.tok("Tn")])
            wxn2 = env.dma(o, xnJ_l[:].rearrange("p (g t) -> p g t", g=NG),
                           dt["S_xn_lo"][:].rearrange("p (g t) -> p g t", g=NG)
                           [:, :, J * 512:(J + 1) * 512])
            for ti in range(4):
                tau = J * 4 + ti
                for g in range(NG):
                    env.mm3(o, P[ti][:],
                            xnJ_h[:, g * 512 + ti * 128: g * 512 + ti * 128 + 128],
                            xnJ_l[:, g * 512 + ti * 128: g * 512 + ti * 128 + 128],
                            wvh[:, g * 512:(g + 1) * 512], wvl[:, g * 512:(g + 1) * 512],
                            start=(g == 0), stop=(g == NG - 1),
                            waits=[wxn1, wxn2, wv_spl[-1], wones, pfree] if g == 0 else (),
                            inc=(g == NG - 1))
            wvm = env.tok("Tn")
            for ti in range(4):
                tau = J * 4 + ti
                env.split(o, V_hi[:, tau * VROW:tau * VROW + 512],
                          V_lo[:, tau * VROW:tau * VROW + 512], P[ti][:], waits=[wvm])
            kmm = []
            for g in range(NG):
                slot = g % 3
                wl_ = env.dma(o, wsg[slot][:, 0:512], dt["Wk"][g * 128:(g + 1) * 128, :],
                              waits=[wring] if g == 0 else
                              ([kmm[g - 3]] if g >= 3 else ()))
                whh, wll = env.split(o, wbh[slot][:, 0:512], wbl[slot][:, 0:512],
                                     wsg[slot][:, 0:512], waits=[wl_])
                for kap in range(4):
                    env.mm3(o, P[4 + kap][:],
                            wbh[slot][:, kap * 128:(kap + 1) * 128],
                            wbl[slot][:, kap * 128:(kap + 1) * 128],
                            xnJ_h[:, g * 512:(g + 1) * 512],
                            xnJ_l[:, g * 512:(g + 1) * 512],
                            start=(g == 0), stop=(g == NG - 1),
                            waits=([wll] + ([pfree] if g == 0 else []))
                            if kap == 0 else (),
                            inc=(kap == 3))
                kmm.append(env.tok("Tn"))
            wkm = env.tok("Tn")
            for g2 in range(2):
                pa, pb_ = P[4 + 2 * g2], P[4 + 2 * g2 + 1]
                cs = cosT[:, J * 512:(J + 1) * 512]
                sn = sinT[:, J * 512:(J + 1) * 512]
                env.ven(o, lambda e, p=pa, c=cs: e.tensor_tensor(rop[0][:], p[:], c, OP.mult),
                        waits=[wkm])
                env.ven(o, lambda e, p=pb_, s=sn: e.tensor_tensor(rop[1][:], p[:], s, OP.mult))
                env.ven(o, lambda e, p=pb_, c=cs: e.tensor_tensor(rop[2][:], p[:], c, OP.mult))
                env.ven(o, lambda e, p=pa, s=sn: e.tensor_tensor(rop[3][:], p[:], s, OP.mult))
                wA = env.ven(o, lambda e: e.tensor_tensor(rop[0][:], rop[0][:], rop[1][:],
                                                          OP.subtract))
                wB = env.ven(o, lambda e: e.tensor_tensor(rop[2][:], rop[2][:], rop[3][:],
                                                          OP.add))
                kA = (2 * g2) * T + J * 512
                kB = (2 * g2 + 1) * T + J * 512
                env.split(o, K_hi[:, kA:kA + 512], K_lo[:, kA:kA + 512], rop[0][:], waits=[wA])
                env.split(o, K_hi[:, kB:kB + 512], K_lo[:, kB:kB + 512], rop[2][:], waits=[wB])
            pfree = env.tok("Vv")
            wring = wkm
            for hq in range(2):
                qmm = []
                for g in range(NG):
                    slot = g % 3
                    wl_ = env.dma(o, wsg[slot][:],
                                  dt["Wq"][g * 128:(g + 1) * 128,
                                           hq * 1024:(hq + 1) * 1024],
                                  waits=[wring] if g == 0 else
                                  ([qmm[g - 3]] if g >= 3 else ()))
                    whh, wll = env.split(o, wbh[slot][:], wbl[slot][:], wsg[slot][:],
                                         waits=[wl_])
                    for hh in range(4):
                        for kk in range(2):
                            env.mm3(o, P[hh * 2 + kk][:],
                                    wbh[slot][:, hh * 256 + kk * 128:
                                             hh * 256 + kk * 128 + 128],
                                    wbl[slot][:, hh * 256 + kk * 128:
                                             hh * 256 + kk * 128 + 128],
                                    xnJ_h[:, g * 512:(g + 1) * 512],
                                    xnJ_l[:, g * 512:(g + 1) * 512],
                                    start=(g == 0), stop=(g == NG - 1),
                                    waits=([wll] + ([pfree] if g == 0 else []))
                                    if (hh == 0 and kk == 0) else (),
                                    inc=(hh == 3 and kk == 1))
                    qmm.append(env.tok("Tn"))
                wqm = env.tok("Tn")
                for hh in range(4):
                    h = hq * 4 + hh
                    pa, pb_ = P[hh * 2], P[hh * 2 + 1]
                    cs = cosT[:, J * 512:(J + 1) * 512]
                    sn = sinT[:, J * 512:(J + 1) * 512]
                    wqf = qst_free if qst_free else []
                    env.ven(o, lambda e, p=pa, c=cs: e.tensor_tensor(rop[0][:], p[:], c,
                                                                     OP.mult),
                            waits=[wqm] + wqf)
                    env.ven(o, lambda e, p=pb_, s=sn: e.tensor_tensor(rop[1][:], p[:], s,
                                                                      OP.mult))
                    env.ven(o, lambda e, p=pb_, c=cs: e.tensor_tensor(rop[2][:], p[:], c,
                                                                      OP.mult))
                    env.ven(o, lambda e, p=pa, s=sn: e.tensor_tensor(rop[3][:], p[:], s,
                                                                     OP.mult))
                    wA = env.ven(o, lambda e: e.tensor_tensor(rop[0][:], rop[0][:], rop[1][:],
                                                              OP.subtract))
                    wB = env.ven(o, lambda e: e.tensor_tensor(rop[2][:], rop[2][:], rop[3][:],
                                                              OP.add))
                    env.split(o, qsth[:, 0:512], qstl[:, 0:512], rop[0][:], waits=[wA])
                    _, wsp = env.split(o, qsth[:, 512:1024], qstl[:, 512:1024], rop[2][:],
                                       waits=[wB])
                    wq_a = env.dma(o, dt["S_q_hi"][:].rearrange("p (g t) -> p g t", g=NG)
                                   [:, 2 * h:2 * h + 2, J * 512:(J + 1) * 512],
                                   qsth[:].rearrange("p (g t) -> p g t", g=2),
                                   waits=[wsp], eng="gpsimd")
                    wq_b = env.dma(o, dt["S_q_lo"][:].rearrange("p (g t) -> p g t", g=NG)
                                   [:, 2 * h:2 * h + 2, J * 512:(J + 1) * 512],
                                   qstl[:].rearrange("p (g t) -> p g t", g=2),
                                   eng="gpsimd")
                    qst_free = [wq_a, wq_b]
                pfree = env.tok("Vv")
                wring = wqm
        env.mile["S_q"] = qst_free
        env.fence(o)
        o.emit(blk)
      _free(scA1)

    if "A2" in phases:
      scA2 = []
      with nc.Block("a2") as blk:
        o = Ops()
        qb_h = [sb("a2qh0", [128, 2 * T], BF16, scA2)]
        qb_l = [sb("a2ql0", [128, 2 * T], BF16, scA2)]
        eb = [sb(f"a2eb{i}", [128, 512], F32, scA2) for i in range(4)]
        ebm = [sb(f"a2ebm{i}", [128, 512], F32, scA2) for i in range(4)]
        e_hi = [sb(f"a2eh{i}", [128, 512], BF16, scA2) for i in range(4)]
        e_lo = [sb(f"a2el{i}", [128, 512], BF16, scA2) for i in range(4)]
        of32 = [sb(f"a2of{i}", [128, 512], F32, scA2) for i in range(2)]
        og32 = [sb(f"a2og{i}", [128, 512], F32, scA2) for i in range(2)]
        ohf = sb("a2ohf", [128, 512], F32, scA2)
        oh_hi = sb("a2ohh", [128, 512], BF16, scA2)
        oh_lo = sb("a2ohl", [128, 512], BF16, scA2)
        kp_row = sb("a2kp", [1, T], F32, scA2)
        qp_row = sb("a2qp", [1, T], F32, scA2)
        rr = sb("a2rr", [1, 512], F32, scA2)
        rrg = sb("a2rrg", [1, 512], F32, scA2)
        pS = [P[0], P[1], P[2], P[3]]; pO = [P[4], P[5]]; pR = P[6]; pB = P[7]
        wstf = [sb(f"a2wsf{i}", [128, C], F32, scA2) for i in range(2)]
        wsth = [sb(f"a2wsh{i}", [128, C], BF16, scA2) for i in range(2)]
        wstl = [sb(f"a2wsl{i}", [128, C], BF16, scA2) for i in range(2)]
        wstage_state = {"idx": 0, "ld": {}, "st": {}}

        def stage_one_w():
            k = wstage_state["idx"]
            if k >= 2 * NF:
                return
            wstage_state["idx"] = k + 1
            phi, which = k // 2, k % 2
            wsrc = dt["w_gate"] if which == 0 else dt["w_up"]
            dsth = dt["S_wg_hi"] if which == 0 else dt["S_wu_hi"]
            dstl = dt["S_wg_lo"] if which == 0 else dt["S_wu_lo"]
            slot = k % 2
            wld = []
            for q4 in range(4):
                ap = wsrc.rearrange("(g p) f -> p g f", p=128)[
                    :, q4 * 4:(q4 + 1) * 4, phi * 128:(phi + 1) * 128]
                wld.append(env.dma(
                    o, wstf[slot][:].rearrange("p (g f) -> p g f", g=NG)
                    [:, q4 * 4:(q4 + 1) * 4, :], ap,
                    waits=(wstage_state["ld"].get(slot, [])
                           if q4 == 0 else []), eng="scalar"))
            for g in range(NG):
                env.ven(o, lambda e, s=slot, g=g: e.tensor_scalar(
                    wstf[s][:, g * 128:(g + 1) * 128], wstf[s][:, g * 128:(g + 1) * 128],
                    n2w_fm[:, g:g + 1], None, OP.mult),
                    waits=(wld + wstage_state["st"].get(slot, [])) if g == 0 else ())
            wts = env.tok("Vv")
            _, wl2 = env.split(o, wsth[slot][:], wstl[slot][:], wstf[slot][:],
                               waits=[wts])
            wstage_state["ld"][slot] = [wl2]
            s1 = env.dma(o, dsth[:, phi * C:(phi + 1) * C], wsth[slot][:], waits=[wl2],
                         eng="gpsimd")
            s2 = env.dma(o, dstl[:, phi * C:(phi + 1) * C], wstl[slot][:], eng="gpsimd")
            wstage_state["st"][slot] = [s1, s2]
        q_free = [None]
        comb_prev = None
        oh_store_prev = None
        prev_exps = []
        gp_free = []
        pv_gate = []
        for h in range(H):
            g2 = h // 4
            qh, ql = qb_h[0], qb_l[0]
            wq1 = env.dma(o, qh[:], dt["S_q_hi"][:, (2 * h) * T:(2 * h + 2) * T],
                          waits=(env.mile.get("S_q") or []) + (
                              [q_free[0]] if q_free[0] else []))
            wq2 = env.dma(o, ql[:], dt["S_q_lo"][:, (2 * h) * T:(2 * h + 2) * T])
            whd = env.tok("Vv")
            kq_banks = [pS[0], pS[1], pS[2], pS[3]]
            qp_banks = [pO[0], pO[1], pR, pB]
            for J in range(NJ):
                b0 = (2 * g2) * T + J * 512
                pk = kq_banks[J]
                o.em("tensor", lambda e, b=b0, h=h, p=pk: e.matmul(
                    p[0:1, :], dirs_h[:, h:h + 1], K_hi[0:3, b:b + 512],
                    start=True, stop=False),
                    waits=[whd] + list(prev_exps))
                o.em("tensor", lambda e, b=b0, h=h, p=pk: e.matmul(
                    p[0:1, :], dirs_h[:, h:h + 1], K_lo[0:3, b:b + 512],
                    start=False, stop=False))
                env.ten(o, lambda e, b=b0, h=h, p=pk: e.matmul(
                    p[0:1, :], dirs_l[:, h:h + 1], K_hi[0:3, b:b + 512],
                    start=False, stop=True))
            wkm_all = env.tok("Tn")
            for J in range(NJ):
                env.ven(o, lambda e, J=J, p=kq_banks[J]: e.tensor_copy(
                    kp_row[0:1, J * 512:(J + 1) * 512], p[0:1, :]),
                    waits=[wkm_all] if J == 0 else ())
            wkp_done = env.tok("Vv")
            for J in range(NJ):
                pq = qp_banks[J]
                o.em("tensor", lambda e, J=J, q=qh, h=h, p=pq: e.matmul(
                    p[0:1, :], dirs_h[:, h:h + 1], q[0:3, J * 512:J * 512 + 512],
                    start=True, stop=False),
                    waits=[wq1, wq2] + pv_gate + gp_free
                    + ([comb_prev] if comb_prev else []))
                o.em("tensor", lambda e, J=J, q=ql, h=h, p=pq: e.matmul(
                    p[0:1, :], dirs_h[:, h:h + 1], q[0:3, J * 512:J * 512 + 512],
                    start=False, stop=False))
                env.ten(o, lambda e, J=J, q=qh, h=h, p=pq: e.matmul(
                    p[0:1, :], dirs_l[:, h:h + 1], q[0:3, J * 512:J * 512 + 512],
                    start=False, stop=True))
            wqm_all = env.tok("Tn")
            for J in range(NJ):
                env.ven(o, lambda e, J=J, p=qp_banks[J]: e.tensor_copy(
                    qp_row[0:1, J * 512:(J + 1) * 512], p[0:1, :]),
                    waits=[wqm_all] if J == 0 else ())
            w_rows = env.tok("Vv")
            for j in range(NJ):
                ni = 4 * j + 4
                evac_toks = []
                for sweep in range(2):
                    exp_tok = []
                    split_tok = []
                    for i in range(ni):
                        ps = pS[i % 4]
                        wbank = [exp_tok[i - 4]] if i >= 4 else (
                            list(prev_exps) + [w_rows])
                        if sweep == 0:
                            for dc in range(2):
                                env.mm3(o, ps[:],
                                        K_hi[:, (2 * g2 + dc) * T + i * 128:
                                             (2 * g2 + dc) * T + i * 128 + 128],
                                        K_lo[:, (2 * g2 + dc) * T + i * 128:
                                             (2 * g2 + dc) * T + i * 128 + 128],
                                        qh[:, dc * T + j * 512:dc * T + j * 512 + 512],
                                        ql[:, dc * T + j * 512:dc * T + j * 512 + 512],
                                        start=(dc == 0), stop=(dc == 1),
                                        waits=([wq1, wq2] + wbank) if dc == 0 else (),
                                        inc=(dc == 1))
                        else:
                            env.ten(o, lambda e, i=i, j=j, p=ps: e.matmul(
                                p[:], kp_row[0:1, i * 128:(i + 1) * 128],
                                qp_row[0:1, j * 512:(j + 1) * 512], start=True, stop=True),
                                waits=[w_rows] + wbank)
                        wmm = env.tok("Tn")
                        scl = 0.0625 if sweep == 0 else 1.0
                        wex = env.scn(o, lambda e, p=ps, b=eb[i % 4], s=scl: e.activation(
                            out=b[:], in_=p[:], func=AF.Exp, scale=s),
                            waits=[wmm] + ([split_tok[i - 4]] if i >= 4 else []))
                        exp_tok.append(wex)
                        ksl = i - 4 * j
                        if 0 <= ksl <= 3:
                            wmk = env.ven(o, lambda e, k=ksl, a=ebm[i % 4], b=eb[i % 4]:
                                          e.tensor_tensor(a[:], b[:],
                                                          mask4[:, k * 512:(k + 1) * 512],
                                                          OP.mult), waits=[wex])
                            esrc = ebm[i % 4]
                        else:
                            wmk = wex
                            esrc = eb[i % 4]
                        slot = i % 4
                        whh, wll = env.split(o, e_hi[slot][:], e_lo[slot][:], esrc[:],
                                             waits=[wmk])
                        split_tok.append(wll)
                        base = i * VROW + g2 * 256
                        for dc in range(2):
                            env.mm3(o, pO[dc][:],
                                    V_hi[:, base + dc * 128: base + dc * 128 + 128],
                                    V_lo[:, base + dc * 128: base + dc * 128 + 128],
                                    e_hi[slot][:], e_lo[slot][:],
                                    start=(i == 0), stop=(i == ni - 1),
                                    waits=([whh, wll, w_rows] + pv_gate) if i == 0
                                    else [whh, wll],
                                    inc=False)
                        o.em("tensor", lambda e, i=i, s=slot: e.matmul(
                            pR[0:1, :], V_hi[:, i * VROW + 512:i * VROW + 513],
                            e_hi[s][:], start=(i == 0), stop=False),
                            waits=([whh, w_rows] + pv_gate) if i == 0 else [whh])
                        env.ten(o, lambda e, i=i, s=slot, last=(i == ni - 1): e.matmul(
                            pR[0:1, :], V_hi[:, i * VROW + 512:i * VROW + 513],
                            e_lo[s][:], start=False, stop=last), waits=[wll])
                    wpv = env.tok("Tn")
                    dsts = of32 if sweep == 0 else og32
                    rdst = rr if sweep == 0 else rrg
                    wcb = [comb_prev] if comb_prev else []
                    ev0 = env.scn(o, lambda e, d=dsts: e.copy(d[0][:], pO[0][:]),
                                  waits=[wpv] + wcb)
                    ev1 = env.scn(o, lambda e, d=dsts: e.copy(d[1][:], pO[1][:]))
                    evr = env.ven(o, lambda e, a=rdst: e.reciprocal(a[:], pR[0:1, :]),
                                  waits=[wpv] + wcb)
                    evac_toks += [ev0, ev1, evr]
                    pv_gate = [ev0, ev1, evr]
                    prev_exps = exp_tok[-4:]
                wba = env.ten(o, lambda e: e.matmul(pB[:], ones_row[:], rr[0:1, :],
                                                    start=True, stop=True),
                              waits=[evac_toks[2]] + gp_free)
                wbb = env.ten(o, lambda e: e.matmul(pR[:], ones_row[:], rrg[0:1, :],
                                                    start=True, stop=True),
                              waits=[evac_toks[5]])
                wt1a = env.ven(o, lambda e, h=h: e.scalar_tensor_tensor(
                    of32[0][:], of32[0][:], hs1m_col[:, h:h + 1], pB[:], OP.mult, OP.mult),
                    waits=[wba, evac_toks[0], evac_toks[1], evac_toks[3], evac_toks[4]])
                wt1b = env.ven(o, lambda e, h=h: e.scalar_tensor_tensor(
                    of32[1][:], of32[1][:], hs1m_col[:, h:h + 1], pB[:], OP.mult, OP.mult))
                for dc in range(2):
                    wt2 = env.ven(o, lambda e, dc=dc, h=h: e.scalar_tensor_tensor(
                        og32[dc][:], og32[dc][:], hs_col[:, h:h + 1], pR[:],
                        OP.mult, OP.mult), waits=[wbb])
                    wt3 = env.ven(o, lambda e, dc=dc: e.tensor_tensor(
                        ohf[:], of32[dc][:], og32[dc][:], OP.add))
                    whh, wll = env.split(o, oh_hi[:], oh_lo[:], ohf[:],
                                         waits=[wt3] + (oh_store_prev or []))
                    go = (2 * h + dc) * T + j * 512
                    wo_a = env.dma(o, dt["S_oh_hi"][:, go:go + 512], oh_hi[:],
                                   waits=[whh, wll], eng="gpsimd")
                    wo_b = env.dma(o, dt["S_oh_lo"][:, go:go + 512], oh_lo[:],
                                   eng="gpsimd")
                    oh_store_prev = [wo_a, wo_b]
                gp_free = [env.tok("Vv")]
                pv_gate = [evac_toks[3], evac_toks[4], env.tok("Vv")]
                comb_prev = env.tok("Vv")
                stage_one_w()
                stage_one_w()
                stage_one_w()
            q_free[0] = env.tok("Tn")
        while wstage_state["idx"] < 2 * NF:
            stage_one_w()
        env.mile["S_oh"] = env.qtok("G")
        env.mile["S_wgu"] = env.qtok("G")
        env.fence(o)
        o.emit(blk)
      _free(scA2)
    _free(scA)

    scCD = []
    h2fm_hi = sb("h2fm_hi", [128, NG * T], BF16, scCD)
    h2fm_lo = sb("h2fm_lo", [128, NG * T], BF16, scCD)
    if "C" in phases:
      scC = []
      with nc.Block("c") as blk:
        o = Ops()
        wbh = [sb(f"cwbh{i}", [128, C], BF16, scC) for i in range(2)]
        wbl = [sb(f"cwbl{i}", [128, C], BF16, scC) for i in range(2)]
        xnP_h = sb("cxnh", [128, NG * 256], BF16, scC)
        xnP_l = sb("cxnl", [128, NG * 256], BF16, scC)
        vfm_hi = sb("cvfh", [128, NG * 256], BF16, scC)
        vfm_lo = sb("cvfl", [128, NG * 256], BF16, scC)
        zaw = [sb(f"czaw{i}", [128, C], F32, scC) for i in range(2)]
        xcb = sb("cxb", [128, C], F32, scC)
        c0_bc = sb("c0_bc", [128, C], BF16, scC)
        acc = sb("cacc", [128, 2], F32, scC)
        prev = {"lo": None, "mm": None, "st": None}
        for g in range(NG):
            slot = g % 2
            wl_ = env.dma(o, zaw[0][:], dt["wo"][g * 128:(g + 1) * 128, :],
                          waits=[prev["lo"], prev["mm"]])
            whh, wll = env.split(o, wbh[slot][:], wbl[slot][:], zaw[0][:],
                                 waits=[wl_] + (prev["st"] or []))
            for dtl in range(4):
                env.ten(o, lambda e, g=g, d=dtl: e.matmul(
                    P[4 + d][0:1, :], bvf[:, g:g + 1], zaw[0][:, d * 512:(d + 1) * 512],
                    start=(g == 0), stop=(g == NG - 1)), waits=[wl_])
            prev["mm"] = env.tok("Tn")
            prev["lo"] = wll
            ws_a = env.dma(o, dt["S_wo_hi"][:, g * C:(g + 1) * C], wbh[slot][:],
                           waits=[whh, wll], eng="gpsimd")
            ws_b = env.dma(o, dt["S_wo_lo"][:, g * C:(g + 1) * C], wbl[slot][:],
                           eng="gpsimd")
            prev["st"] = [ws_a, ws_b]
        wc0 = env.tok("Tn")
        for dtl in range(4):
            env.ven(o, lambda e, d=dtl: e.tensor_copy(
                zaw[0][0:1, d * 512:(d + 1) * 512], P[4 + d][0:1, :]), waits=[wc0])
        wrow = env.tok("Vv")
        for dtl in range(4):
            wm = env.ten(o, lambda e, d=dtl: e.matmul(
                P[4 + d][:], ones_row[:], zaw[0][0:1, d * 512:(d + 1) * 512],
                start=True, stop=True), waits=[wrow])
            env.ven(o, lambda e, d=dtl: e.tensor_copy(c0_bc[:, d * 512:(d + 1) * 512],
                                                      P[4 + d][:]), waits=[wm])
        wc0bc = env.tok("Vv")
        for g in range(NG):
            slot = g % 2
            wl_ = env.dma(o, zaw[0][:], dt["wv"][g * 128:(g + 1) * 128, :],
                          waits=[prev["lo"], wc0bc if g == 0 else None])
            wsc = env.ven(o, lambda e, g=g: e.tensor_scalar(
                zaw[0][:], zaw[0][:], n1w_fm[:, g:g + 1], None, OP.mult), waits=[wl_])
            whh, wll = env.split(o, wbh[slot][:], wbl[slot][:], zaw[0][:],
                                 waits=[wsc] + (prev["st"] or []))
            prev["lo"] = wll
            ws_a = env.dma(o, dt["S_wv_hi"][:, g * C:(g + 1) * C], wbh[slot][:],
                           waits=[whh, wll], eng="gpsimd")
            ws_b = env.dma(o, dt["S_wv_lo"][:, g * C:(g + 1) * C], wbl[slot][:],
                           eng="gpsimd")
            prev["st"] = [ws_a, ws_b]
        env.mile["S_wv"] = list(prev["st"])
        h2_tr = {}
        pfree = wc0bc
        za_store = {}
        vload_tok = {}
        woload_tok = {}
        for tp in range(8):
            t0 = 2 * tp
            wxn1 = env.dma(o, xnP_h[:].rearrange("p (g t) -> p g t", g=NG),
                           dt["S_xn_hi"][:].rearrange("p (g t) -> p g t", g=NG)
                           [:, :, t0 * 128:t0 * 128 + 256],
                           waits=(env.mile["S_xnJ"].get(t0 // 4) or [])
                           + (env.mile.get("S_wv") or []) + [env.tok("Tn")]
                           + (h2_tr.get(t0 - 1) or []))
            wxn2 = env.dma(o, xnP_l[:].rearrange("p (g t) -> p g t", g=NG),
                           dt["S_xn_lo"][:].rearrange("p (g t) -> p g t", g=NG)
                           [:, :, t0 * 128:t0 * 128 + 256])
            vsp = []
            mmg = {}
            for rnd in range(2):
                for g in range(NG):
                    k = rnd * NG + g
                    slot = k % 2
                    ww = [mmg[k - 2]] if k >= 2 else [pfree]
                    co = g * C + rnd * 1024
                    wl1 = env.dma(o, wbh[slot][:, 0:1024], dt["S_wv_hi"][:, co:co + 1024],
                                  waits=ww + (env.mile.get("S_wv") or []))
                    wl2 = env.dma(o, wbl[slot][:, 0:1024], dt["S_wv_lo"][:, co:co + 1024],
                                  waits=ww + (env.mile.get("S_wv") or []), eng="scalar")
                    for e8 in range(8):
                        env.mm3(o, P[e8][:, 0:256],
                                wbh[slot][:, e8 * 128:(e8 + 1) * 128],
                                wbl[slot][:, e8 * 128:(e8 + 1) * 128],
                                xnP_h[:, g * 256:(g + 1) * 256],
                                xnP_l[:, g * 256:(g + 1) * 256],
                                start=(g == 0), stop=(g == NG - 1),
                                waits=([wl1, wl2]
                                       + ([wxn1, wxn2] if g == 0 else [])
                                       + ([vsp[-1]] if (g == 0 and rnd == 1) else []))
                                if e8 == 0 else (),
                                inc=(e8 == 7))
                    mmg[k] = env.tok("Tn")
                wvm = env.tok("Tn")
                for e8 in range(8):
                    eps = rnd * 8 + e8
                    _, wsp = env.split(o, vfm_hi[:, eps * 256:(eps + 1) * 256],
                                       vfm_lo[:, eps * 256:(eps + 1) * 256],
                                       P[e8][:, 0:256],
                                       waits=[wvm] if e8 == 0 else ())
                    vsp.append(wsp)
            vdone = vsp[-1]
            mmg = {}
            for eps in range(NG):
                slot = eps % 2
                ww = [mmg[eps - 2]] if eps >= 2 else [env.tok("Tn")]
                wl1 = env.dma(o, wbh[slot][:], dt["S_wo_hi"][:, eps * C:(eps + 1) * C],
                              waits=ww)
                wl2 = env.dma(o, wbl[slot][:], dt["S_wo_lo"][:, eps * C:(eps + 1) * C],
                              waits=ww, eng="scalar")
                for ti in range(2):
                    for dtl in range(4):
                        env.mm3(o, P[ti * 4 + dtl][:],
                                vfm_hi[:, eps * 256 + ti * 128: eps * 256 + ti * 128 + 128],
                                vfm_lo[:, eps * 256 + ti * 128: eps * 256 + ti * 128 + 128],
                                wbh[slot][:, dtl * 512:(dtl + 1) * 512],
                                wbl[slot][:, dtl * 512:(dtl + 1) * 512],
                                start=(eps == 0), stop=(eps == NG - 1),
                                waits=([wl1, wl2] + ([vdone] if eps == 0 else []))
                                if (ti == 0 and dtl == 0) else (),
                                inc=(ti == 1 and dtl == 3))
                mmg[eps] = env.tok("Tn")
            wop = env.tok("Tn")
            wza_tok = {}
            for ti in range(2):
                tau = t0 + ti
                for dtl in range(4):
                    env.ven(o, lambda e, p=P[ti * 4 + dtl], d=dtl, tau=tau, ti=ti:
                            e.scalar_tensor_tensor(zaw[ti][:, d * 512:(d + 1) * 512], p[:],
                                                   st["g1"][:, tau:tau + 1],
                                                   c0_bc[:, d * 512:(d + 1) * 512],
                                                   OP.mult, OP.add),
                            waits=([wop, wc0bc] + (env.mile.get("S_wv") or [])
                                   + ([za_store.get(tau - 1 - ti)] if tau - 1 - ti >= 0
                                      else []))
                            if dtl == 0 else ())
                wza_tok[ti] = env.tok("Vv")
                za_store[tau] = env.dma(o, dt["S_za"][tau * 128:(tau + 1) * 128, :],
                                        zaw[ti][:], waits=[wza_tok[ti]], eng="gpsimd")
            pfree = wza_tok[1]
            prev_split = None
            for ti in range(2):
                tau = t0 + ti
                wxl = env.dma(o, xcb[:], dt["x"][tau * 128:(tau + 1) * 128, :],
                              waits=[wza_tok[1]] if ti == 0 else [prev_split])
                wxa = env.ven(o, lambda e, tau=tau: e.tensor_scalar(
                    xcb[:], xcb[:], st["rs"][:, tau:tau + 1], st["w"][:, tau:tau + 1],
                    OP.mult, OP.add), waits=[wxl])
                wh_ = env.ven(o, lambda e, ti=ti: e.tensor_tensor(xcb[:], zaw[ti][:], xcb[:],
                                                                  OP.add), waits=[wxa])
                wsq = env.scn(o, lambda e, ti=ti: e.activation(
                    out=zaw[ti][:], in_=xcb[:], func=AF.Square, accum_out=acc[:, 0:1]),
                    waits=[wh_, za_store[tau]])
                vpe2, sd2, r02, t12, t22 = (sm[:, k:k + 1] for k in range(10, 15))
                env.ven(o, lambda e, a=vpe2: e.tensor_scalar(
                    a, acc[:, 0:1], 1.0 / C, EPS_RMS, OP.mult, OP.add), waits=[wsq])
                wv_ = env.tok("Vv")
                wsd = env.scn(o, lambda e, a=sd2, b=vpe2: e.sqrt(a, b), waits=[wv_])
                wrc = env.ven(o, lambda e, a=r02, b=sd2: e.reciprocal(a, b), waits=[wsd])
                env.ven(o, lambda e, a=t12, b=r02: e.tensor_tensor(a, b, b, OP.mult),
                        waits=[wrc])
                env.ven(o, lambda e, a=t22, b=t12, c=vpe2: e.tensor_tensor(a, b, c, OP.mult))
                env.ven(o, lambda e, a=t22: e.tensor_scalar(a, a, -0.5, 1.5, OP.mult, OP.add))
                env.ven(o, lambda e, a=r02, b=t22, tau=tau: e.tensor_tensor(
                    st["rs2"][:, tau:tau + 1], a, b, OP.mult))
                wh2 = env.ven(o, lambda e, tau=tau: e.tensor_scalar(
                    xcb[:], xcb[:], st["rs2"][:, tau:tau + 1], None, OP.mult))
                wtp = h2_tr[tau - 1] if tau >= 1 else []
                whh, wll = env.split(o, xnP_h[:, 0:C], xnP_l[:, 0:C], xcb[:],
                                     waits=[wh2] + wtp)
                prev_split = wll
                dsth = h2fm_hi[:].rearrange("p (g t) -> p g t", g=NG)[
                    :, :, tau * 128:(tau + 1) * 128]
                dstl = h2fm_lo[:].rearrange("p (g t) -> p g t", g=NG)[
                    :, :, tau * 128:(tau + 1) * 128]
                wht_a = env.dmaT(o, dsth, xnP_h[:, 0:C], waits=[whh, wll])
                wht_b = env.dmaT(o, dstl, xnP_l[:, 0:C], waits=[wll])
                h2_tr[tau] = [wht_a, wht_b]
        env.mile["S_za"] = env.qtok("G")
        env.mile["h2fm"] = list(h2_tr[NT - 1])
        env.fence(o)
        o.emit(blk)
      _free(scC)

    if "D" in phases:
      scD = []
      with nc.Block("d") as blk:
        o = Ops()
        wgh = [sb(f"dwgh{i}", [128, NG * 128], BF16, scD) for i in range(2)]
        wgl = [sb(f"dwgl{i}", [128, NG * 128], BF16, scD) for i in range(2)]
        wuh = [sb(f"dwuh{i}", [128, NG * 128], BF16, scD) for i in range(2)]
        wul = [sb(f"dwul{i}", [128, NG * 128], BF16, scD) for i in range(2)]
        sg = sb("dsg", [128, 512], F32, scD)
        pr = sb("dpr", [128, 512], F32, scD)
        prh = [sb(f"dprh{i}", [128, 512], BF16, scD) for i in range(2)]
        prl = [sb(f"dprl{i}", [128, 512], BF16, scD) for i in range(2)]
        mm_tok = []
        ev_tok = None
        st_tok = {}
        ld_tok = {}
        for phi in range(NF):
            slot = phi % 2
            wprev = ([mm_tok[phi - 2]] if phi >= 2 else []) + (
                env.mile.get("h2fm") or []) + (env.mile.get("S_wgu") or [])
            wl1 = env.dma(o, wgh[slot][:], dt["S_wg_hi"][:, phi * C:(phi + 1) * C],
                          waits=wprev)
            wl2 = env.dma(o, wgl[slot][:], dt["S_wg_lo"][:, phi * C:(phi + 1) * C],
                          waits=wprev)
            wl3 = env.dma(o, wuh[slot][:], dt["S_wu_hi"][:, phi * C:(phi + 1) * C],
                          eng="scalar", waits=wprev)
            wl4 = env.dma(o, wul[slot][:], dt["S_wu_lo"][:, phi * C:(phi + 1) * C],
                          eng="scalar", waits=wprev)
            wgs, wus = wl1, wl4
            wld = [wl1, wl2, wl3, wl4]
            w0 = wld + ([ev_tok] if ev_tok else [])
            for g in range(NG):
                for J in range(NJ):
                    env.mm3(o, P[J][:],
                            wgh[slot][:, g * 128:(g + 1) * 128],
                            wgl[slot][:, g * 128:(g + 1) * 128],
                            h2fm_hi[:, g * T + J * 512: g * T + J * 512 + 512],
                            h2fm_lo[:, g * T + J * 512: g * T + J * 512 + 512],
                            start=(g == 0), stop=(g == NG - 1),
                            waits=w0 if (g == 0 and J == 0) else (), inc=False)
                for J in range(NJ):
                    env.mm3(o, P[4 + J][:],
                            wuh[slot][:, g * 128:(g + 1) * 128],
                            wul[slot][:, g * 128:(g + 1) * 128],
                            h2fm_hi[:, g * T + J * 512: g * T + J * 512 + 512],
                            h2fm_lo[:, g * T + J * 512: g * T + J * 512 + 512],
                            start=(g == 0), stop=(g == NG - 1),
                            inc=(g == NG - 1 and J == NJ - 1))
            wmm = env.tok("Tn")
            mm_tok.append(wmm)
            for J in range(NJ):
                wsl = env.scn(o, lambda e, J=J: e.activation(out=sg[:], in_=P[J][:],
                                                             func=AF.Silu), waits=[wmm])
                wpr = env.ven(o, lambda e, J=J: e.tensor_tensor(pr[:], sg[:], P[4 + J][:],
                                                                OP.mult), waits=[wsl])
                sl2 = J % 2
                wst = (st_tok[(phi, J - 2)] if (phi, J - 2) in st_tok
                       else (st_tok[(phi - 1, J + 2)] if (phi - 1, J + 2) in st_tok
                             else []))
                whh, wll = env.split(o, prh[sl2][:], prl[sl2][:], pr[:], waits=[wpr] + wst)
                go = phi * T + J * 512
                wp_a = env.dma(o, dt["S_pr_hi"][:, go:go + 512], prh[sl2][:],
                               waits=[whh, wll], eng="gpsimd")
                wp_b = env.dma(o, dt["S_pr_lo"][:, go:go + 512], prl[sl2][:],
                               eng="gpsimd")
                st_tok[(phi, J)] = [wp_a, wp_b]
            ev_tok = env.tok("Vv")
        env.mile["S_pr"] = env.qtok("G")
        env.fence(o)
        o.emit(blk)
      _free(scD)
    _free(scCD)

    if "E1" in phases:
      scE = []
      woh = sb("e1woh", [128, 8 * C], BF16, scE)
      wol = sb("e1wol", [128, 8 * C], BF16, scE)
      wstg = sb("e1w", [128, C], F32, scE)
      ohJ_h = sb("e1ohh", [128, 8 * 512], BF16, scE)
      ohJ_l = sb("e1ohl", [128, 8 * 512], BF16, scE)
      aow = sb("e1aow", [128, C], F32, scE)
      aop = sb("e1aop", [128, C], F32, scE)
      with nc.Block("e1") as blk:
        o = Ops()
        ev_prev = None
        for half in range(2):
            spl_tok = []
            for gi in range(8):
                g = half * 8 + gi
                wws = [spl_tok[gi - 1]] if gi >= 1 else (
                    [env.tok("Tn")] if half == 1 else [])
                wl_ = env.dma(o, wstg[:], dt["Wo"][g * 128:(g + 1) * 128, :], waits=wws)
                _, wll = env.split(o, woh[:, gi * C:(gi + 1) * C],
                                   wol[:, gi * C:(gi + 1) * C], wstg[:], waits=[wl_])
                spl_tok.append(wll)
            wsplit = spl_tok[-1]
            for J in range(NJ):
                w1 = env.dma(o, ohJ_h[:].rearrange("p (g t) -> p g t", g=8),
                             dt["S_oh_hi"][:].rearrange("p (g t) -> p g t", g=NG)
                             [:, half * 8:half * 8 + 8, J * 512:(J + 1) * 512],
                             waits=env.mile.get("S_oh", []) + [env.tok("Tn")])
                w2 = env.dma(o, ohJ_l[:].rearrange("p (g t) -> p g t", g=8),
                             dt["S_oh_lo"][:].rearrange("p (g t) -> p g t", g=NG)
                             [:, half * 8:half * 8 + 8, J * 512:(J + 1) * 512])
                for ti in range(4):
                    tau = J * 4 + ti
                    for gi in range(8):
                        for et in range(4):
                            env.mm3(o, P[et][:],
                                    ohJ_h[:, gi * 512 + ti * 128: gi * 512 + ti * 128 + 128],
                                    ohJ_l[:, gi * 512 + ti * 128: gi * 512 + ti * 128 + 128],
                                    woh[:, gi * C + et * 512: gi * C + et * 512 + 512],
                                    wol[:, gi * C + et * 512: gi * C + et * 512 + 512],
                                    start=(gi == 0), stop=(gi == 7),
                                    waits=([w1, w2, wsplit] + ([ev_prev] if ev_prev else []))
                                    if (gi == 0 and et == 0) else (),
                                    inc=(gi == 7 and et == 3))
                    wmm = env.tok("Tn")
                    if half == 0:
                        for et in range(4):
                            env.ven(o, lambda e, et=et: e.tensor_copy(
                                aow[:, et * 512:(et + 1) * 512], P[et][:]),
                                waits=([wmm] + env.qtok("G")) if et == 0 else ())
                        wev = env.tok("Vv")
                        env.dma(o, dt["S_ao"][tau * 128:(tau + 1) * 128, :], aow[:],
                                waits=[wev], eng="gpsimd")
                    else:
                        wld = env.dma(o, aop[:], dt["S_ao"][tau * 128:(tau + 1) * 128, :],
                                      waits=env.mile.get("e1_half0", []) + [ev_prev])
                        for et in range(4):
                            env.ven(o, lambda e, et=et: e.tensor_tensor(
                                aow[:, et * 512:(et + 1) * 512], P[et][:],
                                aop[:, et * 512:(et + 1) * 512], OP.add),
                                waits=([wmm, wld] + env.qtok("G")) if et == 0 else ())
                        wev = env.tok("Vv")
                        env.dma(o, dt["S_ao"][tau * 128:(tau + 1) * 128, :], aow[:],
                                waits=[wev], eng="gpsimd")
                    ev_prev = env.tok("Vv")
            if half == 0:
                env.mile["e1_half0"] = env.qtok("G")
        env.mile["S_ao"] = env.qtok("G")
        env.fence(o)
        o.emit(blk)
      _free(scE)

    if "E2" in phases:
      scE = []
      NH3 = (15, 15, 14)
      wdh = sb("e2wdh", [128, 15 * C], BF16, scE)
      wdl = sb("e2wdl", [128, 15 * C], BF16, scE)
      wstg = sb("e2w", [128, C], F32, scE)
      prJ_h = sb("e2prh", [128, 15 * 512], BF16, scE)
      prJ_l = sb("e2prl", [128, 15 * 512], BF16, scE)
      zw = sb("e2zw", [128, C], F32, scE)
      zp = sb("e2zp", [128, C], F32, scE)
      aot = sb("e2ao", [128, C], F32, scE)
      xt_ = sb("e2x", [128, C], F32, scE)
      ow = sb("e2ow", [128, C], F32, scE)
      with nc.Block("e2") as blk:
        o = Ops()
        ev_prev = None
        fbase = 0
        for third in range(3):
            nf3 = NH3[third]
            spl_tok = []
            for fi in range(nf3):
                f = fbase + fi
                wws = [spl_tok[fi - 1]] if fi >= 1 else (
                    [env.tok("Tn")] if third > 0 else [])
                wl_ = env.dma(o, wstg[:], dt["w_down"][f * 128:(f + 1) * 128, :], waits=wws)
                _, wll = env.split(o, wdh[:, fi * C:(fi + 1) * C],
                                   wdl[:, fi * C:(fi + 1) * C], wstg[:], waits=[wl_])
                spl_tok.append(wll)
            wsplit = spl_tok[-1]
            for J in range(NJ):
                w1 = env.dma(o, prJ_h[:, 0:nf3 * 512].rearrange("p (g t) -> p g t", g=nf3),
                             dt["S_pr_hi"][:].rearrange("p (g t) -> p g t", g=NF)
                             [:, fbase:fbase + nf3, J * 512:(J + 1) * 512],
                             waits=env.mile.get("S_pr", []) + [env.tok("Tn")])
                w2 = env.dma(o, prJ_l[:, 0:nf3 * 512].rearrange("p (g t) -> p g t", g=nf3),
                             dt["S_pr_lo"][:].rearrange("p (g t) -> p g t", g=NF)
                             [:, fbase:fbase + nf3, J * 512:(J + 1) * 512])
                for ti in range(4):
                    tau = J * 4 + ti
                    for fi in range(nf3):
                        for et in range(4):
                            env.mm3(o, P[et][:],
                                    prJ_h[:, fi * 512 + ti * 128: fi * 512 + ti * 128 + 128],
                                    prJ_l[:, fi * 512 + ti * 128: fi * 512 + ti * 128 + 128],
                                    wdh[:, fi * C + et * 512: fi * C + et * 512 + 512],
                                    wdl[:, fi * C + et * 512: fi * C + et * 512 + 512],
                                    start=(fi == 0), stop=(fi == nf3 - 1),
                                    waits=([w1, w2, wsplit] + ([ev_prev] if ev_prev else []))
                                    if (fi == 0 and et == 0) else (),
                                    inc=(fi == nf3 - 1 and et == 3))
                    wmm = env.tok("Tn")
                    if third < 2:
                        wza = env.dma(o, zp[:], dt["S_za"][tau * 128:(tau + 1) * 128, :],
                                      waits=(env.mile.get("S_za", []) if third == 0
                                             else env.mile.get("e2_prev", [])) + [ev_prev])
                        for et in range(4):
                            env.ven(o, lambda e, et=et: e.tensor_tensor(
                                zw[:, et * 512:(et + 1) * 512], P[et][:],
                                zp[:, et * 512:(et + 1) * 512], OP.add),
                                waits=([wmm, wza] + env.qtok("G")) if et == 0 else ())
                        wev = env.tok("Vv")
                        env.dma(o, dt["S_za"][tau * 128:(tau + 1) * 128, :], zw[:],
                                waits=[wev], eng="gpsimd")
                    else:
                        wza = env.dma(o, zp[:], dt["S_za"][tau * 128:(tau + 1) * 128, :],
                                      waits=env.mile.get("e2_prev", []) + [ev_prev])
                        wao = env.dma(o, aot[:], dt["S_ao"][tau * 128:(tau + 1) * 128, :],
                                      waits=env.mile.get("S_ao", []))
                        wxl = env.dma(o, xt_[:], dt["x"][tau * 128:(tau + 1) * 128, :])
                        for et in range(4):
                            env.ven(o, lambda e, et=et: e.tensor_tensor(
                                zw[:, et * 512:(et + 1) * 512], P[et][:],
                                zp[:, et * 512:(et + 1) * 512], OP.add),
                                waits=([wmm, wza] + env.qtok("G")) if et == 0 else ())
                        env.ven(o, lambda e, tau=tau: e.tensor_scalar(
                            zw[:], zw[:], st["rt"][:, tau:tau + 1], None, OP.mult))
                        env.ven(o, lambda e: e.tensor_tensor(zw[:], zw[:], aot[:], OP.add),
                                waits=[wao])
                        env.ven(o, lambda e, tau=tau: e.tensor_scalar(
                            xt_[:], xt_[:], st["u"][:, tau:tau + 1], st["w"][:, tau:tau + 1],
                            OP.mult, OP.add), waits=[wxl])
                        wfin = env.ven(o, lambda e: e.tensor_tensor(ow[:], zw[:], xt_[:],
                                                                    OP.add))
                        env.dma(o, out[tau * 128:(tau + 1) * 128, :], ow[:],
                                waits=[wfin], eng="gpsimd")
                    ev_prev = env.tok("Vv")
            env.mile["e2_prev"] = env.qtok("G")
            fbase += nf3
        o.em("sync", lambda e: e.nop(), waits=env.qtok("G") + env.qtok("L") + env.qtok("A") + env.qtok("X"))
        env.fence(o)
        o.emit(blk)
      _free(scE)


def kernel(**inputs):
    consts = make_consts()
    nc = build_nc()
    def arr(k):
        return np.ascontiguousarray(np.asarray(inputs[k], dtype=np.float32))
    x = arr("L0_attn_x")
    base = {
        "Wq": arr("L0_attn_Wq"), "Wk": arr("L0_attn_Wk"), "Wv": arr("L0_attn_Wv"),
        "Wo": arr("L0_attn_Wo"),
        "head_scales": arr("L0_attn_head_scales").reshape(1, H),
        "w_router": arr("L0_ffn_w_router"),
        "b_router": arr("L0_ffn_b_router").reshape(1, 1),
        "norm1_w": arr("L0_ffn_norm1_w"), "wv": arr("L0_ffn_wv"), "bv": arr("L0_ffn_bv"),
        "wo": arr("L0_ffn_wo"), "norm2_w": arr("L0_ffn_norm2_w"),
        "w_gate": arr("L0_ffn_w_gate"), "w_up": arr("L0_ffn_w_up"),
        "w_down": arr("L0_ffn_w_down"),
    }
    base.update(consts)
    in_maps = [dict(base, x=np.ascontiguousarray(x[i])) for i in range(8)]
    res = run_bass_kernel_spmd(nc, in_maps, core_ids=list(range(8)))
    return np.stack([r["out"] for r in res.results], axis=0)


# version 07
# speedup vs baseline: 1.4000x; 1.1698x over previous
import math
import numpy as np
import concourse.bass as bass
import concourse.mybir as mybir
from concourse import library_config
from concourse.bass_utils import run_bass_kernel_spmd

F32 = mybir.dt.float32
BF16 = mybir.dt.bfloat16
AF = mybir.ActivationFunctionType
OP = mybir.AluOpType

T = 2048
C = 2048
FF = 5632
H = 8
NT = 16
NJ = 4
NG = 16
NF = 44
VROW = 528
EPS_LN = 1e-5
EPS_RMS = 1e-6

ALL_PHASES = ("P1", "A1", "A2", "C", "D", "E1", "E2")


def make_consts():
    import jax
    import jax.numpy as jnp
    with jax.default_device(jax.devices("cpu")[0]):
        inv = 1.0 / 10000.0 ** (jnp.arange(0, 256, 2, dtype=jnp.float32) / 256)
        f = jnp.arange(T, dtype=jnp.float32)[:, None] * inv[None, :]
        cos_hi = np.ascontiguousarray(np.asarray(jnp.cos(f)).T)
        sin_hi = np.ascontiguousarray(np.asarray(jnp.sin(f)).T)
    cos_lo = np.zeros_like(cos_hi)
    sin_lo = np.zeros_like(sin_hi)
    m = np.zeros((128, 4 * 512), dtype=np.float32)
    r = np.arange(128)[:, None]
    c = np.arange(512)[None, :]
    for k in range(4):
        m[:, k * 512:(k + 1) * 512] = (c >= k * 128 + r).astype(np.float32)
    trig = np.array([[sx, sy, sz] for sx in (1.0, -1.0) for sy in (1.0, -1.0)
                     for sz in (1.0, -1.0)], dtype=np.float64) / math.sqrt(3.0)
    dirs64 = np.ascontiguousarray(trig.T)
    return {"cos_hi": cos_hi, "cos_lo": cos_lo, "sin_hi": sin_hi, "sin_lo": sin_lo,
            "bandmask": m, "dirs64": dirs64.astype(np.float32),
            "dirs_lo64": (dirs64 - dirs64.astype(np.float32)).astype(np.float32)}


class Ops:
    def __init__(self):
        self.ops = []

    def em(self, eng, fn, inc=None, waits=()):
        self.ops.append((eng, fn, inc, tuple(w for w in waits if w is not None)))

    def emit(self, block):
        by_eng = {}
        for op in self.ops:
            by_eng.setdefault(op[0], []).append(op)
        for eng, items in by_eng.items():
            def body(e, items=items):
                seen = {}
                for _, fn, inc, waits in items:
                    for sem, val in waits:
                        if val > 0 and seen.get(id(sem), -1) < val:
                            e.wait_ge(sem, val)
                            seen[id(sem)] = val
                    inst = fn(e)
                    if inc is not None:
                        inst.then_inc(inc[0], inc[1])
            getattr(block, eng)(body)


class Env:
    DMA_POOL = {"L": 6, "A": 4, "G": 4, "X": 2}

    def __init__(self, nc):
        self.nc = nc
        self._ctx = []
        self.S = {}
        self.V = {}
        for nm in ("Tn", "Vv", "Sc"):
            cm = nc.semaphore(nm)
            self.S[nm] = cm.__enter__()
            self._ctx.append(cm)
            self.V[nm] = 0
        self.dpool = {}
        self.dcnt = {}
        self.didx = {}
        for nm, r in self.DMA_POOL.items():
            self.dpool[nm] = []
            self.dcnt[nm] = []
            self.didx[nm] = 0
            for i in range(r):
                cm = nc.semaphore(f"{nm}{i}")
                self.dpool[nm].append(cm.__enter__())
                self._ctx.append(cm)
                self.dcnt[nm].append(0)
        self.mile = {}

    def sb(self, name, shape, dtype, ctx=None):
        cm = self.nc.sbuf_tensor(name, shape, dtype)
        h = cm.__enter__()
        (ctx if ctx is not None else self._ctx).append(cm)
        return h

    def psum(self, name):
        cm = self.nc.psum_tensor(name, [128, 512], F32)
        h = cm.__enter__()
        self._ctx.append(cm)
        return h

    def tok(self, nm):
        return (self.S[nm], self.V[nm])

    def _dma_slot(self, nm):
        i = self.didx[nm] % len(self.dpool[nm])
        self.didx[nm] += 1
        sem = self.dpool[nm][i]
        prev = self.dcnt[nm][i]
        self.dcnt[nm][i] = prev + 16
        return sem, prev

    def dma(self, o, outap, inap, waits=(), eng="sync", slow=False):
        nm = {"gpsimd": "G", "scalar": "A"}.get(eng, "L")
        sem, prev = self._dma_slot(nm)
        o.em(eng, lambda e, a=outap, b=inap, s=slow: e.dma_start(
            out=a, in_=b, allow_slow_non_contiguous=s),
             inc=(sem, 16), waits=list(waits) + [(sem, prev)])
        return (sem, prev + 16)

    def dmaT(self, o, outap, inap, waits=()):
        sem, prev = self._dma_slot("X")
        o.em("sync", lambda e, a=outap, b=inap: e.dma_start_transpose(a, b),
             inc=(sem, 16), waits=list(waits) + [(sem, prev)])
        return (sem, prev + 16)

    def ven(self, o, fn, waits=()):
        prev = self.V["Vv"]
        self.V["Vv"] = prev + 1
        o.em("vector", fn, inc=(self.S["Vv"], 1),
             waits=list(waits) + [(self.S["Vv"], prev)])
        return (self.S["Vv"], prev + 1)

    def scn(self, o, fn, waits=()):
        prev = self.V["Sc"]
        self.V["Sc"] = prev + 1
        o.em("scalar", fn, inc=(self.S["Sc"], 1),
             waits=list(waits) + [(self.S["Sc"], prev)])
        return (self.S["Sc"], prev + 1)

    def gpn(self, o, fn, waits=()):
        if "Gp" not in self.S:
            cm = self.nc.semaphore("Gp")
            self.S["Gp"] = cm.__enter__()
            self._ctx.append(cm)
            self.V["Gp"] = 0
        prev = self.V["Gp"]
        self.V["Gp"] = prev + 1
        o.em("gpsimd", fn, inc=(self.S["Gp"], 1),
             waits=list(waits) + [(self.S["Gp"], prev)])
        return (self.S["Gp"], prev + 1)

    def ten(self, o, fn, waits=()):
        self.V["Tn"] += 1
        o.em("tensor", fn, inc=(self.S["Tn"], 1), waits=waits)
        return (self.S["Tn"], self.V["Tn"])

    def mm3(self, o, pb, lh, ll, rh, rl, start, stop, waits=(), inc=False):
        o.em("tensor", lambda e, a=pb, b=lh, c=rh, s=start:
             e.matmul(a, b, c, start=s, stop=False), waits=waits)
        o.em("tensor", lambda e, a=pb, b=lh, c=rl:
             e.matmul(a, b, c, start=False, stop=False))
        token = None
        if inc:
            self.V["Tn"] += 1
            token = (self.S["Tn"], self.V["Tn"])
        o.em("tensor", lambda e, a=pb, b=ll, c=rh, s=stop:
             e.matmul(a, b, c, start=False, stop=s),
             inc=(self.S["Tn"], 1) if inc else None)
        return token

    def qtok(self, nm):
        return [(s, c) for s, c in zip(self.dpool[nm], self.dcnt[nm]) if c > 0]

    def fence(self, o):
        o.em("sync", lambda e: e.nop(),
             waits=self.qtok("G") + self.qtok("L") + self.qtok("A") + self.qtok("X"))

    def split(self, o, hi, lo, src, waits=(), hi_on_sc=True):
        if hi_on_sc:
            w1 = self.scn(o, lambda e, a=hi, b=src: e.copy(a, b), waits=waits)
        else:
            w1 = self.ven(o, lambda e, a=hi, b=src: e.tensor_copy(a, b), waits=waits)
        w2 = self.ven(o, lambda e, a=lo, b=src, c=hi:
                      e.tensor_tensor(a, b, c, OP.subtract), waits=[w1])
        return w1, w2


def build_nc(phases=ALL_PHASES):
    nc = bass.Bass("TRN2", target_bir_lowering=False)
    dt = {}

    def din(name, shape):
        dt[name] = nc.dram_tensor(name, shape, F32, kind="ExternalInput").ap()
    din("x", [T, C])
    din("Wq", [C, C]); din("Wk", [C, 512]); din("Wv", [C, 512]); din("Wo", [C, C])
    din("head_scales", [1, H])
    din("w_router", [C]); din("b_router", [1, 1]); din("norm1_w", [C])
    din("wv", [C, C]); din("bv", [C]); din("wo", [C, C]); din("norm2_w", [C])
    din("w_gate", [C, FF]); din("w_up", [C, FF]); din("w_down", [FF, C])
    din("cos_hi", [128, T]); din("cos_lo", [128, T])
    din("sin_hi", [128, T]); din("sin_lo", [128, T])
    din("bandmask", [128, 4 * 512]); din("dirs64", [3, 8]); din("dirs_lo64", [3, 8])
    out = nc.dram_tensor("out", [T, C], F32, kind="ExternalOutput").ap()

    def scr(name, shape, dtype):
        dt[name] = nc.dram_tensor(name, shape, dtype, kind="Internal").ap()
    for nm in ("S_xn_hi", "S_xn_lo", "S_q_hi", "S_q_lo", "S_oh_hi", "S_oh_lo",
               "S_xn2_hi", "S_xn2_lo"):
        scr(nm, [128, NG * T], BF16)
    scr("S_wv_hi", [128, NG * C], BF16); scr("S_wv_lo", [128, NG * C], BF16)
    scr("S_wo_hi", [128, NG * C], BF16); scr("S_wo_lo", [128, NG * C], BF16)
    scr("S_pr_hi", [128, NF * T], BF16); scr("S_pr_lo", [128, NF * T], BF16)
    scr("S_wg_hi", [128, NF * C], BF16); scr("S_wg_lo", [128, NF * C], BF16)
    scr("S_wu_hi", [128, NF * C], BF16); scr("S_wu_lo", [128, NF * C], BF16)
    scr("S_za", [T, C], F32)
    scr("S_ao", [T, C], F32)

    env = Env(nc)
    try:
        _program(nc, env, dt, out, phases)
    finally:
        for cm in reversed(env._ctx):
            try:
                cm.__exit__(None, None, None)
            except Exception:
                pass
    return nc


def _free(scope):
    for cm in reversed(scope):
        cm.__exit__(None, None, None)
    scope.clear()


def _program(nc, env, dt, out, phases):
    S = env.S
    sb = env.sb

    st = {nm: sb("st_" + nm, [128, NT], F32) for nm in ("rs", "u", "w", "g1", "rt", "rs2")}
    ones_row = sb("ones_row", [1, 128], F32)
    hs_row = sb("hs_row", [1, H], F32)
    hs1m_row = sb("hs1m_row", [1, H], F32)
    hs_col = sb("hs_col", [128, H], F32)
    hs1m_col = sb("hs1m_col", [128, H], F32)
    brc = sb("brc", [128, 1], F32)
    brs = sb("brs", [1, 1], F32)
    n1w_fm = sb("n1w_fm", [128, NG], F32)
    n2w_fm = sb("n2w_fm", [128, NG], F32)
    bvf = sb("bvf", [128, NG], F32)
    sm = sb("smallA", [128, 16], F32)
    P = [env.psum(f"pb{i}") for i in range(8)]

    sp = []
    wrbc = sb("wrbc", [128, C], F32, sp)
    with nc.Block("p0") as blk:
        o = Ops()
        tmp = sb("p0tmp", [128, C], F32, sp)
        w1 = env.dma(o, hs_row[:], dt["head_scales"])
        wh1 = env.ven(o, lambda e: e.tensor_scalar(hs1m_row[:], hs_row[:], -1.0, 1.0,
                                                   OP.mult, OP.add), waits=[w1])
        won = env.ven(o, lambda e: e.memset(ones_row[:], 1.0))
        for h in range(H):
            wm = env.ten(o, lambda e, h=h: e.matmul(
                P[2][:, 0:1], ones_row[:], hs_row[0:1, h:h + 1],
                start=True, stop=True), waits=[wh1, won, env.tok("Vv")])
            env.ven(o, lambda e, h=h: e.tensor_copy(hs_col[:, h:h + 1], P[2][:, 0:1]),
                    waits=[wm])
            wm = env.ten(o, lambda e, h=h: e.matmul(
                P[3][:, 0:1], ones_row[:], hs1m_row[0:1, h:h + 1],
                start=True, stop=True), waits=[env.tok("Vv")])
            env.ven(o, lambda e, h=h: e.tensor_copy(hs1m_col[:, h:h + 1], P[3][:, 0:1]),
                    waits=[wm])
        wr = env.dma(o, tmp[0:1, 0:C], dt["w_router"].rearrange("(r c) -> r c", r=1))
        wb = env.dma(o, brs[:], dt["b_router"])
        for j in range(4):
            wm = env.ten(o, lambda e, j=j: e.matmul(
                P[0][:], ones_row[:], tmp[0:1, j * 512:(j + 1) * 512],
                start=True, stop=True), waits=[wr, env.tok("Vv")])
            env.ven(o, lambda e, j=j: e.tensor_copy(wrbc[:, j * 512:(j + 1) * 512], P[0][:]),
                    waits=[wm])
        wm = env.ten(o, lambda e: e.matmul(P[1][:, 0:1], ones_row[:], brs[:],
                                           start=True, stop=True),
                     waits=[wb, env.tok("Vv")])
        env.ven(o, lambda e: e.tensor_copy(brc[:], P[1][:, 0:1]), waits=[wm])
        env.dma(o, n1w_fm[:], dt["norm1_w"].rearrange("(g p) -> p g", p=128), slow=True)
        env.dma(o, n2w_fm[:], dt["norm2_w"].rearrange("(g p) -> p g", p=128), slow=True)
        env.dma(o, bvf[:], dt["bv"].rearrange("(g p) -> p g", p=128), slow=True)
        env.fence(o)
        o.emit(blk)

    env.mile["S_xnJ"] = {}
    if "P1" in phases:
      with nc.Block("p1") as blk:
        o = Ops()
        xb = [sb(f"p1x{i}", [128, C], F32, sp) for i in range(2)]
        xnb = sb("p1xn", [128, C], F32, sp)
        hib = [sb(f"p1hi{i}", [128, C], BF16, sp) for i in range(2)]
        lob = [sb(f"p1lo{i}", [128, C], BF16, sp) for i in range(2)]
        dump = sb("p1dump", [128, C], F32, sp)
        acc = sb("p1acc", [128, 4], F32, sp)
        stg_h = [sb(f"p1sh{i}", [128, NG * 512], BF16, sp) for i in range(2)]
        stg_l = [sb(f"p1sl{i}", [128, NG * 512], BF16, sp) for i in range(2)]
        xn_done = []
        tr_done = []
        stg_free = [None, None]
        for tau in range(NT):
            J = tau // 4
            bx = xb[tau % 2]
            wx = env.dma(o, bx[:], dt["x"][tau * 128:(tau + 1) * 128, :],
                         waits=[xn_done[tau - 2]] if tau >= 2 else ())
            ws1 = env.scn(o, lambda e, b=bx: e.activation(
                out=dump[:], in_=b[:], func=AF.Copy, accum_out=acc[:, 0:1]),
                waits=[wx, env.tok("Vv")])
            ws2 = env.scn(o, lambda e, b=bx: e.activation(
                out=dump[:], in_=b[:], func=AF.Square, accum_out=acc[:, 1:2]),
                waits=[ws1])
            mc, vc, vpe, sd, r0, t1, t2, t3, dn, g0 = (sm[:, k:k + 1] for k in range(10))
            env.ven(o, lambda e, a=mc: e.tensor_scalar(a, acc[:, 0:1], 1.0 / C, None, OP.mult),
                    waits=[ws1, ws2, env.tok("Sc")])
            env.ven(o, lambda e, a=vc: e.tensor_scalar(a, acc[:, 1:2], 1.0 / C, None, OP.mult))
            env.ven(o, lambda e, a=t3, b=mc, c=vc: e.scalar_tensor_tensor(
                a, b, b[:, 0:1], c, OP.mult, OP.subtract))
            env.ven(o, lambda e, a=vc, b=t3: e.tensor_scalar(a, b, -1.0, None, OP.mult))
            wv_ = env.ven(o, lambda e, a=vpe, b=vc: e.tensor_scalar(a, b, EPS_LN, None, OP.add))
            wsd = env.scn(o, lambda e, a=sd, b=vpe: e.sqrt(a, b), waits=[wv_])
            wrc = env.ven(o, lambda e, a=r0, b=sd: e.reciprocal(a, b), waits=[wsd])
            env.ven(o, lambda e, a=t1, b=r0: e.tensor_tensor(a, b, b, OP.mult), waits=[wrc])
            env.ven(o, lambda e, a=t2, b=t1, c=vpe: e.tensor_tensor(a, b, c, OP.mult))
            env.ven(o, lambda e, a=t3, b=t2: e.tensor_scalar(a, b, -0.5, 1.5, OP.mult, OP.add))
            rsc = st["rs"][:, tau:tau + 1]
            env.ven(o, lambda e, a=rsc, b=r0, c=t3: e.tensor_tensor(a, b, c, OP.mult))
            env.ven(o, lambda e, a=rsc, tau=tau: e.tensor_scalar(
                st["u"][:, tau:tau + 1], a, 1.0, None, OP.add))
            env.ven(o, lambda e, a=rsc, b=mc, tau=tau: e.scalar_tensor_tensor(
                st["w"][:, tau:tau + 1], b, -1.0, a, OP.mult, OP.mult))
            env.ven(o, lambda e, a=dn, b=vpe: e.reciprocal(a, b))
            env.ven(o, lambda e, a=t1, b=vc, c=dn: e.tensor_tensor(a, b, c, OP.mult))
            wv_ = env.ven(o, lambda e, a=t2, b=t1: e.tensor_scalar(a, b, EPS_RMS, None, OP.add))
            wsd = env.scn(o, lambda e, a=sd, b=t2: e.sqrt(a, b), waits=[wv_])
            wrc = env.ven(o, lambda e, a=g0, b=sd: e.reciprocal(a, b), waits=[wsd])
            env.ven(o, lambda e, a=t1, b=g0: e.tensor_tensor(a, b, b, OP.mult), waits=[wrc])
            env.ven(o, lambda e, a=t1, b=t2: e.tensor_tensor(a, a, b, OP.mult))
            env.ven(o, lambda e, a=t1: e.tensor_scalar(a, a, -0.5, 1.5, OP.mult, OP.add))
            env.ven(o, lambda e, a=g0, b=t1, tau=tau: e.tensor_tensor(
                st["g1"][:, tau:tau + 1], a, b, OP.mult))
            wxn = env.ven(o, lambda e, b=bx, a=rsc, tau=tau: e.tensor_scalar(
                xnb[:], b[:], a, st["w"][:, tau:tau + 1], OP.mult, OP.add))
            xn_done.append(wxn)
            wrt = env.ven(o, lambda e: e.scalar_tensor_tensor(
                dump[:], xnb[:], 1.0, wrbc[:], OP.mult, OP.mult, accum_out=acc[:, 2:3]),
                waits=[wxn])
            env.ven(o, lambda e: e.tensor_scalar(acc[:, 3:4], acc[:, 2:3], brc[:, 0:1], 0.0,
                                                 OP.add, OP.is_gt), waits=[wrt])
            env.ven(o, lambda e, tau=tau: e.tensor_copy(st["rt"][:, tau:tau + 1], acc[:, 3:4]))
            bh, bl = hib[tau % 2], lob[tau % 2]
            wprev = tr_done[tau - 2] if tau >= 2 else []
            whh, wll = env.split(o, bh[:], bl[:], xnb[:], waits=[wxn] + wprev)
            sj = J % 2
            dst_h = stg_h[sj][:].rearrange("p (g t) -> p g t", g=NG)[
                :, :, (tau % 4) * 128:(tau % 4) * 128 + 128]
            dst_l = stg_l[sj][:].rearrange("p (g t) -> p g t", g=NG)[
                :, :, (tau % 4) * 128:(tau % 4) * 128 + 128]
            wtw = stg_free[sj] if (tau % 4 == 0 and stg_free[sj]) else []
            wt1 = env.dmaT(o, dst_h, bh[:], waits=[whh, wll] + wtw)
            wt2 = env.dmaT(o, dst_l, bl[:], waits=[wll] + wtw)
            tr_done.append([wt1, wt2])
            if tau % 4 == 3:
                wsh = env.dma(o, dt["S_xn_hi"][:].rearrange("p (g t) -> p g t", g=NG)
                              [:, :, J * 512:(J + 1) * 512],
                              stg_h[sj][:].rearrange("p (g t) -> p g t", g=NG),
                              waits=env.qtok("X"), eng="gpsimd")
                wst = env.dma(o, dt["S_xn_lo"][:].rearrange("p (g t) -> p g t", g=NG)
                              [:, :, J * 512:(J + 1) * 512],
                              stg_l[sj][:].rearrange("p (g t) -> p g t", g=NG),
                              eng="gpsimd")
                toks = [wsh, wst]
                for u in range(4):
                    toks.append(env.dma(
                        o, dt["S_xn2_hi"][:, (J * 4 + u) * C:(J * 4 + u + 1) * C],
                        stg_h[sj][:].rearrange("p (g u t) -> p g u t", g=NG, u=4)
                        [:, :, u, :], eng="gpsimd"))
                    toks.append(env.dma(
                        o, dt["S_xn2_lo"][:, (J * 4 + u) * C:(J * 4 + u + 1) * C],
                        stg_l[sj][:].rearrange("p (g u t) -> p g u t", g=NG, u=4)
                        [:, :, u, :], eng="gpsimd"))
                stg_free[sj] = toks
                env.mile["S_xnJ"][J] = toks
        env.fence(o)
        o.emit(blk)
    _free(sp)

    scA = []
    K_hi = sb("K_hi", [128, 4 * T], BF16, scA)
    K_lo = sb("K_lo", [128, 4 * T], BF16, scA)
    V_hi = sb("V_hi", [128, NT * VROW], BF16, scA)
    V_lo = sb("V_lo", [128, NT * VROW], BF16, scA)
    mask4 = sb("mask4", [128, 4 * 512], F32, scA)
    dirs_h = sb("dirs_h", [3, 8], BF16, scA)
    dirs_l = sb("dirs_l", [3, 8], BF16, scA)
    cosT = sb("cosT", [128, T], F32, scA)
    sinT = sb("sinT", [128, T], F32, scA)

    if "A1" in phases:
      scA1 = []
      with nc.Block("a1") as blk:
        o = Ops()
        xnJ_h = sb("a1xnh", [128, NG * 512], BF16, scA1)
        xnJ_l = sb("a1xnl", [128, NG * 512], BF16, scA1)
        wvh = sb("a1wvh", [128, NG * 512], BF16, scA1)
        wvl = sb("a1wvl", [128, NG * 512], BF16, scA1)
        wsg = [sb(f"a1w{i}", [128, 1024], F32, scA1) for i in range(3)]
        wbh = [sb(f"a1bh{i}", [128, 1024], BF16, scA1) for i in range(3)]
        wbl = [sb(f"a1bl{i}", [128, 1024], BF16, scA1) for i in range(3)]
        rop = [sb(f"a1rop{i}", [128, 512], F32, scA1) for i in range(4)]
        qsth = sb("a1qh", [128, 2 * 512], BF16, scA1)
        qstl = sb("a1ql", [128, 2 * 512], BF16, scA1)
        env.dma(o, mask4[:], dt["bandmask"])
        w1 = env.dma(o, rop[0][:], dt["cos_hi"][:, 0:512])
        w1b = env.dma(o, rop[1][:], dt["cos_hi"][:, 512:1024])
        w1c = env.dma(o, rop[2][:], dt["cos_hi"][:, 1024:1536])
        w1d = env.dma(o, rop[3][:], dt["cos_hi"][:, 1536:2048])
        w2 = env.dma(o, cosT[:], dt["cos_lo"])
        wc = env.ven(o, lambda e: e.tensor_tensor(cosT[:, 0:512], rop[0][:],
                                                  cosT[:, 0:512], OP.add), waits=[w1, w2])
        env.ven(o, lambda e: e.tensor_tensor(cosT[:, 512:1024], rop[1][:],
                                             cosT[:, 512:1024], OP.add), waits=[w1b])
        env.ven(o, lambda e: e.tensor_tensor(cosT[:, 1024:1536], rop[2][:],
                                             cosT[:, 1024:1536], OP.add), waits=[w1c])
        wc = env.ven(o, lambda e: e.tensor_tensor(cosT[:, 1536:2048], rop[3][:],
                                                  cosT[:, 1536:2048], OP.add), waits=[w1d])
        w1 = env.dma(o, rop[0][:], dt["sin_hi"][:, 0:512], waits=[wc])
        w1b = env.dma(o, rop[1][:], dt["sin_hi"][:, 512:1024])
        w1c = env.dma(o, rop[2][:], dt["sin_hi"][:, 1024:1536])
        w1d = env.dma(o, rop[3][:], dt["sin_hi"][:, 1536:2048])
        w2 = env.dma(o, sinT[:], dt["sin_lo"])
        env.ven(o, lambda e: e.tensor_tensor(sinT[:, 0:512], rop[0][:],
                                             sinT[:, 0:512], OP.add), waits=[w1, w2])
        env.ven(o, lambda e: e.tensor_tensor(sinT[:, 512:1024], rop[1][:],
                                             sinT[:, 512:1024], OP.add), waits=[w1b])
        env.ven(o, lambda e: e.tensor_tensor(sinT[:, 1024:1536], rop[2][:],
                                             sinT[:, 1024:1536], OP.add), waits=[w1c])
        wsin = env.ven(o, lambda e: e.tensor_tensor(sinT[:, 1536:2048], rop[3][:],
                                                    sinT[:, 1536:2048], OP.add), waits=[w1d])
        wd1 = env.dma(o, wsg[0][0:3, 0:8], dt["dirs64"], waits=[wsin])
        wd2 = env.dma(o, wsg[0][0:3, 8:16], dt["dirs_lo64"])
        wdh = env.scn(o, lambda e: e.copy(dirs_h[:], wsg[0][0:3, 0:8]), waits=[wd1, wd2])
        env.ven(o, lambda e: e.tensor_tensor(rop[1][0:3, 0:8], wsg[0][0:3, 0:8],
                                             dirs_h[:], OP.subtract), waits=[wdh])
        env.ven(o, lambda e: e.tensor_tensor(rop[1][0:3, 0:8], rop[1][0:3, 0:8],
                                             wsg[0][0:3, 8:16], OP.add))
        wdirs = env.ven(o, lambda e: e.tensor_copy(dirs_l[:], rop[1][0:3, 0:8]))
        wv_spl = [wdirs]
        for g in range(NG):
            slot = g % 3
            wl_ = env.dma(o, wsg[slot][:, 0:512], dt["Wv"][g * 128:(g + 1) * 128, :],
                          waits=[wv_spl[-3]] if len(wv_spl) >= 3 else [wdirs])
            _, w2_ = env.split(o, wvh[:, g * 512:(g + 1) * 512],
                               wvl[:, g * 512:(g + 1) * 512], wsg[slot][:, 0:512],
                               waits=[wl_])
            wv_spl.append(w2_)
        for tau in range(NT):
            env.ven(o, lambda e, tau=tau: e.memset(
                V_hi[:, tau * VROW + 512:tau * VROW + 513], 1.0))
            env.ven(o, lambda e, tau=tau: e.memset(
                V_lo[:, tau * VROW + 512:tau * VROW + 513], 0.0))
        wones = env.tok("Vv")
        pfree = wones
        wring = wv_spl[-1]
        qst_free = None
        for J in range(NJ):
            wxn1 = env.dma(o, xnJ_h[:].rearrange("p (g t) -> p g t", g=NG),
                           dt["S_xn_hi"][:].rearrange("p (g t) -> p g t", g=NG)
                           [:, :, J * 512:(J + 1) * 512],
                           waits=(env.mile["S_xnJ"].get(J) or []) + [env.tok("Tn")])
            wxn2 = env.dma(o, xnJ_l[:].rearrange("p (g t) -> p g t", g=NG),
                           dt["S_xn_lo"][:].rearrange("p (g t) -> p g t", g=NG)
                           [:, :, J * 512:(J + 1) * 512])
            for ti in range(4):
                tau = J * 4 + ti
                for g in range(NG):
                    env.mm3(o, P[ti][:],
                            xnJ_h[:, g * 512 + ti * 128: g * 512 + ti * 128 + 128],
                            xnJ_l[:, g * 512 + ti * 128: g * 512 + ti * 128 + 128],
                            wvh[:, g * 512:(g + 1) * 512], wvl[:, g * 512:(g + 1) * 512],
                            start=(g == 0), stop=(g == NG - 1),
                            waits=[wxn1, wxn2, wv_spl[-1], wones, pfree] if g == 0 else (),
                            inc=(g == NG - 1))
            wvm = env.tok("Tn")
            for ti in range(4):
                tau = J * 4 + ti
                env.split(o, V_hi[:, tau * VROW:tau * VROW + 512],
                          V_lo[:, tau * VROW:tau * VROW + 512], P[ti][:], waits=[wvm])
            kmm = []
            for g in range(NG):
                slot = g % 3
                wl_ = env.dma(o, wsg[slot][:, 0:512], dt["Wk"][g * 128:(g + 1) * 128, :],
                              waits=[wring] if g == 0 else
                              ([kmm[g - 3]] if g >= 3 else ()))
                whh, wll = env.split(o, wbh[slot][:, 0:512], wbl[slot][:, 0:512],
                                     wsg[slot][:, 0:512], waits=[wl_])
                for kap in range(4):
                    env.mm3(o, P[4 + kap][:],
                            wbh[slot][:, kap * 128:(kap + 1) * 128],
                            wbl[slot][:, kap * 128:(kap + 1) * 128],
                            xnJ_h[:, g * 512:(g + 1) * 512],
                            xnJ_l[:, g * 512:(g + 1) * 512],
                            start=(g == 0), stop=(g == NG - 1),
                            waits=([wll] + ([pfree] if g == 0 else []))
                            if kap == 0 else (),
                            inc=(kap == 3))
                kmm.append(env.tok("Tn"))
            wkm = env.tok("Tn")
            for g2 in range(2):
                pa, pb_ = P[4 + 2 * g2], P[4 + 2 * g2 + 1]
                cs = cosT[:, J * 512:(J + 1) * 512]
                sn = sinT[:, J * 512:(J + 1) * 512]
                env.ven(o, lambda e, p=pa, c=cs: e.tensor_tensor(rop[0][:], p[:], c, OP.mult),
                        waits=[wkm])
                env.ven(o, lambda e, p=pb_, s=sn: e.tensor_tensor(rop[1][:], p[:], s, OP.mult))
                env.ven(o, lambda e, p=pb_, c=cs: e.tensor_tensor(rop[2][:], p[:], c, OP.mult))
                env.ven(o, lambda e, p=pa, s=sn: e.tensor_tensor(rop[3][:], p[:], s, OP.mult))
                wA = env.ven(o, lambda e: e.tensor_tensor(rop[0][:], rop[0][:], rop[1][:],
                                                          OP.subtract))
                wB = env.ven(o, lambda e: e.tensor_tensor(rop[2][:], rop[2][:], rop[3][:],
                                                          OP.add))
                kA = (2 * g2) * T + J * 512
                kB = (2 * g2 + 1) * T + J * 512
                env.split(o, K_hi[:, kA:kA + 512], K_lo[:, kA:kA + 512], rop[0][:], waits=[wA])
                env.split(o, K_hi[:, kB:kB + 512], K_lo[:, kB:kB + 512], rop[2][:], waits=[wB])
            pfree = env.tok("Vv")
            wring = wkm
            for hq in range(2):
                qmm = []
                for g in range(NG):
                    slot = g % 3
                    wl_ = env.dma(o, wsg[slot][:],
                                  dt["Wq"][g * 128:(g + 1) * 128,
                                           hq * 1024:(hq + 1) * 1024],
                                  waits=[wring] if g == 0 else
                                  ([qmm[g - 3]] if g >= 3 else ()))
                    whh, wll = env.split(o, wbh[slot][:], wbl[slot][:], wsg[slot][:],
                                         waits=[wl_])
                    for hh in range(4):
                        for kk in range(2):
                            env.mm3(o, P[hh * 2 + kk][:],
                                    wbh[slot][:, hh * 256 + kk * 128:
                                             hh * 256 + kk * 128 + 128],
                                    wbl[slot][:, hh * 256 + kk * 128:
                                             hh * 256 + kk * 128 + 128],
                                    xnJ_h[:, g * 512:(g + 1) * 512],
                                    xnJ_l[:, g * 512:(g + 1) * 512],
                                    start=(g == 0), stop=(g == NG - 1),
                                    waits=([wll] + ([pfree] if g == 0 else []))
                                    if (hh == 0 and kk == 0) else (),
                                    inc=(hh == 3 and kk == 1))
                    qmm.append(env.tok("Tn"))
                wqm = env.tok("Tn")
                for hh in range(4):
                    h = hq * 4 + hh
                    pa, pb_ = P[hh * 2], P[hh * 2 + 1]
                    cs = cosT[:, J * 512:(J + 1) * 512]
                    sn = sinT[:, J * 512:(J + 1) * 512]
                    wqf = qst_free if qst_free else []
                    env.ven(o, lambda e, p=pa, c=cs: e.tensor_tensor(rop[0][:], p[:], c,
                                                                     OP.mult),
                            waits=[wqm] + wqf)
                    env.ven(o, lambda e, p=pb_, s=sn: e.tensor_tensor(rop[1][:], p[:], s,
                                                                      OP.mult))
                    env.ven(o, lambda e, p=pb_, c=cs: e.tensor_tensor(rop[2][:], p[:], c,
                                                                      OP.mult))
                    env.ven(o, lambda e, p=pa, s=sn: e.tensor_tensor(rop[3][:], p[:], s,
                                                                     OP.mult))
                    wA = env.ven(o, lambda e: e.tensor_tensor(rop[0][:], rop[0][:], rop[1][:],
                                                              OP.subtract))
                    wB = env.ven(o, lambda e: e.tensor_tensor(rop[2][:], rop[2][:], rop[3][:],
                                                              OP.add))
                    env.split(o, qsth[:, 0:512], qstl[:, 0:512], rop[0][:], waits=[wA])
                    _, wsp = env.split(o, qsth[:, 512:1024], qstl[:, 512:1024], rop[2][:],
                                       waits=[wB])
                    wq_a = env.dma(o, dt["S_q_hi"][:].rearrange("p (g t) -> p g t", g=NG)
                                   [:, 2 * h:2 * h + 2, J * 512:(J + 1) * 512],
                                   qsth[:].rearrange("p (g t) -> p g t", g=2),
                                   waits=[wsp], eng="gpsimd")
                    wq_b = env.dma(o, dt["S_q_lo"][:].rearrange("p (g t) -> p g t", g=NG)
                                   [:, 2 * h:2 * h + 2, J * 512:(J + 1) * 512],
                                   qstl[:].rearrange("p (g t) -> p g t", g=2),
                                   eng="gpsimd")
                    qst_free = [wq_a, wq_b]
                pfree = env.tok("Vv")
                wring = wqm
        env.mile["S_q"] = qst_free
        env.fence(o)
        o.emit(blk)
      _free(scA1)

    if "A2" in phases:
      scA2 = []
      with nc.Block("a2") as blk:
        o = Ops()
        qb_h = [sb("a2qh0", [128, 2 * T], BF16, scA2)]
        qb_l = [sb("a2ql0", [128, 2 * T], BF16, scA2)]
        eb = [sb(f"a2eb{i}", [128, 512], F32, scA2) for i in range(4)]
        ebm = [sb(f"a2ebm{i}", [128, 512], F32, scA2) for i in range(4)]
        e_hi = [sb(f"a2eh{i}", [128, 512], BF16, scA2) for i in range(4)]
        e_lo = [sb(f"a2el{i}", [128, 512], BF16, scA2) for i in range(4)]
        of32 = [sb(f"a2of{i}", [128, 512], F32, scA2) for i in range(2)]
        og32 = [sb(f"a2og{i}", [128, 512], F32, scA2) for i in range(2)]
        ohf = sb("a2ohf", [128, 512], F32, scA2)
        oh_hi = sb("a2ohh", [128, 512], BF16, scA2)
        oh_lo = sb("a2ohl", [128, 512], BF16, scA2)
        kp_row = sb("a2kp", [1, T], F32, scA2)
        qp_row = sb("a2qp", [1, T], F32, scA2)
        rr = sb("a2rr", [1, 512], F32, scA2)
        rrg = sb("a2rrg", [1, 512], F32, scA2)
        pS = [P[0], P[1], P[2], P[3]]; pO = [P[4], P[5]]; pR = P[6]; pB = P[7]
        wstf = [sb(f"a2wsf{i}", [128, C], F32, scA2) for i in range(2)]
        wsth = [sb(f"a2wsh{i}", [128, C], BF16, scA2) for i in range(2)]
        wstl = [sb(f"a2wsl{i}", [128, C], BF16, scA2) for i in range(2)]
        wstage_state = {"idx": 0, "ld": {}, "st": {}}

        def stage_one_w():
            k = wstage_state["idx"]
            if k >= 2 * NF:
                return
            wstage_state["idx"] = k + 1
            phi, which = k // 2, k % 2
            wsrc = dt["w_gate"] if which == 0 else dt["w_up"]
            dsth = dt["S_wg_hi"] if which == 0 else dt["S_wu_hi"]
            dstl = dt["S_wg_lo"] if which == 0 else dt["S_wu_lo"]
            slot = k % 2
            wld = []
            for q4 in range(4):
                ap = wsrc.rearrange("(g p) f -> p g f", p=128)[
                    :, q4 * 4:(q4 + 1) * 4, phi * 128:(phi + 1) * 128]
                wld.append(env.dma(
                    o, wstf[slot][:].rearrange("p (g f) -> p g f", g=NG)
                    [:, q4 * 4:(q4 + 1) * 4, :], ap,
                    waits=(wstage_state["ld"].get(slot, [])
                           if q4 == 0 else []), eng="scalar"))
            for g in range(NG):
                env.scn(o, lambda e, s=slot, g=g: e.activation(
                    out=wsth[s][:, g * 128:(g + 1) * 128],
                    in_=wstf[s][:, g * 128:(g + 1) * 128],
                    func=AF.Copy, scale=n2w_fm[:, g:g + 1]),
                    waits=(wld + wstage_state["st"].get(slot, [])) if g == 0 else ())
            wsc_dn = env.tok("Sc")
            for g in range(NG):
                wl2 = env.ven(o, lambda e, s=slot, g=g: e.scalar_tensor_tensor(
                    wstl[s][:, g * 128:(g + 1) * 128], wstf[s][:, g * 128:(g + 1) * 128],
                    n2w_fm[:, g:g + 1], wsth[s][:, g * 128:(g + 1) * 128],
                    OP.mult, OP.subtract),
                    waits=[wsc_dn] if g == 0 else ())
            wstage_state["ld"][slot] = [wl2]
            s1 = env.dma(o, dsth[:, phi * C:(phi + 1) * C], wsth[slot][:], waits=[wl2],
                         eng="gpsimd")
            s2 = env.dma(o, dstl[:, phi * C:(phi + 1) * C], wstl[slot][:], eng="gpsimd")
            wstage_state["st"][slot] = [s1, s2]
        q_free = [None]
        comb_prev = None
        oh_store_prev = None
        prev_exps = []
        gp_free = []
        pv_gate = []
        for h in range(H):
            g2 = h // 4
            qh, ql = qb_h[0], qb_l[0]
            wq1 = env.dma(o, qh[:], dt["S_q_hi"][:, (2 * h) * T:(2 * h + 2) * T],
                          waits=(env.mile.get("S_q") or []) + (
                              [q_free[0]] if q_free[0] else []))
            wq2 = env.dma(o, ql[:], dt["S_q_lo"][:, (2 * h) * T:(2 * h + 2) * T])
            whd = env.tok("Vv")
            kq_banks = [pS[0], pS[1], pS[2], pS[3]]
            qp_banks = [pO[0], pO[1], pR, pB]
            for J in range(NJ):
                b0 = (2 * g2) * T + J * 512
                pk = kq_banks[J]
                o.em("tensor", lambda e, b=b0, h=h, p=pk: e.matmul(
                    p[0:1, :], dirs_h[:, h:h + 1], K_hi[0:3, b:b + 512],
                    start=True, stop=False),
                    waits=[whd] + list(prev_exps))
                o.em("tensor", lambda e, b=b0, h=h, p=pk: e.matmul(
                    p[0:1, :], dirs_h[:, h:h + 1], K_lo[0:3, b:b + 512],
                    start=False, stop=False))
                env.ten(o, lambda e, b=b0, h=h, p=pk: e.matmul(
                    p[0:1, :], dirs_l[:, h:h + 1], K_hi[0:3, b:b + 512],
                    start=False, stop=True))
            wkm_all = env.tok("Tn")
            for J in range(NJ):
                env.ven(o, lambda e, J=J, p=kq_banks[J]: e.tensor_copy(
                    kp_row[0:1, J * 512:(J + 1) * 512], p[0:1, :]),
                    waits=[wkm_all] if J == 0 else ())
            wkp_done = env.tok("Vv")
            for J in range(NJ):
                pq = qp_banks[J]
                o.em("tensor", lambda e, J=J, q=qh, h=h, p=pq: e.matmul(
                    p[0:1, :], dirs_h[:, h:h + 1], q[0:3, J * 512:J * 512 + 512],
                    start=True, stop=False),
                    waits=[wq1, wq2] + pv_gate + gp_free
                    + ([comb_prev] if comb_prev else []))
                o.em("tensor", lambda e, J=J, q=ql, h=h, p=pq: e.matmul(
                    p[0:1, :], dirs_h[:, h:h + 1], q[0:3, J * 512:J * 512 + 512],
                    start=False, stop=False))
                env.ten(o, lambda e, J=J, q=qh, h=h, p=pq: e.matmul(
                    p[0:1, :], dirs_l[:, h:h + 1], q[0:3, J * 512:J * 512 + 512],
                    start=False, stop=True))
            wqm_all = env.tok("Tn")
            for J in range(NJ):
                env.ven(o, lambda e, J=J, p=qp_banks[J]: e.tensor_copy(
                    qp_row[0:1, J * 512:(J + 1) * 512], p[0:1, :]),
                    waits=[wqm_all] if J == 0 else ())
            w_rows = env.tok("Vv")
            for j in range(NJ):
                ni = 4 * j + 4
                evac_toks = []
                for sweep in range(2):
                    exp_tok = []
                    split_tok = []
                    pv_tok = []
                    pend = []
                    LEAD = 3
                    for it in range(ni + LEAD):
                      if it < ni:
                        i = it
                        ps = pS[i % 4]
                        wbank = [exp_tok[i - 4]] if i >= 4 else (
                            list(prev_exps) + [w_rows])
                        if sweep == 0:
                            for dc in range(2):
                                env.mm3(o, ps[:],
                                        K_hi[:, (2 * g2 + dc) * T + i * 128:
                                             (2 * g2 + dc) * T + i * 128 + 128],
                                        K_lo[:, (2 * g2 + dc) * T + i * 128:
                                             (2 * g2 + dc) * T + i * 128 + 128],
                                        qh[:, dc * T + j * 512:dc * T + j * 512 + 512],
                                        ql[:, dc * T + j * 512:dc * T + j * 512 + 512],
                                        start=(dc == 0), stop=(dc == 1),
                                        waits=([wq1, wq2] + wbank) if dc == 0 else (),
                                        inc=(dc == 1))
                        else:
                            env.ten(o, lambda e, i=i, j=j, p=ps: e.matmul(
                                p[:], kp_row[0:1, i * 128:(i + 1) * 128],
                                qp_row[0:1, j * 512:(j + 1) * 512], start=True, stop=True),
                                waits=[w_rows] + wbank)
                        wmm = env.tok("Tn")
                        scl = 0.0625 if sweep == 0 else 1.0
                        wex = env.scn(o, lambda e, p=ps, b=eb[i % 4], s=scl: e.activation(
                            out=b[:], in_=p[:], func=AF.Exp, scale=s),
                            waits=[wmm] + ([split_tok[i - 4]] if i >= 4 else []))
                        exp_tok.append(wex)
                        ksl = i - 4 * j
                        if 0 <= ksl <= 3:
                            wmk = env.ven(o, lambda e, k=ksl, a=ebm[i % 4], b=eb[i % 4]:
                                          e.tensor_tensor(a[:], b[:],
                                                          mask4[:, k * 512:(k + 1) * 512],
                                                          OP.mult), waits=[wex])
                            esrc = ebm[i % 4]
                        else:
                            wmk = wex
                            esrc = eb[i % 4]
                        slot = i % 4
                        whh, wll = env.split(o, e_hi[slot][:], e_lo[slot][:], esrc[:],
                                             waits=[wmk] + (
                                                 [pv_tok[i - 4]] if i >= 4 else []))
                        split_tok.append(wll)
                        pend.append((i, slot, whh, wll))
                      if it >= LEAD and pend:
                        i, slot, whh, wll = pend.pop(0)
                        base = i * VROW + g2 * 256
                        for dc in range(2):
                            env.mm3(o, pO[dc][:],
                                    V_hi[:, base + dc * 128: base + dc * 128 + 128],
                                    V_lo[:, base + dc * 128: base + dc * 128 + 128],
                                    e_hi[slot][:], e_lo[slot][:],
                                    start=(i == 0), stop=(i == ni - 1),
                                    waits=([whh, wll, w_rows] + pv_gate) if i == 0
                                    else [whh, wll],
                                    inc=False)
                        o.em("tensor", lambda e, i=i, s=slot: e.matmul(
                            pR[0:1, :], V_hi[:, i * VROW + 512:i * VROW + 513],
                            e_hi[s][:], start=(i == 0), stop=False),
                            waits=([whh, w_rows] + pv_gate) if i == 0 else [whh])
                        pv_tok.append(env.ten(
                            o, lambda e, i=i, s=slot, last=(i == ni - 1): e.matmul(
                                pR[0:1, :], V_hi[:, i * VROW + 512:i * VROW + 513],
                                e_lo[s][:], start=False, stop=last), waits=[wll]))
                    wpv = env.tok("Tn")
                    dsts = of32 if sweep == 0 else og32
                    rdst = rr if sweep == 0 else rrg
                    wcb = [comb_prev] if comb_prev else []
                    ev0 = env.scn(o, lambda e, d=dsts: e.copy(d[0][:], pO[0][:]),
                                  waits=[wpv] + wcb)
                    ev1 = env.scn(o, lambda e, d=dsts: e.copy(d[1][:], pO[1][:]))
                    evr = env.ven(o, lambda e, a=rdst: e.reciprocal(a[:], pR[0:1, :]),
                                  waits=[wpv] + wcb)
                    evac_toks += [ev0, ev1, evr]
                    pv_gate = [ev0, ev1, evr]
                    prev_exps = exp_tok[-4:]
                wba = env.ten(o, lambda e: e.matmul(pB[:], ones_row[:], rr[0:1, :],
                                                    start=True, stop=True),
                              waits=[evac_toks[2]] + gp_free)
                wbb = env.ten(o, lambda e: e.matmul(pR[:], ones_row[:], rrg[0:1, :],
                                                    start=True, stop=True),
                              waits=[evac_toks[5]])
                wt1a = env.ven(o, lambda e, h=h: e.scalar_tensor_tensor(
                    of32[0][:], of32[0][:], hs1m_col[:, h:h + 1], pB[:], OP.mult, OP.mult),
                    waits=[wba, evac_toks[0], evac_toks[1], evac_toks[3], evac_toks[4]])
                wt1b = env.ven(o, lambda e, h=h: e.scalar_tensor_tensor(
                    of32[1][:], of32[1][:], hs1m_col[:, h:h + 1], pB[:], OP.mult, OP.mult))
                for dc in range(2):
                    wt2 = env.ven(o, lambda e, dc=dc, h=h: e.scalar_tensor_tensor(
                        og32[dc][:], og32[dc][:], hs_col[:, h:h + 1], pR[:],
                        OP.mult, OP.mult), waits=[wbb])
                    wt3 = env.ven(o, lambda e, dc=dc: e.tensor_tensor(
                        ohf[:], of32[dc][:], og32[dc][:], OP.add))
                    whh, wll = env.split(o, oh_hi[:], oh_lo[:], ohf[:],
                                         waits=[wt3] + (oh_store_prev or []))
                    go = (2 * h + dc) * T + j * 512
                    wo_a = env.dma(o, dt["S_oh_hi"][:, go:go + 512], oh_hi[:],
                                   waits=[whh, wll], eng="gpsimd")
                    wo_b = env.dma(o, dt["S_oh_lo"][:, go:go + 512], oh_lo[:],
                                   eng="gpsimd")
                    oh_store_prev = [wo_a, wo_b]
                gp_free = [env.tok("Vv")]
                pv_gate = [evac_toks[3], evac_toks[4], env.tok("Vv")]
                comb_prev = env.tok("Vv")
                stage_one_w()
                stage_one_w()
                stage_one_w()
            q_free[0] = env.tok("Tn")
        while wstage_state["idx"] < 2 * NF:
            stage_one_w()
        env.mile["S_oh"] = env.qtok("G")
        env.mile["S_wgu"] = env.qtok("G")
        env.fence(o)
        o.emit(blk)
      _free(scA2)
    _free(scA)

    scCD = []
    h2fm_hi = sb("h2fm_hi", [128, NG * T], BF16, scCD)
    h2fm_lo = sb("h2fm_lo", [128, NG * T], BF16, scCD)
    if "C" in phases:
      scC = []
      with nc.Block("c") as blk:
        o = Ops()
        wbh = [sb(f"cwbh{i}", [128, C], BF16, scC) for i in range(2)]
        wbl = [sb(f"cwbl{i}", [128, C], BF16, scC) for i in range(2)]
        xnP_h = sb("cxnh", [128, NG * 256], BF16, scC)
        xnP_l = sb("cxnl", [128, NG * 256], BF16, scC)
        vfm_hi = sb("cvfh", [128, NG * 256], BF16, scC)
        vfm_lo = sb("cvfl", [128, NG * 256], BF16, scC)
        zaw = [sb(f"czaw{i}", [128, C], F32, scC) for i in range(2)]
        xcb = sb("cxb", [128, C], F32, scC)
        c0_bc = sb("c0_bc", [128, C], BF16, scC)
        acc = sb("cacc", [128, 2], F32, scC)
        prev = {"lo": None, "mm": None, "st": None}
        for g in range(NG):
            slot = g % 2
            wl_ = env.dma(o, zaw[0][:], dt["wo"][g * 128:(g + 1) * 128, :],
                          waits=[prev["lo"], prev["mm"]])
            whh, wll = env.split(o, wbh[slot][:], wbl[slot][:], zaw[0][:],
                                 waits=[wl_] + (prev["st"] or []))
            for dtl in range(4):
                env.ten(o, lambda e, g=g, d=dtl: e.matmul(
                    P[4 + d][0:1, :], bvf[:, g:g + 1], zaw[0][:, d * 512:(d + 1) * 512],
                    start=(g == 0), stop=(g == NG - 1)), waits=[wl_])
            prev["mm"] = env.tok("Tn")
            prev["lo"] = wll
            ws_a = env.dma(o, dt["S_wo_hi"][:, g * C:(g + 1) * C], wbh[slot][:],
                           waits=[whh, wll], eng="gpsimd")
            ws_b = env.dma(o, dt["S_wo_lo"][:, g * C:(g + 1) * C], wbl[slot][:],
                           eng="gpsimd")
            prev["st"] = [ws_a, ws_b]
        wc0 = env.tok("Tn")
        for dtl in range(4):
            env.ven(o, lambda e, d=dtl: e.tensor_copy(
                zaw[0][0:1, d * 512:(d + 1) * 512], P[4 + d][0:1, :]), waits=[wc0])
        wrow = env.tok("Vv")
        for dtl in range(4):
            wm = env.ten(o, lambda e, d=dtl: e.matmul(
                P[4 + d][:], ones_row[:], zaw[0][0:1, d * 512:(d + 1) * 512],
                start=True, stop=True), waits=[wrow])
            env.ven(o, lambda e, d=dtl: e.tensor_copy(c0_bc[:, d * 512:(d + 1) * 512],
                                                      P[4 + d][:]), waits=[wm])
        wc0bc = env.tok("Vv")
        for g in range(NG):
            slot = g % 2
            wl_ = env.dma(o, zaw[0][:], dt["wv"][g * 128:(g + 1) * 128, :],
                          waits=[prev["lo"], wc0bc if g == 0 else None])
            wsc = env.ven(o, lambda e, g=g: e.tensor_scalar(
                zaw[0][:], zaw[0][:], n1w_fm[:, g:g + 1], None, OP.mult), waits=[wl_])
            whh, wll = env.split(o, wbh[slot][:], wbl[slot][:], zaw[0][:],
                                 waits=[wsc] + (prev["st"] or []))
            prev["lo"] = wll
            ws_a = env.dma(o, dt["S_wv_hi"][:, g * C:(g + 1) * C], wbh[slot][:],
                           waits=[whh, wll], eng="gpsimd")
            ws_b = env.dma(o, dt["S_wv_lo"][:, g * C:(g + 1) * C], wbl[slot][:],
                           eng="gpsimd")
            prev["st"] = [ws_a, ws_b]
        env.mile["S_wv"] = list(prev["st"])
        h2_tr = {}
        pfree = wc0bc
        pair_wop = env.tok("Tn")
        za_store = {}
        for tp in range(8):
            t0 = 2 * tp
            wxn1 = env.dma(o, xnP_h[:], dt["S_xn2_hi"][:, t0 * C:(t0 + 2) * C],
                           waits=(env.mile["S_xnJ"].get(t0 // 4) or [])
                           + (env.mile.get("S_wv") or []) + [env.tok("Tn")]
                           + (h2_tr.get(t0 - 1) or []))
            wxn2 = env.dma(o, xnP_l[:], dt["S_xn2_lo"][:, t0 * C:(t0 + 2) * C],
                           waits=(h2_tr.get(t0 - 1) or []))
            vsp = []
            mmg = {}
            for rnd in range(2):
                for g in range(NG):
                    k = rnd * NG + g
                    slot = k % 2
                    ww = [mmg[k - 2]] if k >= 2 else [pair_wop]
                    co = g * C + rnd * 1024
                    wl1 = env.dma(o, wbh[slot][:, 0:1024], dt["S_wv_hi"][:, co:co + 1024],
                                  waits=ww + (env.mile.get("S_wv") or []))
                    wl2 = env.dma(o, wbl[slot][:, 0:1024], dt["S_wv_lo"][:, co:co + 1024],
                                  waits=ww + (env.mile.get("S_wv") or []), eng="scalar")
                    for e8 in range(8):
                        env.mm3(o, P[e8][:, 0:256],
                                wbh[slot][:, e8 * 128:(e8 + 1) * 128],
                                wbl[slot][:, e8 * 128:(e8 + 1) * 128],
                                xnP_h[:].rearrange("p (u g t) -> p g u t", u=2, g=NG)
                                [:, g, :, :],
                                xnP_l[:].rearrange("p (u g t) -> p g u t", u=2, g=NG)
                                [:, g, :, :],
                                start=(g == 0), stop=(g == NG - 1),
                                waits=([wl1, wl2]
                                       + ([wxn1, wxn2] if g == 0 else [])
                                       + ([vsp[-1]] if (g == 0 and rnd == 1) else []))
                                if e8 == 0 else (),
                                inc=(e8 == 7))
                    mmg[k] = env.tok("Tn")
                wvm = env.tok("Tn")
                for e8 in range(8):
                    eps = rnd * 8 + e8
                    _, wsp = env.split(o, vfm_hi[:, eps * 256:(eps + 1) * 256],
                                       vfm_lo[:, eps * 256:(eps + 1) * 256],
                                       P[e8][:, 0:256],
                                       waits=[wvm] if e8 == 0 else ())
                    vsp.append(wsp)
            vdone = vsp[-1]
            mmg = {}
            for eps in range(NG):
                slot = eps % 2
                ww = [mmg[eps - 2]] if eps >= 2 else [env.tok("Tn")]
                wl1 = env.dma(o, wbh[slot][:], dt["S_wo_hi"][:, eps * C:(eps + 1) * C],
                              waits=ww)
                wl2 = env.dma(o, wbl[slot][:], dt["S_wo_lo"][:, eps * C:(eps + 1) * C],
                              waits=ww, eng="scalar")
                for ti in range(2):
                    for dtl in range(4):
                        env.mm3(o, P[ti * 4 + dtl][:],
                                vfm_hi[:, eps * 256 + ti * 128: eps * 256 + ti * 128 + 128],
                                vfm_lo[:, eps * 256 + ti * 128: eps * 256 + ti * 128 + 128],
                                wbh[slot][:, dtl * 512:(dtl + 1) * 512],
                                wbl[slot][:, dtl * 512:(dtl + 1) * 512],
                                start=(eps == 0), stop=(eps == NG - 1),
                                waits=([wl1, wl2] + ([vdone] if eps == 0 else []))
                                if (ti == 0 and dtl == 0) else (),
                                inc=(ti == 1 and dtl == 3))
                mmg[eps] = env.tok("Tn")
            wop = env.tok("Tn")
            pair_wop = wop
            wza_tok = {}
            for ti in range(2):
                tau = t0 + ti
                for dtl in range(4):
                    env.ven(o, lambda e, p=P[ti * 4 + dtl], d=dtl, tau=tau, ti=ti:
                            e.scalar_tensor_tensor(zaw[ti][:, d * 512:(d + 1) * 512], p[:],
                                                   st["g1"][:, tau:tau + 1],
                                                   c0_bc[:, d * 512:(d + 1) * 512],
                                                   OP.mult, OP.add),
                            waits=([wop, wc0bc] + (env.mile.get("S_wv") or [])
                                   + ([za_store.get(tau - 1 - ti)] if tau - 1 - ti >= 0
                                      else []))
                            if dtl == 0 else ())
                wza_tok[ti] = env.tok("Vv")
                za_store[tau] = env.dma(o, dt["S_za"][tau * 128:(tau + 1) * 128, :],
                                        zaw[ti][:], waits=[wza_tok[ti]], eng="gpsimd")
            pfree = wza_tok[1]
            prev_split = None
            for ti in range(2):
                tau = t0 + ti
                wxl = env.dma(o, xcb[:], dt["x"][tau * 128:(tau + 1) * 128, :],
                              waits=[wza_tok[1]] if ti == 0 else [prev_split])
                wxa = env.ven(o, lambda e, tau=tau: e.tensor_scalar(
                    xcb[:], xcb[:], st["rs"][:, tau:tau + 1], st["w"][:, tau:tau + 1],
                    OP.mult, OP.add), waits=[wxl])
                wh_ = env.ven(o, lambda e, ti=ti: e.tensor_tensor(xcb[:], zaw[ti][:], xcb[:],
                                                                  OP.add), waits=[wxa])
                wsq = env.scn(o, lambda e, ti=ti: e.activation(
                    out=zaw[ti][:], in_=xcb[:], func=AF.Square, accum_out=acc[:, 0:1]),
                    waits=[wh_, za_store[tau]])
                vpe2, sd2, r02, t12, t22 = (sm[:, k:k + 1] for k in range(10, 15))
                env.ven(o, lambda e, a=vpe2: e.tensor_scalar(
                    a, acc[:, 0:1], 1.0 / C, EPS_RMS, OP.mult, OP.add), waits=[wsq])
                wv_ = env.tok("Vv")
                wsd = env.scn(o, lambda e, a=sd2, b=vpe2: e.sqrt(a, b), waits=[wv_])
                wrc = env.ven(o, lambda e, a=r02, b=sd2: e.reciprocal(a, b), waits=[wsd])
                env.ven(o, lambda e, a=t12, b=r02: e.tensor_tensor(a, b, b, OP.mult),
                        waits=[wrc])
                env.ven(o, lambda e, a=t22, b=t12, c=vpe2: e.tensor_tensor(a, b, c, OP.mult))
                env.ven(o, lambda e, a=t22: e.tensor_scalar(a, a, -0.5, 1.5, OP.mult, OP.add))
                env.ven(o, lambda e, a=r02, b=t22, tau=tau: e.tensor_tensor(
                    st["rs2"][:, tau:tau + 1], a, b, OP.mult))
                wh2 = env.ven(o, lambda e, tau=tau: e.tensor_scalar(
                    xcb[:], xcb[:], st["rs2"][:, tau:tau + 1], None, OP.mult))
                wtp = h2_tr[tau - 1] if tau >= 1 else []
                whh, wll = env.split(o, xnP_h[:, 0:C], xnP_l[:, 0:C], xcb[:],
                                     waits=[wh2] + wtp)
                prev_split = wll
                dsth = h2fm_hi[:].rearrange("p (g t) -> p g t", g=NG)[
                    :, :, tau * 128:(tau + 1) * 128]
                dstl = h2fm_lo[:].rearrange("p (g t) -> p g t", g=NG)[
                    :, :, tau * 128:(tau + 1) * 128]
                wht_a = env.dmaT(o, dsth, xnP_h[:, 0:C], waits=[whh, wll])
                wht_b = env.dmaT(o, dstl, xnP_l[:, 0:C], waits=[wll])
                h2_tr[tau] = [wht_a, wht_b]
        env.mile["S_za"] = env.qtok("G")
        env.mile["h2fm"] = list(h2_tr[NT - 1])
        env.fence(o)
        o.emit(blk)
      _free(scC)

    if "D" in phases:
      scD = []
      with nc.Block("d") as blk:
        o = Ops()
        wgh = [sb(f"dwgh{i}", [128, NG * 128], BF16, scD) for i in range(3)]
        wgl = [sb(f"dwgl{i}", [128, NG * 128], BF16, scD) for i in range(3)]
        wuh = [sb(f"dwuh{i}", [128, NG * 128], BF16, scD) for i in range(3)]
        wul = [sb(f"dwul{i}", [128, NG * 128], BF16, scD) for i in range(3)]
        sg = sb("dsg", [128, 512], F32, scD)
        pr = sb("dpr", [128, 512], F32, scD)
        prh = [sb(f"dprh{i}", [128, 512], BF16, scD) for i in range(2)]
        prl = [sb(f"dprl{i}", [128, 512], BF16, scD) for i in range(2)]
        mm_tok = []
        ev_tok = None
        st_tok = {}
        ld_tok = {}
        for phi in range(NF):
            slot = phi % 3
            wprev = ([mm_tok[phi - 3]] if phi >= 3 else []) + (
                env.mile.get("h2fm") or []) + (env.mile.get("S_wgu") or [])
            wl1 = env.dma(o, wgh[slot][:], dt["S_wg_hi"][:, phi * C:(phi + 1) * C],
                          waits=wprev)
            wl2 = env.dma(o, wgl[slot][:], dt["S_wg_lo"][:, phi * C:(phi + 1) * C],
                          waits=wprev)
            wl3 = env.dma(o, wuh[slot][:], dt["S_wu_hi"][:, phi * C:(phi + 1) * C],
                          eng="scalar", waits=wprev)
            wl4 = env.dma(o, wul[slot][:], dt["S_wu_lo"][:, phi * C:(phi + 1) * C],
                          eng="scalar", waits=wprev)
            wgs, wus = wl1, wl4
            wld = [wl1, wl2, wl3, wl4]
            w0 = wld + ([ev_tok] if ev_tok else [])
            for g in range(NG):
                for J in range(NJ):
                    env.mm3(o, P[J][:],
                            wgh[slot][:, g * 128:(g + 1) * 128],
                            wgl[slot][:, g * 128:(g + 1) * 128],
                            h2fm_hi[:, g * T + J * 512: g * T + J * 512 + 512],
                            h2fm_lo[:, g * T + J * 512: g * T + J * 512 + 512],
                            start=(g == 0), stop=(g == NG - 1),
                            waits=w0 if (g == 0 and J == 0) else (), inc=False)
                for J in range(NJ):
                    env.mm3(o, P[4 + J][:],
                            wuh[slot][:, g * 128:(g + 1) * 128],
                            wul[slot][:, g * 128:(g + 1) * 128],
                            h2fm_hi[:, g * T + J * 512: g * T + J * 512 + 512],
                            h2fm_lo[:, g * T + J * 512: g * T + J * 512 + 512],
                            start=(g == 0), stop=(g == NG - 1),
                            inc=(g == NG - 1 and J == NJ - 1))
            wmm = env.tok("Tn")
            mm_tok.append(wmm)
            for J in range(NJ):
                wsl = env.scn(o, lambda e, J=J: e.activation(out=sg[:], in_=P[J][:],
                                                             func=AF.Silu), waits=[wmm])
                wpr = env.ven(o, lambda e, J=J: e.tensor_tensor(pr[:], sg[:], P[4 + J][:],
                                                                OP.mult), waits=[wsl])
                sl2 = J % 2
                wst = (st_tok[(phi, J - 2)] if (phi, J - 2) in st_tok
                       else (st_tok[(phi - 1, J + 2)] if (phi - 1, J + 2) in st_tok
                             else []))
                whh, wll = env.split(o, prh[sl2][:], prl[sl2][:], pr[:], waits=[wpr] + wst)
                go = phi * T + J * 512
                wp_a = env.dma(o, dt["S_pr_hi"][:, go:go + 512], prh[sl2][:],
                               waits=[whh, wll], eng="gpsimd")
                wp_b = env.dma(o, dt["S_pr_lo"][:, go:go + 512], prl[sl2][:],
                               eng="gpsimd")
                st_tok[(phi, J)] = [wp_a, wp_b]
            ev_tok = env.tok("Vv")
        env.mile["S_pr"] = env.qtok("G")
        env.fence(o)
        o.emit(blk)
      _free(scD)
    _free(scCD)

    if "E1" in phases:
      scE = []
      woh = sb("e1woh", [128, 8 * C], BF16, scE)
      wol = sb("e1wol", [128, 8 * C], BF16, scE)
      wstg = sb("e1w", [128, C], F32, scE)
      ohJ_h = sb("e1ohh", [128, 8 * 512], BF16, scE)
      ohJ_l = sb("e1ohl", [128, 8 * 512], BF16, scE)
      aow = sb("e1aow", [128, C], F32, scE)
      aop = sb("e1aop", [128, C], F32, scE)
      with nc.Block("e1") as blk:
        o = Ops()
        ev_prev = None
        for half in range(2):
            spl_tok = []
            for gi in range(8):
                g = half * 8 + gi
                wws = [spl_tok[gi - 1]] if gi >= 1 else (
                    [env.tok("Tn")] if half == 1 else [])
                wl_ = env.dma(o, wstg[:], dt["Wo"][g * 128:(g + 1) * 128, :], waits=wws)
                _, wll = env.split(o, woh[:, gi * C:(gi + 1) * C],
                                   wol[:, gi * C:(gi + 1) * C], wstg[:], waits=[wl_])
                spl_tok.append(wll)
            wsplit = spl_tok[-1]
            for J in range(NJ):
                w1 = env.dma(o, ohJ_h[:].rearrange("p (g t) -> p g t", g=8),
                             dt["S_oh_hi"][:].rearrange("p (g t) -> p g t", g=NG)
                             [:, half * 8:half * 8 + 8, J * 512:(J + 1) * 512],
                             waits=env.mile.get("S_oh", []) + [env.tok("Tn")])
                w2 = env.dma(o, ohJ_l[:].rearrange("p (g t) -> p g t", g=8),
                             dt["S_oh_lo"][:].rearrange("p (g t) -> p g t", g=NG)
                             [:, half * 8:half * 8 + 8, J * 512:(J + 1) * 512])
                for ti in range(4):
                    tau = J * 4 + ti
                    for gi in range(8):
                        for et in range(4):
                            env.mm3(o, P[et][:],
                                    ohJ_h[:, gi * 512 + ti * 128: gi * 512 + ti * 128 + 128],
                                    ohJ_l[:, gi * 512 + ti * 128: gi * 512 + ti * 128 + 128],
                                    woh[:, gi * C + et * 512: gi * C + et * 512 + 512],
                                    wol[:, gi * C + et * 512: gi * C + et * 512 + 512],
                                    start=(gi == 0), stop=(gi == 7),
                                    waits=([w1, w2, wsplit] + ([ev_prev] if ev_prev else []))
                                    if (gi == 0 and et == 0) else (),
                                    inc=(gi == 7 and et == 3))
                    wmm = env.tok("Tn")
                    if half == 0:
                        for et in range(4):
                            env.ven(o, lambda e, et=et: e.tensor_copy(
                                aow[:, et * 512:(et + 1) * 512], P[et][:]),
                                waits=([wmm] + env.qtok("G")) if et == 0 else ())
                        wev = env.tok("Vv")
                        env.dma(o, dt["S_ao"][tau * 128:(tau + 1) * 128, :], aow[:],
                                waits=[wev], eng="gpsimd")
                    else:
                        wld = env.dma(o, aop[:], dt["S_ao"][tau * 128:(tau + 1) * 128, :],
                                      waits=env.mile.get("e1_half0", []) + [ev_prev])
                        for et in range(4):
                            env.ven(o, lambda e, et=et: e.tensor_tensor(
                                aow[:, et * 512:(et + 1) * 512], P[et][:],
                                aop[:, et * 512:(et + 1) * 512], OP.add),
                                waits=([wmm, wld] + env.qtok("G")) if et == 0 else ())
                        wev = env.tok("Vv")
                        env.dma(o, dt["S_ao"][tau * 128:(tau + 1) * 128, :], aow[:],
                                waits=[wev], eng="gpsimd")
                    ev_prev = env.tok("Vv")
            if half == 0:
                env.mile["e1_half0"] = env.qtok("G")
        env.mile["S_ao"] = env.qtok("G")
        env.fence(o)
        o.emit(blk)
      _free(scE)

    if "E2" in phases:
      scE = []
      NH3 = (15, 15, 14)
      wdh = sb("e2wdh", [128, 15 * C], BF16, scE)
      wdl = sb("e2wdl", [128, 15 * C], BF16, scE)
      wstg = sb("e2w", [128, C], F32, scE)
      prJ_h = sb("e2prh", [128, 15 * 512], BF16, scE)
      prJ_l = sb("e2prl", [128, 15 * 512], BF16, scE)
      zw = sb("e2zw", [128, C], F32, scE)
      zp = sb("e2zp", [128, C], F32, scE)
      aot = sb("e2ao", [128, C], F32, scE)
      xt_ = sb("e2x", [128, C], F32, scE)
      ow = sb("e2ow", [128, C], F32, scE)
      with nc.Block("e2") as blk:
        o = Ops()
        ev_prev = None
        fbase = 0
        for third in range(3):
            nf3 = NH3[third]
            spl_tok = []
            for fi in range(nf3):
                f = fbase + fi
                wws = [spl_tok[fi - 1]] if fi >= 1 else (
                    [env.tok("Tn")] if third > 0 else [])
                wl_ = env.dma(o, wstg[:], dt["w_down"][f * 128:(f + 1) * 128, :], waits=wws)
                _, wll = env.split(o, wdh[:, fi * C:(fi + 1) * C],
                                   wdl[:, fi * C:(fi + 1) * C], wstg[:], waits=[wl_])
                spl_tok.append(wll)
            wsplit = spl_tok[-1]
            for J in range(NJ):
                w1 = env.dma(o, prJ_h[:, 0:nf3 * 512].rearrange("p (g t) -> p g t", g=nf3),
                             dt["S_pr_hi"][:].rearrange("p (g t) -> p g t", g=NF)
                             [:, fbase:fbase + nf3, J * 512:(J + 1) * 512],
                             waits=env.mile.get("S_pr", []) + [env.tok("Tn")])
                w2 = env.dma(o, prJ_l[:, 0:nf3 * 512].rearrange("p (g t) -> p g t", g=nf3),
                             dt["S_pr_lo"][:].rearrange("p (g t) -> p g t", g=NF)
                             [:, fbase:fbase + nf3, J * 512:(J + 1) * 512])
                for ti in range(4):
                    tau = J * 4 + ti
                    for fi in range(nf3):
                        for et in range(4):
                            env.mm3(o, P[et][:],
                                    prJ_h[:, fi * 512 + ti * 128: fi * 512 + ti * 128 + 128],
                                    prJ_l[:, fi * 512 + ti * 128: fi * 512 + ti * 128 + 128],
                                    wdh[:, fi * C + et * 512: fi * C + et * 512 + 512],
                                    wdl[:, fi * C + et * 512: fi * C + et * 512 + 512],
                                    start=(fi == 0), stop=(fi == nf3 - 1),
                                    waits=([w1, w2, wsplit] + ([ev_prev] if ev_prev else []))
                                    if (fi == 0 and et == 0) else (),
                                    inc=(fi == nf3 - 1 and et == 3))
                    wmm = env.tok("Tn")
                    if third < 2:
                        wza = env.dma(o, zp[:], dt["S_za"][tau * 128:(tau + 1) * 128, :],
                                      waits=(env.mile.get("S_za", []) if third == 0
                                             else env.mile.get("e2_prev", [])) + [ev_prev])
                        for et in range(4):
                            env.ven(o, lambda e, et=et: e.tensor_tensor(
                                zw[:, et * 512:(et + 1) * 512], P[et][:],
                                zp[:, et * 512:(et + 1) * 512], OP.add),
                                waits=([wmm, wza] + env.qtok("G")) if et == 0 else ())
                        wev = env.tok("Vv")
                        env.dma(o, dt["S_za"][tau * 128:(tau + 1) * 128, :], zw[:],
                                waits=[wev], eng="gpsimd")
                    else:
                        wza = env.dma(o, zp[:], dt["S_za"][tau * 128:(tau + 1) * 128, :],
                                      waits=env.mile.get("e2_prev", []) + [ev_prev])
                        wao = env.dma(o, aot[:], dt["S_ao"][tau * 128:(tau + 1) * 128, :],
                                      waits=env.mile.get("S_ao", []))
                        wxl = env.dma(o, xt_[:], dt["x"][tau * 128:(tau + 1) * 128, :])
                        for et in range(4):
                            env.ven(o, lambda e, et=et: e.tensor_tensor(
                                zw[:, et * 512:(et + 1) * 512], P[et][:],
                                zp[:, et * 512:(et + 1) * 512], OP.add),
                                waits=([wmm, wza] + env.qtok("G")) if et == 0 else ())
                        env.ven(o, lambda e, tau=tau: e.tensor_scalar(
                            zw[:], zw[:], st["rt"][:, tau:tau + 1], None, OP.mult))
                        env.ven(o, lambda e: e.tensor_tensor(zw[:], zw[:], aot[:], OP.add),
                                waits=[wao])
                        env.ven(o, lambda e, tau=tau: e.tensor_scalar(
                            xt_[:], xt_[:], st["u"][:, tau:tau + 1], st["w"][:, tau:tau + 1],
                            OP.mult, OP.add), waits=[wxl])
                        wfin = env.ven(o, lambda e: e.tensor_tensor(ow[:], zw[:], xt_[:],
                                                                    OP.add))
                        env.dma(o, out[tau * 128:(tau + 1) * 128, :], ow[:],
                                waits=[wfin], eng="gpsimd")
                    ev_prev = env.tok("Vv")
            env.mile["e2_prev"] = env.qtok("G")
            fbase += nf3
        o.em("sync", lambda e: e.nop(), waits=env.qtok("G") + env.qtok("L") + env.qtok("A") + env.qtok("X"))
        env.fence(o)
        o.emit(blk)
      _free(scE)


def kernel(**inputs):
    consts = make_consts()
    nc = build_nc()
    def arr(k):
        return np.ascontiguousarray(np.asarray(inputs[k], dtype=np.float32))
    x = arr("L0_attn_x")
    base = {
        "Wq": arr("L0_attn_Wq"), "Wk": arr("L0_attn_Wk"), "Wv": arr("L0_attn_Wv"),
        "Wo": arr("L0_attn_Wo"),
        "head_scales": arr("L0_attn_head_scales").reshape(1, H),
        "w_router": arr("L0_ffn_w_router"),
        "b_router": arr("L0_ffn_b_router").reshape(1, 1),
        "norm1_w": arr("L0_ffn_norm1_w"), "wv": arr("L0_ffn_wv"), "bv": arr("L0_ffn_bv"),
        "wo": arr("L0_ffn_wo"), "norm2_w": arr("L0_ffn_norm2_w"),
        "w_gate": arr("L0_ffn_w_gate"), "w_up": arr("L0_ffn_w_up"),
        "w_down": arr("L0_ffn_w_down"),
    }
    base.update(consts)
    in_maps = [dict(base, x=np.ascontiguousarray(x[i])) for i in range(8)]
    res = run_bass_kernel_spmd(nc, in_maps, core_ids=list(range(8)))
    return np.stack([r["out"] for r in res.results], axis=0)


# version 08
# speedup vs baseline: 1.4370x; 1.0265x over previous
import math
import numpy as np
import concourse.bass as bass
import concourse.mybir as mybir
from concourse import library_config
from concourse.bass_utils import run_bass_kernel_spmd

F32 = mybir.dt.float32
BF16 = mybir.dt.bfloat16
AF = mybir.ActivationFunctionType
OP = mybir.AluOpType

T = 2048
C = 2048
FF = 5632
H = 8
NT = 16
NJ = 4
NG = 16
NF = 44
VROW = 528
EPS_LN = 1e-5
EPS_RMS = 1e-6

ALL_PHASES = ("P1", "A1", "A2", "C", "D", "E1", "E2")


def make_consts():
    import jax
    import jax.numpy as jnp
    with jax.default_device(jax.devices("cpu")[0]):
        inv = 1.0 / 10000.0 ** (jnp.arange(0, 256, 2, dtype=jnp.float32) / 256)
        f = jnp.arange(T, dtype=jnp.float32)[:, None] * inv[None, :]
        cos_hi = np.ascontiguousarray(np.asarray(jnp.cos(f)).T)
        sin_hi = np.ascontiguousarray(np.asarray(jnp.sin(f)).T)
    cos_lo = np.zeros_like(cos_hi)
    sin_lo = np.zeros_like(sin_hi)
    m = np.zeros((128, 4 * 512), dtype=np.float32)
    r = np.arange(128)[:, None]
    c = np.arange(512)[None, :]
    for k in range(4):
        m[:, k * 512:(k + 1) * 512] = (c >= k * 128 + r).astype(np.float32)
    trig = np.array([[sx, sy, sz] for sx in (1.0, -1.0) for sy in (1.0, -1.0)
                     for sz in (1.0, -1.0)], dtype=np.float64) / math.sqrt(3.0)
    dirs64 = np.ascontiguousarray(trig.T)
    return {"cos_hi": cos_hi, "cos_lo": cos_lo, "sin_hi": sin_hi, "sin_lo": sin_lo,
            "bandmask": m, "dirs64": dirs64.astype(np.float32),
            "dirs_lo64": (dirs64 - dirs64.astype(np.float32)).astype(np.float32)}


class Ops:
    def __init__(self):
        self.ops = []

    def em(self, eng, fn, inc=None, waits=()):
        self.ops.append((eng, fn, inc, tuple(w for w in waits if w is not None)))

    def emit(self, block):
        by_eng = {}
        for op in self.ops:
            by_eng.setdefault(op[0], []).append(op)
        for eng, items in by_eng.items():
            def body(e, items=items):
                seen = {}
                for _, fn, inc, waits in items:
                    for sem, val in waits:
                        if val > 0 and seen.get(id(sem), -1) < val:
                            e.wait_ge(sem, val)
                            seen[id(sem)] = val
                    inst = fn(e)
                    if inc is not None:
                        inst.then_inc(inc[0], inc[1])
            getattr(block, eng)(body)


class Env:
    DMA_POOL = {"L": 6, "A": 4, "G": 4, "X": 2}

    def __init__(self, nc):
        self.nc = nc
        self._ctx = []
        self.S = {}
        self.V = {}
        for nm in ("Tn", "Vv", "Sc"):
            cm = nc.semaphore(nm)
            self.S[nm] = cm.__enter__()
            self._ctx.append(cm)
            self.V[nm] = 0
        self.dpool = {}
        self.dcnt = {}
        self.didx = {}
        for nm, r in self.DMA_POOL.items():
            self.dpool[nm] = []
            self.dcnt[nm] = []
            self.didx[nm] = 0
            for i in range(r):
                cm = nc.semaphore(f"{nm}{i}")
                self.dpool[nm].append(cm.__enter__())
                self._ctx.append(cm)
                self.dcnt[nm].append(0)
        self.mile = {}

    def sb(self, name, shape, dtype, ctx=None):
        cm = self.nc.sbuf_tensor(name, shape, dtype)
        h = cm.__enter__()
        (ctx if ctx is not None else self._ctx).append(cm)
        return h

    def psum(self, name):
        cm = self.nc.psum_tensor(name, [128, 512], F32)
        h = cm.__enter__()
        self._ctx.append(cm)
        return h

    def tok(self, nm):
        return (self.S[nm], self.V[nm])

    def _dma_slot(self, nm):
        i = self.didx[nm] % len(self.dpool[nm])
        self.didx[nm] += 1
        sem = self.dpool[nm][i]
        prev = self.dcnt[nm][i]
        self.dcnt[nm][i] = prev + 16
        return sem, prev

    def dma(self, o, outap, inap, waits=(), eng="sync", slow=False):
        nm = {"gpsimd": "G", "scalar": "A"}.get(eng, "L")
        sem, prev = self._dma_slot(nm)
        o.em(eng, lambda e, a=outap, b=inap, s=slow: e.dma_start(
            out=a, in_=b, allow_slow_non_contiguous=s),
             inc=(sem, 16), waits=list(waits) + [(sem, prev)])
        return (sem, prev + 16)

    def dmaT(self, o, outap, inap, waits=()):
        sem, prev = self._dma_slot("X")
        o.em("sync", lambda e, a=outap, b=inap: e.dma_start_transpose(a, b),
             inc=(sem, 16), waits=list(waits) + [(sem, prev)])
        return (sem, prev + 16)

    def ven(self, o, fn, waits=()):
        prev = self.V["Vv"]
        self.V["Vv"] = prev + 1
        o.em("vector", fn, inc=(self.S["Vv"], 1),
             waits=list(waits) + [(self.S["Vv"], prev)])
        return (self.S["Vv"], prev + 1)

    def scn(self, o, fn, waits=()):
        prev = self.V["Sc"]
        self.V["Sc"] = prev + 1
        o.em("scalar", fn, inc=(self.S["Sc"], 1),
             waits=list(waits) + [(self.S["Sc"], prev)])
        return (self.S["Sc"], prev + 1)

    def gpn(self, o, fn, waits=()):
        if "Gp" not in self.S:
            cm = self.nc.semaphore("Gp")
            self.S["Gp"] = cm.__enter__()
            self._ctx.append(cm)
            self.V["Gp"] = 0
        prev = self.V["Gp"]
        self.V["Gp"] = prev + 1
        o.em("gpsimd", fn, inc=(self.S["Gp"], 1),
             waits=list(waits) + [(self.S["Gp"], prev)])
        return (self.S["Gp"], prev + 1)

    def ten(self, o, fn, waits=()):
        self.V["Tn"] += 1
        o.em("tensor", fn, inc=(self.S["Tn"], 1), waits=waits)
        return (self.S["Tn"], self.V["Tn"])

    def mm3(self, o, pb, lh, ll, rh, rl, start, stop, waits=(), inc=False):
        o.em("tensor", lambda e, a=pb, b=lh, c=rh, s=start:
             e.matmul(a, b, c, start=s, stop=False), waits=waits)
        o.em("tensor", lambda e, a=pb, b=lh, c=rl:
             e.matmul(a, b, c, start=False, stop=False))
        token = None
        if inc:
            self.V["Tn"] += 1
            token = (self.S["Tn"], self.V["Tn"])
        o.em("tensor", lambda e, a=pb, b=ll, c=rh, s=stop:
             e.matmul(a, b, c, start=False, stop=s),
             inc=(self.S["Tn"], 1) if inc else None)
        return token

    def qtok(self, nm):
        return [(s, c) for s, c in zip(self.dpool[nm], self.dcnt[nm]) if c > 0]

    def fence(self, o):
        o.em("sync", lambda e: e.nop(),
             waits=self.qtok("G") + self.qtok("L") + self.qtok("A") + self.qtok("X"))

    def split(self, o, hi, lo, src, waits=(), hi_on_sc=True):
        if hi_on_sc:
            w1 = self.scn(o, lambda e, a=hi, b=src: e.copy(a, b), waits=waits)
        else:
            w1 = self.ven(o, lambda e, a=hi, b=src: e.tensor_copy(a, b), waits=waits)
        w2 = self.ven(o, lambda e, a=lo, b=src, c=hi:
                      e.tensor_tensor(a, b, c, OP.subtract), waits=[w1])
        return w1, w2


def build_nc(phases=ALL_PHASES):
    nc = bass.Bass("TRN2", target_bir_lowering=False)
    dt = {}

    def din(name, shape):
        dt[name] = nc.dram_tensor(name, shape, F32, kind="ExternalInput").ap()
    din("x", [T, C])
    din("Wq", [C, C]); din("Wk", [C, 512]); din("Wv", [C, 512]); din("Wo", [C, C])
    din("head_scales", [1, H])
    din("w_router", [C]); din("b_router", [1, 1]); din("norm1_w", [C])
    din("wv", [C, C]); din("bv", [C]); din("wo", [C, C]); din("norm2_w", [C])
    din("w_gate", [C, FF]); din("w_up", [C, FF]); din("w_down", [FF, C])
    din("cos_hi", [128, T]); din("cos_lo", [128, T])
    din("sin_hi", [128, T]); din("sin_lo", [128, T])
    din("bandmask", [128, 4 * 512]); din("dirs64", [3, 8]); din("dirs_lo64", [3, 8])
    out = nc.dram_tensor("out", [T, C], F32, kind="ExternalOutput").ap()

    def scr(name, shape, dtype):
        dt[name] = nc.dram_tensor(name, shape, dtype, kind="Internal").ap()
    for nm in ("S_xn_hi", "S_xn_lo", "S_q_hi", "S_q_lo", "S_oh_hi", "S_oh_lo",
               "S_xn2_hi", "S_xn2_lo"):
        scr(nm, [128, NG * T], BF16)
    scr("S_wv_hi", [128, NG * C], BF16); scr("S_wv_lo", [128, NG * C], BF16)
    scr("S_wo_hi", [128, NG * C], BF16); scr("S_wo_lo", [128, NG * C], BF16)
    scr("S_pr_hi", [128, NF * T], BF16); scr("S_pr_lo", [128, NF * T], BF16)
    scr("S_wg_hi", [128, NF * C], BF16); scr("S_wg_lo", [128, NF * C], BF16)
    scr("S_wu_hi", [128, NF * C], BF16); scr("S_wu_lo", [128, NF * C], BF16)
    scr("S_za", [T, C], F32)
    scr("S_ao", [T, C], F32)

    env = Env(nc)
    try:
        _program(nc, env, dt, out, phases)
    finally:
        for cm in reversed(env._ctx):
            try:
                cm.__exit__(None, None, None)
            except Exception:
                pass
    return nc


def _free(scope):
    for cm in reversed(scope):
        cm.__exit__(None, None, None)
    scope.clear()


def _program(nc, env, dt, out, phases):
    S = env.S
    sb = env.sb

    st = {nm: sb("st_" + nm, [128, NT], F32) for nm in ("rs", "u", "w", "g1", "rt", "rs2")}
    ones_row = sb("ones_row", [1, 128], F32)
    hs_row = sb("hs_row", [1, H], F32)
    hs1m_row = sb("hs1m_row", [1, H], F32)
    hs_col = sb("hs_col", [128, H], F32)
    hs1m_col = sb("hs1m_col", [128, H], F32)
    brc = sb("brc", [128, 1], F32)
    brs = sb("brs", [1, 1], F32)
    n1w_fm = sb("n1w_fm", [128, NG], F32)
    n2w_fm = sb("n2w_fm", [128, NG], F32)
    bvf = sb("bvf", [128, NG], F32)
    sm = sb("smallA", [128, 16], F32)
    P = [env.psum(f"pb{i}") for i in range(8)]

    sp = []
    wrbc = sb("wrbc", [128, C], F32, sp)
    with nc.Block("p0") as blk:
        o = Ops()
        tmp = sb("p0tmp", [128, C], F32, sp)
        w1 = env.dma(o, hs_row[:], dt["head_scales"])
        wh1 = env.ven(o, lambda e: e.tensor_scalar(hs1m_row[:], hs_row[:], -1.0, 1.0,
                                                   OP.mult, OP.add), waits=[w1])
        won = env.ven(o, lambda e: e.memset(ones_row[:], 1.0))
        for h in range(H):
            wm = env.ten(o, lambda e, h=h: e.matmul(
                P[2][:, 0:1], ones_row[:], hs_row[0:1, h:h + 1],
                start=True, stop=True), waits=[wh1, won, env.tok("Vv")])
            env.ven(o, lambda e, h=h: e.tensor_copy(hs_col[:, h:h + 1], P[2][:, 0:1]),
                    waits=[wm])
            wm = env.ten(o, lambda e, h=h: e.matmul(
                P[3][:, 0:1], ones_row[:], hs1m_row[0:1, h:h + 1],
                start=True, stop=True), waits=[env.tok("Vv")])
            env.ven(o, lambda e, h=h: e.tensor_copy(hs1m_col[:, h:h + 1], P[3][:, 0:1]),
                    waits=[wm])
        wr = env.dma(o, tmp[0:1, 0:C], dt["w_router"].rearrange("(r c) -> r c", r=1))
        wb = env.dma(o, brs[:], dt["b_router"])
        for j in range(4):
            wm = env.ten(o, lambda e, j=j: e.matmul(
                P[0][:], ones_row[:], tmp[0:1, j * 512:(j + 1) * 512],
                start=True, stop=True), waits=[wr, env.tok("Vv")])
            env.ven(o, lambda e, j=j: e.tensor_copy(wrbc[:, j * 512:(j + 1) * 512], P[0][:]),
                    waits=[wm])
        wm = env.ten(o, lambda e: e.matmul(P[1][:, 0:1], ones_row[:], brs[:],
                                           start=True, stop=True),
                     waits=[wb, env.tok("Vv")])
        env.ven(o, lambda e: e.tensor_copy(brc[:], P[1][:, 0:1]), waits=[wm])
        env.dma(o, n1w_fm[:], dt["norm1_w"].rearrange("(g p) -> p g", p=128), slow=True)
        env.dma(o, n2w_fm[:], dt["norm2_w"].rearrange("(g p) -> p g", p=128), slow=True)
        env.dma(o, bvf[:], dt["bv"].rearrange("(g p) -> p g", p=128), slow=True)
        env.fence(o)
        o.emit(blk)

    env.mile["S_xnJ"] = {}
    if "P1" in phases:
      with nc.Block("p1") as blk:
        o = Ops()
        xb = [sb(f"p1x{i}", [128, C], F32, sp) for i in range(2)]
        xnb = sb("p1xn", [128, C], F32, sp)
        hib = [sb(f"p1hi{i}", [128, C], BF16, sp) for i in range(2)]
        lob = [sb(f"p1lo{i}", [128, C], BF16, sp) for i in range(2)]
        dump = sb("p1dump", [128, C], F32, sp)
        acc = sb("p1acc", [128, 4], F32, sp)
        stg_h = [sb(f"p1sh{i}", [128, NG * 512], BF16, sp) for i in range(2)]
        stg_l = [sb(f"p1sl{i}", [128, NG * 512], BF16, sp) for i in range(2)]
        xn_done = []
        tr_done = []
        stg_free = [None, None]
        for tau in range(NT):
            J = tau // 4
            bx = xb[tau % 2]
            wx = env.dma(o, bx[:], dt["x"][tau * 128:(tau + 1) * 128, :],
                         waits=[xn_done[tau - 2]] if tau >= 2 else ())
            ws1 = env.scn(o, lambda e, b=bx: e.activation(
                out=dump[:], in_=b[:], func=AF.Copy, accum_out=acc[:, 0:1]),
                waits=[wx, env.tok("Vv")])
            ws2 = env.scn(o, lambda e, b=bx: e.activation(
                out=dump[:], in_=b[:], func=AF.Square, accum_out=acc[:, 1:2]),
                waits=[ws1])
            mc, vc, vpe, sd, r0, t1, t2, t3, dn, g0 = (sm[:, k:k + 1] for k in range(10))
            env.ven(o, lambda e, a=mc: e.tensor_scalar(a, acc[:, 0:1], 1.0 / C, None, OP.mult),
                    waits=[ws1, ws2, env.tok("Sc")])
            env.ven(o, lambda e, a=vc: e.tensor_scalar(a, acc[:, 1:2], 1.0 / C, None, OP.mult))
            env.ven(o, lambda e, a=t3, b=mc, c=vc: e.scalar_tensor_tensor(
                a, b, b[:, 0:1], c, OP.mult, OP.subtract))
            env.ven(o, lambda e, a=vc, b=t3: e.tensor_scalar(a, b, -1.0, None, OP.mult))
            wv_ = env.ven(o, lambda e, a=vpe, b=vc: e.tensor_scalar(a, b, EPS_LN, None, OP.add))
            wsd = env.scn(o, lambda e, a=sd, b=vpe: e.sqrt(a, b), waits=[wv_])
            wrc = env.ven(o, lambda e, a=r0, b=sd: e.reciprocal(a, b), waits=[wsd])
            env.ven(o, lambda e, a=t1, b=r0: e.tensor_tensor(a, b, b, OP.mult), waits=[wrc])
            env.ven(o, lambda e, a=t2, b=t1, c=vpe: e.tensor_tensor(a, b, c, OP.mult))
            env.ven(o, lambda e, a=t3, b=t2: e.tensor_scalar(a, b, -0.5, 1.5, OP.mult, OP.add))
            rsc = st["rs"][:, tau:tau + 1]
            env.ven(o, lambda e, a=rsc, b=r0, c=t3: e.tensor_tensor(a, b, c, OP.mult))
            env.ven(o, lambda e, a=rsc, tau=tau: e.tensor_scalar(
                st["u"][:, tau:tau + 1], a, 1.0, None, OP.add))
            env.ven(o, lambda e, a=rsc, b=mc, tau=tau: e.scalar_tensor_tensor(
                st["w"][:, tau:tau + 1], b, -1.0, a, OP.mult, OP.mult))
            env.ven(o, lambda e, a=dn, b=vpe: e.reciprocal(a, b))
            env.ven(o, lambda e, a=t1, b=vc, c=dn: e.tensor_tensor(a, b, c, OP.mult))
            wv_ = env.ven(o, lambda e, a=t2, b=t1: e.tensor_scalar(a, b, EPS_RMS, None, OP.add))
            wsd = env.scn(o, lambda e, a=sd, b=t2: e.sqrt(a, b), waits=[wv_])
            wrc = env.ven(o, lambda e, a=g0, b=sd: e.reciprocal(a, b), waits=[wsd])
            env.ven(o, lambda e, a=t1, b=g0: e.tensor_tensor(a, b, b, OP.mult), waits=[wrc])
            env.ven(o, lambda e, a=t1, b=t2: e.tensor_tensor(a, a, b, OP.mult))
            env.ven(o, lambda e, a=t1: e.tensor_scalar(a, a, -0.5, 1.5, OP.mult, OP.add))
            env.ven(o, lambda e, a=g0, b=t1, tau=tau: e.tensor_tensor(
                st["g1"][:, tau:tau + 1], a, b, OP.mult))
            wxn = env.ven(o, lambda e, b=bx, a=rsc, tau=tau: e.tensor_scalar(
                xnb[:], b[:], a, st["w"][:, tau:tau + 1], OP.mult, OP.add))
            xn_done.append(wxn)
            wrt = env.ven(o, lambda e: e.scalar_tensor_tensor(
                dump[:], xnb[:], 1.0, wrbc[:], OP.mult, OP.mult, accum_out=acc[:, 2:3]),
                waits=[wxn])
            env.ven(o, lambda e: e.tensor_scalar(acc[:, 3:4], acc[:, 2:3], brc[:, 0:1], 0.0,
                                                 OP.add, OP.is_gt), waits=[wrt])
            env.ven(o, lambda e, tau=tau: e.tensor_copy(st["rt"][:, tau:tau + 1], acc[:, 3:4]))
            bh, bl = hib[tau % 2], lob[tau % 2]
            wprev = tr_done[tau - 2] if tau >= 2 else []
            whh, wll = env.split(o, bh[:], bl[:], xnb[:], waits=[wxn] + wprev)
            sj = J % 2
            dst_h = stg_h[sj][:].rearrange("p (g t) -> p g t", g=NG)[
                :, :, (tau % 4) * 128:(tau % 4) * 128 + 128]
            dst_l = stg_l[sj][:].rearrange("p (g t) -> p g t", g=NG)[
                :, :, (tau % 4) * 128:(tau % 4) * 128 + 128]
            wtw = stg_free[sj] if (tau % 4 == 0 and stg_free[sj]) else []
            wt1 = env.dmaT(o, dst_h, bh[:], waits=[whh, wll] + wtw)
            wt2 = env.dmaT(o, dst_l, bl[:], waits=[wll] + wtw)
            tr_done.append([wt1, wt2])
            if tau % 4 == 3:
                wsh = env.dma(o, dt["S_xn_hi"][:].rearrange("p (g t) -> p g t", g=NG)
                              [:, :, J * 512:(J + 1) * 512],
                              stg_h[sj][:].rearrange("p (g t) -> p g t", g=NG),
                              waits=env.qtok("X"), eng="gpsimd")
                wst = env.dma(o, dt["S_xn_lo"][:].rearrange("p (g t) -> p g t", g=NG)
                              [:, :, J * 512:(J + 1) * 512],
                              stg_l[sj][:].rearrange("p (g t) -> p g t", g=NG),
                              eng="gpsimd")
                toks = [wsh, wst]
                for u in range(4):
                    toks.append(env.dma(
                        o, dt["S_xn2_hi"][:, (J * 4 + u) * C:(J * 4 + u + 1) * C],
                        stg_h[sj][:].rearrange("p (g u t) -> p g u t", g=NG, u=4)
                        [:, :, u, :], eng="gpsimd"))
                    toks.append(env.dma(
                        o, dt["S_xn2_lo"][:, (J * 4 + u) * C:(J * 4 + u + 1) * C],
                        stg_l[sj][:].rearrange("p (g u t) -> p g u t", g=NG, u=4)
                        [:, :, u, :], eng="gpsimd"))
                stg_free[sj] = toks
                env.mile["S_xnJ"][J] = toks
        env.fence(o)
        o.emit(blk)
    _free(sp)

    scA = []
    K_hi = sb("K_hi", [128, 4 * T], BF16, scA)
    K_lo = sb("K_lo", [128, 4 * T], BF16, scA)
    V_hi = sb("V_hi", [128, NT * VROW], BF16, scA)
    V_lo = sb("V_lo", [128, NT * VROW], BF16, scA)
    mask4 = sb("mask4", [128, 4 * 512], F32, scA)
    dirs_h = sb("dirs_h", [3, 8], BF16, scA)
    dirs_l = sb("dirs_l", [3, 8], BF16, scA)
    cosT = sb("cosT", [128, T], F32, scA)
    sinT = sb("sinT", [128, T], F32, scA)

    if "A1" in phases:
      scA1 = []
      with nc.Block("a1") as blk:
        o = Ops()
        xnJ_h = sb("a1xnh", [128, NG * 512], BF16, scA1)
        xnJ_l = sb("a1xnl", [128, NG * 512], BF16, scA1)
        wvh = sb("a1wvh", [128, NG * 512], BF16, scA1)
        wvl = sb("a1wvl", [128, NG * 512], BF16, scA1)
        wsg = [sb(f"a1w{i}", [128, 1024], F32, scA1) for i in range(3)]
        wbh = [sb(f"a1bh{i}", [128, 1024], BF16, scA1) for i in range(3)]
        wbl = [sb(f"a1bl{i}", [128, 1024], BF16, scA1) for i in range(3)]
        rop = [sb(f"a1rop{i}", [128, 512], F32, scA1) for i in range(4)]
        qsth = sb("a1qh", [128, 2 * 512], BF16, scA1)
        qstl = sb("a1ql", [128, 2 * 512], BF16, scA1)
        env.dma(o, mask4[:], dt["bandmask"])
        w1 = env.dma(o, rop[0][:], dt["cos_hi"][:, 0:512])
        w1b = env.dma(o, rop[1][:], dt["cos_hi"][:, 512:1024])
        w1c = env.dma(o, rop[2][:], dt["cos_hi"][:, 1024:1536])
        w1d = env.dma(o, rop[3][:], dt["cos_hi"][:, 1536:2048])
        w2 = env.dma(o, cosT[:], dt["cos_lo"])
        wc = env.ven(o, lambda e: e.tensor_tensor(cosT[:, 0:512], rop[0][:],
                                                  cosT[:, 0:512], OP.add), waits=[w1, w2])
        env.ven(o, lambda e: e.tensor_tensor(cosT[:, 512:1024], rop[1][:],
                                             cosT[:, 512:1024], OP.add), waits=[w1b])
        env.ven(o, lambda e: e.tensor_tensor(cosT[:, 1024:1536], rop[2][:],
                                             cosT[:, 1024:1536], OP.add), waits=[w1c])
        wc = env.ven(o, lambda e: e.tensor_tensor(cosT[:, 1536:2048], rop[3][:],
                                                  cosT[:, 1536:2048], OP.add), waits=[w1d])
        w1 = env.dma(o, rop[0][:], dt["sin_hi"][:, 0:512], waits=[wc])
        w1b = env.dma(o, rop[1][:], dt["sin_hi"][:, 512:1024])
        w1c = env.dma(o, rop[2][:], dt["sin_hi"][:, 1024:1536])
        w1d = env.dma(o, rop[3][:], dt["sin_hi"][:, 1536:2048])
        w2 = env.dma(o, sinT[:], dt["sin_lo"])
        env.ven(o, lambda e: e.tensor_tensor(sinT[:, 0:512], rop[0][:],
                                             sinT[:, 0:512], OP.add), waits=[w1, w2])
        env.ven(o, lambda e: e.tensor_tensor(sinT[:, 512:1024], rop[1][:],
                                             sinT[:, 512:1024], OP.add), waits=[w1b])
        env.ven(o, lambda e: e.tensor_tensor(sinT[:, 1024:1536], rop[2][:],
                                             sinT[:, 1024:1536], OP.add), waits=[w1c])
        wsin = env.ven(o, lambda e: e.tensor_tensor(sinT[:, 1536:2048], rop[3][:],
                                                    sinT[:, 1536:2048], OP.add), waits=[w1d])
        wd1 = env.dma(o, wsg[0][0:3, 0:8], dt["dirs64"], waits=[wsin])
        wd2 = env.dma(o, wsg[0][0:3, 8:16], dt["dirs_lo64"])
        wdh = env.scn(o, lambda e: e.copy(dirs_h[:], wsg[0][0:3, 0:8]), waits=[wd1, wd2])
        env.ven(o, lambda e: e.tensor_tensor(rop[1][0:3, 0:8], wsg[0][0:3, 0:8],
                                             dirs_h[:], OP.subtract), waits=[wdh])
        env.ven(o, lambda e: e.tensor_tensor(rop[1][0:3, 0:8], rop[1][0:3, 0:8],
                                             wsg[0][0:3, 8:16], OP.add))
        wdirs = env.ven(o, lambda e: e.tensor_copy(dirs_l[:], rop[1][0:3, 0:8]))
        wv_spl = [wdirs]
        for g in range(NG):
            slot = g % 3
            wl_ = env.dma(o, wsg[slot][:, 0:512], dt["Wv"][g * 128:(g + 1) * 128, :],
                          waits=[wv_spl[-3]] if len(wv_spl) >= 3 else [wdirs])
            _, w2_ = env.split(o, wvh[:, g * 512:(g + 1) * 512],
                               wvl[:, g * 512:(g + 1) * 512], wsg[slot][:, 0:512],
                               waits=[wl_])
            wv_spl.append(w2_)
        for tau in range(NT):
            env.ven(o, lambda e, tau=tau: e.memset(
                V_hi[:, tau * VROW + 512:tau * VROW + 513], 1.0))
            env.ven(o, lambda e, tau=tau: e.memset(
                V_lo[:, tau * VROW + 512:tau * VROW + 513], 0.0))
        wones = env.tok("Vv")
        pfree = wones
        wring = wv_spl[-1]
        qst_free = None
        for J in range(NJ):
            wxn1 = env.dma(o, xnJ_h[:].rearrange("p (g t) -> p g t", g=NG),
                           dt["S_xn_hi"][:].rearrange("p (g t) -> p g t", g=NG)
                           [:, :, J * 512:(J + 1) * 512],
                           waits=(env.mile["S_xnJ"].get(J) or []) + [env.tok("Tn")])
            wxn2 = env.dma(o, xnJ_l[:].rearrange("p (g t) -> p g t", g=NG),
                           dt["S_xn_lo"][:].rearrange("p (g t) -> p g t", g=NG)
                           [:, :, J * 512:(J + 1) * 512])
            for ti in range(4):
                tau = J * 4 + ti
                for g in range(NG):
                    env.mm3(o, P[ti][:],
                            xnJ_h[:, g * 512 + ti * 128: g * 512 + ti * 128 + 128],
                            xnJ_l[:, g * 512 + ti * 128: g * 512 + ti * 128 + 128],
                            wvh[:, g * 512:(g + 1) * 512], wvl[:, g * 512:(g + 1) * 512],
                            start=(g == 0), stop=(g == NG - 1),
                            waits=[wxn1, wxn2, wv_spl[-1], wones, pfree] if g == 0 else (),
                            inc=(g == NG - 1))
            wvm = env.tok("Tn")
            for ti in range(4):
                tau = J * 4 + ti
                env.split(o, V_hi[:, tau * VROW:tau * VROW + 512],
                          V_lo[:, tau * VROW:tau * VROW + 512], P[ti][:], waits=[wvm])
            kmm = []
            for g in range(NG):
                slot = g % 3
                wl_ = env.dma(o, wsg[slot][:, 0:512], dt["Wk"][g * 128:(g + 1) * 128, :],
                              waits=[wring] if g == 0 else
                              ([kmm[g - 3]] if g >= 3 else ()))
                whh, wll = env.split(o, wbh[slot][:, 0:512], wbl[slot][:, 0:512],
                                     wsg[slot][:, 0:512], waits=[wl_])
                for kap in range(4):
                    env.mm3(o, P[4 + kap][:],
                            wbh[slot][:, kap * 128:(kap + 1) * 128],
                            wbl[slot][:, kap * 128:(kap + 1) * 128],
                            xnJ_h[:, g * 512:(g + 1) * 512],
                            xnJ_l[:, g * 512:(g + 1) * 512],
                            start=(g == 0), stop=(g == NG - 1),
                            waits=([wll] + ([pfree] if g == 0 else []))
                            if kap == 0 else (),
                            inc=(kap == 3))
                kmm.append(env.tok("Tn"))
            wkm = env.tok("Tn")
            for g2 in range(2):
                pa, pb_ = P[4 + 2 * g2], P[4 + 2 * g2 + 1]
                cs = cosT[:, J * 512:(J + 1) * 512]
                sn = sinT[:, J * 512:(J + 1) * 512]
                env.ven(o, lambda e, p=pa, c=cs: e.tensor_tensor(rop[0][:], p[:], c, OP.mult),
                        waits=[wkm])
                env.ven(o, lambda e, p=pb_, s=sn: e.tensor_tensor(rop[1][:], p[:], s, OP.mult))
                env.ven(o, lambda e, p=pb_, c=cs: e.tensor_tensor(rop[2][:], p[:], c, OP.mult))
                env.ven(o, lambda e, p=pa, s=sn: e.tensor_tensor(rop[3][:], p[:], s, OP.mult))
                wA = env.ven(o, lambda e: e.tensor_tensor(rop[0][:], rop[0][:], rop[1][:],
                                                          OP.subtract))
                wB = env.ven(o, lambda e: e.tensor_tensor(rop[2][:], rop[2][:], rop[3][:],
                                                          OP.add))
                kA = (2 * g2) * T + J * 512
                kB = (2 * g2 + 1) * T + J * 512
                env.split(o, K_hi[:, kA:kA + 512], K_lo[:, kA:kA + 512], rop[0][:], waits=[wA])
                env.split(o, K_hi[:, kB:kB + 512], K_lo[:, kB:kB + 512], rop[2][:], waits=[wB])
            pfree = env.tok("Vv")
            wring = wkm
            for hq in range(2):
                qmm = []
                for g in range(NG):
                    slot = g % 3
                    wl_ = env.dma(o, wsg[slot][:],
                                  dt["Wq"][g * 128:(g + 1) * 128,
                                           hq * 1024:(hq + 1) * 1024],
                                  waits=[wring] if g == 0 else
                                  ([qmm[g - 3]] if g >= 3 else ()))
                    whh, wll = env.split(o, wbh[slot][:], wbl[slot][:], wsg[slot][:],
                                         waits=[wl_])
                    for hh in range(4):
                        for kk in range(2):
                            env.mm3(o, P[hh * 2 + kk][:],
                                    wbh[slot][:, hh * 256 + kk * 128:
                                             hh * 256 + kk * 128 + 128],
                                    wbl[slot][:, hh * 256 + kk * 128:
                                             hh * 256 + kk * 128 + 128],
                                    xnJ_h[:, g * 512:(g + 1) * 512],
                                    xnJ_l[:, g * 512:(g + 1) * 512],
                                    start=(g == 0), stop=(g == NG - 1),
                                    waits=([wll] + ([pfree] if g == 0 else []))
                                    if (hh == 0 and kk == 0) else (),
                                    inc=(hh == 3 and kk == 1))
                    qmm.append(env.tok("Tn"))
                wqm = env.tok("Tn")
                for hh in range(4):
                    h = hq * 4 + hh
                    pa, pb_ = P[hh * 2], P[hh * 2 + 1]
                    cs = cosT[:, J * 512:(J + 1) * 512]
                    sn = sinT[:, J * 512:(J + 1) * 512]
                    wqf = qst_free if qst_free else []
                    env.ven(o, lambda e, p=pa, c=cs: e.tensor_tensor(rop[0][:], p[:], c,
                                                                     OP.mult),
                            waits=[wqm] + wqf)
                    env.ven(o, lambda e, p=pb_, s=sn: e.tensor_tensor(rop[1][:], p[:], s,
                                                                      OP.mult))
                    env.ven(o, lambda e, p=pb_, c=cs: e.tensor_tensor(rop[2][:], p[:], c,
                                                                      OP.mult))
                    env.ven(o, lambda e, p=pa, s=sn: e.tensor_tensor(rop[3][:], p[:], s,
                                                                     OP.mult))
                    wA = env.ven(o, lambda e: e.tensor_tensor(rop[0][:], rop[0][:], rop[1][:],
                                                              OP.subtract))
                    wB = env.ven(o, lambda e: e.tensor_tensor(rop[2][:], rop[2][:], rop[3][:],
                                                              OP.add))
                    env.split(o, qsth[:, 0:512], qstl[:, 0:512], rop[0][:], waits=[wA])
                    _, wsp = env.split(o, qsth[:, 512:1024], qstl[:, 512:1024], rop[2][:],
                                       waits=[wB])
                    wq_a = env.dma(o, dt["S_q_hi"][:].rearrange("p (g t) -> p g t", g=NG)
                                   [:, 2 * h:2 * h + 2, J * 512:(J + 1) * 512],
                                   qsth[:].rearrange("p (g t) -> p g t", g=2),
                                   waits=[wsp], eng="gpsimd")
                    wq_b = env.dma(o, dt["S_q_lo"][:].rearrange("p (g t) -> p g t", g=NG)
                                   [:, 2 * h:2 * h + 2, J * 512:(J + 1) * 512],
                                   qstl[:].rearrange("p (g t) -> p g t", g=2),
                                   eng="gpsimd")
                    qst_free = [wq_a, wq_b]
                pfree = env.tok("Vv")
                wring = wqm
        env.mile["S_q"] = qst_free
        env.fence(o)
        o.emit(blk)
      _free(scA1)

    if "A2" in phases:
      scA2 = []
      with nc.Block("a2") as blk:
        o = Ops()
        qb_h = [sb("a2qh0", [128, 2 * T], BF16, scA2)]
        qb_l = [sb("a2ql0", [128, 2 * T], BF16, scA2)]
        eb = [sb(f"a2eb{i}", [128, 512], F32, scA2) for i in range(4)]
        ebm = [sb(f"a2ebm{i}", [128, 512], F32, scA2) for i in range(4)]
        e_hi = [sb(f"a2eh{i}", [128, 512], BF16, scA2) for i in range(4)]
        e_lo = [sb(f"a2el{i}", [128, 512], BF16, scA2) for i in range(4)]
        of32 = [sb(f"a2of{i}", [128, 512], F32, scA2) for i in range(2)]
        og32 = [sb(f"a2og{i}", [128, 512], F32, scA2) for i in range(2)]
        ohf = sb("a2ohf", [128, 512], F32, scA2)
        oh_hi = sb("a2ohh", [128, 512], BF16, scA2)
        oh_lo = sb("a2ohl", [128, 512], BF16, scA2)
        kp_row = sb("a2kp", [1, T], F32, scA2)
        qp_row = sb("a2qp", [1, T], F32, scA2)
        rr = sb("a2rr", [1, 512], F32, scA2)
        rrg = sb("a2rrg", [1, 512], F32, scA2)
        pS = [P[0], P[1], P[2], P[3]]; pO = [P[4], P[5]]; pR = P[6]; pB = P[7]
        wstf = [sb(f"a2wsf{i}", [128, C], F32, scA2) for i in range(2)]
        wsth = [sb(f"a2wsh{i}", [128, C], BF16, scA2) for i in range(2)]
        wstl = [sb(f"a2wsl{i}", [128, C], BF16, scA2) for i in range(2)]
        wstage_state = {"piece": 0, "stage": 0, "slice": 0, "ld": {}, "st": {},
                        "wld": [], "wsc": None}

        def stage_tick(budget):
            st_ = wstage_state
            while budget > 0:
                k = st_["piece"]
                if k >= 2 * NF:
                    return
                phi, which = k // 2, k % 2
                slot = k % 2
                if st_["stage"] == 0:
                    wsrc = dt["w_gate"] if which == 0 else dt["w_up"]
                    for q4 in range(4):
                        ap = wsrc.rearrange("(g p) f -> p g f", p=128)[
                            :, q4 * 4:(q4 + 1) * 4, phi * 128:(phi + 1) * 128]
                        st_["wld"].append(env.dma(
                            o, wstf[slot][:].rearrange("p (g f) -> p g f", g=NG)
                            [:, q4 * 4:(q4 + 1) * 4, :], ap,
                            waits=(st_["ld"].get(slot, []) if q4 == 0 else []),
                            eng="scalar"))
                    st_["stage"] = 1
                    st_["slice"] = 0
                    budget -= 1
                elif st_["stage"] == 1:
                    g = st_["slice"]
                    env.scn(o, lambda e, s=slot, g=g: e.activation(
                        out=wsth[s][:, g * 128:(g + 1) * 128],
                        in_=wstf[s][:, g * 128:(g + 1) * 128],
                        func=AF.Copy, scale=n2w_fm[:, g:g + 1]),
                        waits=(st_["wld"] + st_["st"].get(slot, [])) if g == 0 else ())
                    st_["slice"] += 1
                    budget -= 1
                    if st_["slice"] == NG:
                        st_["wsc"] = env.tok("Sc")
                        st_["stage"] = 2
                        st_["slice"] = 0
                elif st_["stage"] == 2:
                    g = st_["slice"]
                    wl2 = env.ven(o, lambda e, s=slot, g=g: e.scalar_tensor_tensor(
                        wstl[s][:, g * 128:(g + 1) * 128],
                        wstf[s][:, g * 128:(g + 1) * 128],
                        n2w_fm[:, g:g + 1], wsth[s][:, g * 128:(g + 1) * 128],
                        OP.mult, OP.subtract),
                        waits=[st_["wsc"]] if g == 0 else ())
                    st_["slice"] += 1
                    budget -= 1
                    if st_["slice"] == NG:
                        st_["ld"][slot] = [wl2]
                        dsth = dt["S_wg_hi"] if which == 0 else dt["S_wu_hi"]
                        dstl = dt["S_wg_lo"] if which == 0 else dt["S_wu_lo"]
                        s1 = env.dma(o, dsth[:, phi * C:(phi + 1) * C], wsth[slot][:],
                                     waits=[wl2], eng="gpsimd")
                        s2 = env.dma(o, dstl[:, phi * C:(phi + 1) * C], wstl[slot][:],
                                     eng="gpsimd")
                        st_["st"][slot] = [s1, s2]
                        st_["wld"] = []
                        st_["piece"] = k + 1
                        st_["stage"] = 0

        q_free = [None]
        comb_prev = None
        oh_store_prev = None
        prev_exps = []
        gp_free = []
        pv_gate = []
        for h in range(H):
            g2 = h // 4
            qh, ql = qb_h[0], qb_l[0]
            wq1 = env.dma(o, qh[:], dt["S_q_hi"][:, (2 * h) * T:(2 * h + 2) * T],
                          waits=(env.mile.get("S_q") or []) + (
                              [q_free[0]] if q_free[0] else []))
            wq2 = env.dma(o, ql[:], dt["S_q_lo"][:, (2 * h) * T:(2 * h + 2) * T])
            whd = env.tok("Vv")
            kq_banks = [pS[0], pS[1], pS[2], pS[3]]
            qp_banks = [pO[0], pO[1], pR, pB]
            for J in range(NJ):
                b0 = (2 * g2) * T + J * 512
                pk = kq_banks[J]
                o.em("tensor", lambda e, b=b0, h=h, p=pk: e.matmul(
                    p[0:1, :], dirs_h[:, h:h + 1], K_hi[0:3, b:b + 512],
                    start=True, stop=False),
                    waits=[whd] + list(prev_exps))
                o.em("tensor", lambda e, b=b0, h=h, p=pk: e.matmul(
                    p[0:1, :], dirs_h[:, h:h + 1], K_lo[0:3, b:b + 512],
                    start=False, stop=False))
                env.ten(o, lambda e, b=b0, h=h, p=pk: e.matmul(
                    p[0:1, :], dirs_l[:, h:h + 1], K_hi[0:3, b:b + 512],
                    start=False, stop=True))
            wkm_all = env.tok("Tn")
            for J in range(NJ):
                env.ven(o, lambda e, J=J, p=kq_banks[J]: e.tensor_copy(
                    kp_row[0:1, J * 512:(J + 1) * 512], p[0:1, :]),
                    waits=[wkm_all] if J == 0 else ())
            wkp_done = env.tok("Vv")
            for J in range(NJ):
                pq = qp_banks[J]
                o.em("tensor", lambda e, J=J, q=qh, h=h, p=pq: e.matmul(
                    p[0:1, :], dirs_h[:, h:h + 1], q[0:3, J * 512:J * 512 + 512],
                    start=True, stop=False),
                    waits=[wq1, wq2] + pv_gate + gp_free
                    + ([comb_prev] if comb_prev else []))
                o.em("tensor", lambda e, J=J, q=ql, h=h, p=pq: e.matmul(
                    p[0:1, :], dirs_h[:, h:h + 1], q[0:3, J * 512:J * 512 + 512],
                    start=False, stop=False))
                env.ten(o, lambda e, J=J, q=qh, h=h, p=pq: e.matmul(
                    p[0:1, :], dirs_l[:, h:h + 1], q[0:3, J * 512:J * 512 + 512],
                    start=False, stop=True))
            wqm_all = env.tok("Tn")
            for J in range(NJ):
                env.ven(o, lambda e, J=J, p=qp_banks[J]: e.tensor_copy(
                    qp_row[0:1, J * 512:(J + 1) * 512], p[0:1, :]),
                    waits=[wqm_all] if J == 0 else ())
            w_rows = env.tok("Vv")
            for j in range(NJ):
                ni = 4 * j + 4
                evac_toks = []
                for sweep in range(2):
                    exp_tok = []
                    split_tok = []
                    pv_tok = []
                    pend = []
                    LEAD = 3
                    for it in range(ni + LEAD):
                      if it < ni:
                        i = it
                        ps = pS[i % 4]
                        wbank = [exp_tok[i - 4]] if i >= 4 else (
                            list(prev_exps) + [w_rows])
                        if sweep == 0:
                            for dc in range(2):
                                env.mm3(o, ps[:],
                                        K_hi[:, (2 * g2 + dc) * T + i * 128:
                                             (2 * g2 + dc) * T + i * 128 + 128],
                                        K_lo[:, (2 * g2 + dc) * T + i * 128:
                                             (2 * g2 + dc) * T + i * 128 + 128],
                                        qh[:, dc * T + j * 512:dc * T + j * 512 + 512],
                                        ql[:, dc * T + j * 512:dc * T + j * 512 + 512],
                                        start=(dc == 0), stop=(dc == 1),
                                        waits=([wq1, wq2] + wbank) if dc == 0 else (),
                                        inc=(dc == 1))
                        else:
                            env.ten(o, lambda e, i=i, j=j, p=ps: e.matmul(
                                p[:], kp_row[0:1, i * 128:(i + 1) * 128],
                                qp_row[0:1, j * 512:(j + 1) * 512], start=True, stop=True),
                                waits=[w_rows] + wbank)
                        wmm = env.tok("Tn")
                        scl = 0.0625 if sweep == 0 else 1.0
                        wex = env.scn(o, lambda e, p=ps, b=eb[i % 4], s=scl: e.activation(
                            out=b[:], in_=p[:], func=AF.Exp, scale=s),
                            waits=[wmm] + ([split_tok[i - 4]] if i >= 4 else []))
                        exp_tok.append(wex)
                        ksl = i - 4 * j
                        if 0 <= ksl <= 3:
                            wmk = env.ven(o, lambda e, k=ksl, a=ebm[i % 4], b=eb[i % 4]:
                                          e.tensor_tensor(a[:], b[:],
                                                          mask4[:, k * 512:(k + 1) * 512],
                                                          OP.mult), waits=[wex])
                            esrc = ebm[i % 4]
                        else:
                            wmk = wex
                            esrc = eb[i % 4]
                        slot = i % 4
                        whh, wll = env.split(o, e_hi[slot][:], e_lo[slot][:], esrc[:],
                                             waits=[wmk] + (
                                                 [pv_tok[i - 4]] if i >= 4 else []))
                        split_tok.append(wll)
                        pend.append((i, slot, whh, wll))
                        stage_tick(5)
                      if it >= LEAD and pend:
                        i, slot, whh, wll = pend.pop(0)
                        base = i * VROW + g2 * 256
                        for dc in range(2):
                            env.mm3(o, pO[dc][:],
                                    V_hi[:, base + dc * 128: base + dc * 128 + 128],
                                    V_lo[:, base + dc * 128: base + dc * 128 + 128],
                                    e_hi[slot][:], e_lo[slot][:],
                                    start=(i == 0), stop=(i == ni - 1),
                                    waits=([whh, wll, w_rows] + pv_gate) if i == 0
                                    else [whh, wll],
                                    inc=False)
                        o.em("tensor", lambda e, i=i, s=slot: e.matmul(
                            pR[0:1, :], V_hi[:, i * VROW + 512:i * VROW + 513],
                            e_hi[s][:], start=(i == 0), stop=False),
                            waits=([whh, w_rows] + pv_gate) if i == 0 else [whh])
                        pv_tok.append(env.ten(
                            o, lambda e, i=i, s=slot, last=(i == ni - 1): e.matmul(
                                pR[0:1, :], V_hi[:, i * VROW + 512:i * VROW + 513],
                                e_lo[s][:], start=False, stop=last), waits=[wll]))
                    wpv = env.tok("Tn")
                    dsts = of32 if sweep == 0 else og32
                    rdst = rr if sweep == 0 else rrg
                    wcb = [comb_prev] if comb_prev else []
                    ev0 = env.scn(o, lambda e, d=dsts: e.copy(d[0][:], pO[0][:]),
                                  waits=[wpv] + wcb)
                    ev1 = env.scn(o, lambda e, d=dsts: e.copy(d[1][:], pO[1][:]))
                    evr = env.ven(o, lambda e, a=rdst: e.reciprocal(a[:], pR[0:1, :]),
                                  waits=[wpv] + wcb)
                    evac_toks += [ev0, ev1, evr]
                    pv_gate = [ev0, ev1, evr]
                    prev_exps = exp_tok[-4:]
                wba = env.ten(o, lambda e: e.matmul(pB[:], ones_row[:], rr[0:1, :],
                                                    start=True, stop=True),
                              waits=[evac_toks[2]] + gp_free)
                wbb = env.ten(o, lambda e: e.matmul(pR[:], ones_row[:], rrg[0:1, :],
                                                    start=True, stop=True),
                              waits=[evac_toks[5]])
                wt1a = env.ven(o, lambda e, h=h: e.scalar_tensor_tensor(
                    of32[0][:], of32[0][:], hs1m_col[:, h:h + 1], pB[:], OP.mult, OP.mult),
                    waits=[wba, evac_toks[0], evac_toks[1], evac_toks[3], evac_toks[4]])
                wt1b = env.ven(o, lambda e, h=h: e.scalar_tensor_tensor(
                    of32[1][:], of32[1][:], hs1m_col[:, h:h + 1], pB[:], OP.mult, OP.mult))
                for dc in range(2):
                    wt2 = env.ven(o, lambda e, dc=dc, h=h: e.scalar_tensor_tensor(
                        og32[dc][:], og32[dc][:], hs_col[:, h:h + 1], pR[:],
                        OP.mult, OP.mult), waits=[wbb])
                    wt3 = env.ven(o, lambda e, dc=dc: e.tensor_tensor(
                        ohf[:], of32[dc][:], og32[dc][:], OP.add))
                    whh, wll = env.split(o, oh_hi[:], oh_lo[:], ohf[:],
                                         waits=[wt3] + (oh_store_prev or []))
                    go = (2 * h + dc) * T + j * 512
                    wo_a = env.dma(o, dt["S_oh_hi"][:, go:go + 512], oh_hi[:],
                                   waits=[whh, wll], eng="gpsimd")
                    wo_b = env.dma(o, dt["S_oh_lo"][:, go:go + 512], oh_lo[:],
                                   eng="gpsimd")
                    oh_store_prev = [wo_a, wo_b]
                gp_free = [env.tok("Vv")]
                pv_gate = [evac_toks[3], evac_toks[4], env.tok("Vv")]
                comb_prev = env.tok("Vv")
            q_free[0] = env.tok("Tn")
        while wstage_state["piece"] < 2 * NF:
            stage_tick(64)
        env.mile["S_oh"] = env.qtok("G")
        env.mile["S_wgu"] = env.qtok("G")
        env.fence(o)
        o.emit(blk)
      _free(scA2)
    _free(scA)

    scCD = []
    h2fm_hi = sb("h2fm_hi", [128, NG * T], BF16, scCD)
    h2fm_lo = sb("h2fm_lo", [128, NG * T], BF16, scCD)
    if "C" in phases:
      scC = []
      with nc.Block("c") as blk:
        o = Ops()
        wbh = [sb(f"cwbh{i}", [128, C], BF16, scC) for i in range(2)]
        wbl = [sb(f"cwbl{i}", [128, C], BF16, scC) for i in range(2)]
        xnP_h = sb("cxnh", [128, NG * 256], BF16, scC)
        xnP_l = sb("cxnl", [128, NG * 256], BF16, scC)
        vfm_hi = sb("cvfh", [128, NG * 256], BF16, scC)
        vfm_lo = sb("cvfl", [128, NG * 256], BF16, scC)
        zaw = [sb(f"czaw{i}", [128, C], F32, scC) for i in range(2)]
        xcb = sb("cxb", [128, C], F32, scC)
        c0_bc = sb("c0_bc", [128, C], BF16, scC)
        acc = sb("cacc", [128, 2], F32, scC)
        prev = {"lo": None, "mm": None, "st": None}
        for g in range(NG):
            slot = g % 2
            wl_ = env.dma(o, zaw[0][:], dt["wo"][g * 128:(g + 1) * 128, :],
                          waits=[prev["lo"], prev["mm"]])
            whh, wll = env.split(o, wbh[slot][:], wbl[slot][:], zaw[0][:],
                                 waits=[wl_] + (prev["st"] or []))
            for dtl in range(4):
                env.ten(o, lambda e, g=g, d=dtl: e.matmul(
                    P[4 + d][0:1, :], bvf[:, g:g + 1], zaw[0][:, d * 512:(d + 1) * 512],
                    start=(g == 0), stop=(g == NG - 1)), waits=[wl_])
            prev["mm"] = env.tok("Tn")
            prev["lo"] = wll
            ws_a = env.dma(o, dt["S_wo_hi"][:, g * C:(g + 1) * C], wbh[slot][:],
                           waits=[whh, wll], eng="gpsimd")
            ws_b = env.dma(o, dt["S_wo_lo"][:, g * C:(g + 1) * C], wbl[slot][:],
                           eng="gpsimd")
            prev["st"] = [ws_a, ws_b]
        wc0 = env.tok("Tn")
        for dtl in range(4):
            env.ven(o, lambda e, d=dtl: e.tensor_copy(
                zaw[0][0:1, d * 512:(d + 1) * 512], P[4 + d][0:1, :]), waits=[wc0])
        wrow = env.tok("Vv")
        for dtl in range(4):
            wm = env.ten(o, lambda e, d=dtl: e.matmul(
                P[4 + d][:], ones_row[:], zaw[0][0:1, d * 512:(d + 1) * 512],
                start=True, stop=True), waits=[wrow])
            env.ven(o, lambda e, d=dtl: e.tensor_copy(c0_bc[:, d * 512:(d + 1) * 512],
                                                      P[4 + d][:]), waits=[wm])
        wc0bc = env.tok("Vv")
        for g in range(NG):
            slot = g % 2
            wl_ = env.dma(o, zaw[0][:], dt["wv"][g * 128:(g + 1) * 128, :],
                          waits=[prev["lo"], wc0bc if g == 0 else None])
            wsc = env.ven(o, lambda e, g=g: e.tensor_scalar(
                zaw[0][:], zaw[0][:], n1w_fm[:, g:g + 1], None, OP.mult), waits=[wl_])
            whh, wll = env.split(o, wbh[slot][:], wbl[slot][:], zaw[0][:],
                                 waits=[wsc] + (prev["st"] or []))
            prev["lo"] = wll
            ws_a = env.dma(o, dt["S_wv_hi"][:, g * C:(g + 1) * C], wbh[slot][:],
                           waits=[whh, wll], eng="gpsimd")
            ws_b = env.dma(o, dt["S_wv_lo"][:, g * C:(g + 1) * C], wbl[slot][:],
                           eng="gpsimd")
            prev["st"] = [ws_a, ws_b]
        env.mile["S_wv"] = list(prev["st"])
        h2_tr = {}
        pfree = wc0bc
        pair_wop = env.tok("Tn")
        za_store = {}
        for tp in range(8):
            t0 = 2 * tp
            wxn1 = env.dma(o, xnP_h[:], dt["S_xn2_hi"][:, t0 * C:(t0 + 2) * C],
                           waits=(env.mile["S_xnJ"].get(t0 // 4) or [])
                           + (env.mile.get("S_wv") or []) + [env.tok("Tn")]
                           + (h2_tr.get(t0 - 1) or []))
            wxn2 = env.dma(o, xnP_l[:], dt["S_xn2_lo"][:, t0 * C:(t0 + 2) * C],
                           waits=(h2_tr.get(t0 - 1) or []))
            vsp = []
            mmg = {}
            for rnd in range(2):
                for g in range(NG):
                    k = rnd * NG + g
                    slot = k % 2
                    ww = [mmg[k - 2]] if k >= 2 else [pair_wop]
                    co = g * C + rnd * 1024
                    wl1 = env.dma(o, wbh[slot][:, 0:1024], dt["S_wv_hi"][:, co:co + 1024],
                                  waits=ww + (env.mile.get("S_wv") or []))
                    wl2 = env.dma(o, wbl[slot][:, 0:1024], dt["S_wv_lo"][:, co:co + 1024],
                                  waits=ww + (env.mile.get("S_wv") or []), eng="scalar")
                    for e8 in range(8):
                        env.mm3(o, P[e8][:, 0:256],
                                wbh[slot][:, e8 * 128:(e8 + 1) * 128],
                                wbl[slot][:, e8 * 128:(e8 + 1) * 128],
                                xnP_h[:].rearrange("p (u g t) -> p g u t", u=2, g=NG)
                                [:, g, :, :],
                                xnP_l[:].rearrange("p (u g t) -> p g u t", u=2, g=NG)
                                [:, g, :, :],
                                start=(g == 0), stop=(g == NG - 1),
                                waits=([wl1, wl2]
                                       + ([wxn1, wxn2] if g == 0 else [])
                                       + ([vsp[-1]] if (g == 0 and rnd == 1) else []))
                                if e8 == 0 else (),
                                inc=(e8 == 7))
                    mmg[k] = env.tok("Tn")
                wvm = env.tok("Tn")
                for e8 in range(8):
                    eps = rnd * 8 + e8
                    _, wsp = env.split(o, vfm_hi[:, eps * 256:(eps + 1) * 256],
                                       vfm_lo[:, eps * 256:(eps + 1) * 256],
                                       P[e8][:, 0:256],
                                       waits=[wvm] if e8 == 0 else ())
                    vsp.append(wsp)
            vdone = vsp[-1]
            mmg = {}
            for eps in range(NG):
                slot = eps % 2
                ww = [mmg[eps - 2]] if eps >= 2 else [env.tok("Tn")]
                wl1 = env.dma(o, wbh[slot][:], dt["S_wo_hi"][:, eps * C:(eps + 1) * C],
                              waits=ww)
                wl2 = env.dma(o, wbl[slot][:], dt["S_wo_lo"][:, eps * C:(eps + 1) * C],
                              waits=ww, eng="scalar")
                for ti in range(2):
                    for dtl in range(4):
                        env.mm3(o, P[ti * 4 + dtl][:],
                                vfm_hi[:, eps * 256 + ti * 128: eps * 256 + ti * 128 + 128],
                                vfm_lo[:, eps * 256 + ti * 128: eps * 256 + ti * 128 + 128],
                                wbh[slot][:, dtl * 512:(dtl + 1) * 512],
                                wbl[slot][:, dtl * 512:(dtl + 1) * 512],
                                start=(eps == 0), stop=(eps == NG - 1),
                                waits=([wl1, wl2] + ([vdone] if eps == 0 else []))
                                if (ti == 0 and dtl == 0) else (),
                                inc=(ti == 1 and dtl == 3))
                mmg[eps] = env.tok("Tn")
            wop = env.tok("Tn")
            pair_wop = wop
            wza_tok = {}
            for ti in range(2):
                tau = t0 + ti
                for dtl in range(4):
                    env.ven(o, lambda e, p=P[ti * 4 + dtl], d=dtl, tau=tau, ti=ti:
                            e.scalar_tensor_tensor(zaw[ti][:, d * 512:(d + 1) * 512], p[:],
                                                   st["g1"][:, tau:tau + 1],
                                                   c0_bc[:, d * 512:(d + 1) * 512],
                                                   OP.mult, OP.add),
                            waits=([wop, wc0bc] + (env.mile.get("S_wv") or [])
                                   + ([za_store.get(tau - 1 - ti)] if tau - 1 - ti >= 0
                                      else []))
                            if dtl == 0 else ())
                wza_tok[ti] = env.tok("Vv")
                za_store[tau] = env.dma(o, dt["S_za"][tau * 128:(tau + 1) * 128, :],
                                        zaw[ti][:], waits=[wza_tok[ti]], eng="gpsimd")
            pfree = wza_tok[1]
            prev_split = None
            for ti in range(2):
                tau = t0 + ti
                wxl = env.dma(o, xcb[:], dt["x"][tau * 128:(tau + 1) * 128, :],
                              waits=[wza_tok[1]] if ti == 0 else [prev_split])
                wxa = env.ven(o, lambda e, tau=tau: e.tensor_scalar(
                    xcb[:], xcb[:], st["rs"][:, tau:tau + 1], st["w"][:, tau:tau + 1],
                    OP.mult, OP.add), waits=[wxl])
                wh_ = env.ven(o, lambda e, ti=ti: e.tensor_tensor(xcb[:], zaw[ti][:], xcb[:],
                                                                  OP.add), waits=[wxa])
                wsq = env.scn(o, lambda e, ti=ti: e.activation(
                    out=zaw[ti][:], in_=xcb[:], func=AF.Square, accum_out=acc[:, 0:1]),
                    waits=[wh_, za_store[tau]])
                vpe2, sd2, r02, t12, t22 = (sm[:, k:k + 1] for k in range(10, 15))
                env.ven(o, lambda e, a=vpe2: e.tensor_scalar(
                    a, acc[:, 0:1], 1.0 / C, EPS_RMS, OP.mult, OP.add), waits=[wsq])
                wv_ = env.tok("Vv")
                wsd = env.scn(o, lambda e, a=sd2, b=vpe2: e.sqrt(a, b), waits=[wv_])
                wrc = env.ven(o, lambda e, a=r02, b=sd2: e.reciprocal(a, b), waits=[wsd])
                env.ven(o, lambda e, a=t12, b=r02: e.tensor_tensor(a, b, b, OP.mult),
                        waits=[wrc])
                env.ven(o, lambda e, a=t22, b=t12, c=vpe2: e.tensor_tensor(a, b, c, OP.mult))
                env.ven(o, lambda e, a=t22: e.tensor_scalar(a, a, -0.5, 1.5, OP.mult, OP.add))
                env.ven(o, lambda e, a=r02, b=t22, tau=tau: e.tensor_tensor(
                    st["rs2"][:, tau:tau + 1], a, b, OP.mult))
                wh2 = env.ven(o, lambda e, tau=tau: e.tensor_scalar(
                    xcb[:], xcb[:], st["rs2"][:, tau:tau + 1], None, OP.mult))
                wtp = h2_tr[tau - 1] if tau >= 1 else []
                whh, wll = env.split(o, xnP_h[:, 0:C], xnP_l[:, 0:C], xcb[:],
                                     waits=[wh2] + wtp)
                prev_split = wll
                dsth = h2fm_hi[:].rearrange("p (g t) -> p g t", g=NG)[
                    :, :, tau * 128:(tau + 1) * 128]
                dstl = h2fm_lo[:].rearrange("p (g t) -> p g t", g=NG)[
                    :, :, tau * 128:(tau + 1) * 128]
                wht_a = env.dmaT(o, dsth, xnP_h[:, 0:C], waits=[whh, wll])
                wht_b = env.dmaT(o, dstl, xnP_l[:, 0:C], waits=[wll])
                h2_tr[tau] = [wht_a, wht_b]
        env.mile["S_za"] = env.qtok("G")
        env.mile["h2fm"] = list(h2_tr[NT - 1])
        env.fence(o)
        o.emit(blk)
      _free(scC)

    if "D" in phases:
      scD = []
      with nc.Block("d") as blk:
        o = Ops()
        wgh = [sb(f"dwgh{i}", [128, NG * 128], BF16, scD) for i in range(3)]
        wgl = [sb(f"dwgl{i}", [128, NG * 128], BF16, scD) for i in range(3)]
        wuh = [sb(f"dwuh{i}", [128, NG * 128], BF16, scD) for i in range(3)]
        wul = [sb(f"dwul{i}", [128, NG * 128], BF16, scD) for i in range(3)]
        sg = sb("dsg", [128, 512], F32, scD)
        pr = sb("dpr", [128, 512], F32, scD)
        prh = [sb(f"dprh{i}", [128, 512], BF16, scD) for i in range(2)]
        prl = [sb(f"dprl{i}", [128, 512], BF16, scD) for i in range(2)]
        mm_tok = []
        ev_tok = None
        st_tok = {}
        ld_tok = {}
        for phi in range(NF):
            slot = phi % 3
            wprev = ([mm_tok[phi - 3]] if phi >= 3 else []) + (
                env.mile.get("h2fm") or []) + (env.mile.get("S_wgu") or [])
            wl1 = env.dma(o, wgh[slot][:], dt["S_wg_hi"][:, phi * C:(phi + 1) * C],
                          waits=wprev)
            wl2 = env.dma(o, wgl[slot][:], dt["S_wg_lo"][:, phi * C:(phi + 1) * C],
                          waits=wprev)
            wl3 = env.dma(o, wuh[slot][:], dt["S_wu_hi"][:, phi * C:(phi + 1) * C],
                          eng="scalar", waits=wprev)
            wl4 = env.dma(o, wul[slot][:], dt["S_wu_lo"][:, phi * C:(phi + 1) * C],
                          eng="scalar", waits=wprev)
            wgs, wus = wl1, wl4
            wld = [wl1, wl2, wl3, wl4]
            w0 = wld + ([ev_tok] if ev_tok else [])
            for g in range(NG):
                for J in range(NJ):
                    env.mm3(o, P[J][:],
                            wgh[slot][:, g * 128:(g + 1) * 128],
                            wgl[slot][:, g * 128:(g + 1) * 128],
                            h2fm_hi[:, g * T + J * 512: g * T + J * 512 + 512],
                            h2fm_lo[:, g * T + J * 512: g * T + J * 512 + 512],
                            start=(g == 0), stop=(g == NG - 1),
                            waits=w0 if (g == 0 and J == 0) else (), inc=False)
                for J in range(NJ):
                    env.mm3(o, P[4 + J][:],
                            wuh[slot][:, g * 128:(g + 1) * 128],
                            wul[slot][:, g * 128:(g + 1) * 128],
                            h2fm_hi[:, g * T + J * 512: g * T + J * 512 + 512],
                            h2fm_lo[:, g * T + J * 512: g * T + J * 512 + 512],
                            start=(g == 0), stop=(g == NG - 1),
                            inc=(g == NG - 1 and J == NJ - 1))
            wmm = env.tok("Tn")
            mm_tok.append(wmm)
            for J in range(NJ):
                wsl = env.scn(o, lambda e, J=J: e.activation(out=sg[:], in_=P[J][:],
                                                             func=AF.Silu), waits=[wmm])
                wpr = env.ven(o, lambda e, J=J: e.tensor_tensor(pr[:], sg[:], P[4 + J][:],
                                                                OP.mult), waits=[wsl])
                sl2 = J % 2
                wst = (st_tok[(phi, J - 2)] if (phi, J - 2) in st_tok
                       else (st_tok[(phi - 1, J + 2)] if (phi - 1, J + 2) in st_tok
                             else []))
                whh, wll = env.split(o, prh[sl2][:], prl[sl2][:], pr[:], waits=[wpr] + wst)
                go = phi * T + J * 512
                wp_a = env.dma(o, dt["S_pr_hi"][:, go:go + 512], prh[sl2][:],
                               waits=[whh, wll], eng="gpsimd")
                wp_b = env.dma(o, dt["S_pr_lo"][:, go:go + 512], prl[sl2][:],
                               eng="gpsimd")
                st_tok[(phi, J)] = [wp_a, wp_b]
            ev_tok = env.tok("Vv")
        env.mile["S_pr"] = env.qtok("G")
        env.fence(o)
        o.emit(blk)
      _free(scD)
    _free(scCD)

    if "E1" in phases:
      scE = []
      woh = sb("e1woh", [128, 8 * C], BF16, scE)
      wol = sb("e1wol", [128, 8 * C], BF16, scE)
      wstg = sb("e1w", [128, C], F32, scE)
      ohJ_h = sb("e1ohh", [128, 8 * 512], BF16, scE)
      ohJ_l = sb("e1ohl", [128, 8 * 512], BF16, scE)
      aow = sb("e1aow", [128, C], F32, scE)
      aop = sb("e1aop", [128, C], F32, scE)
      with nc.Block("e1") as blk:
        o = Ops()
        ev_prev = None
        for half in range(2):
            spl_tok = []
            for gi in range(8):
                g = half * 8 + gi
                wws = [spl_tok[gi - 1]] if gi >= 1 else (
                    [env.tok("Tn")] if half == 1 else [])
                wl_ = env.dma(o, wstg[:], dt["Wo"][g * 128:(g + 1) * 128, :], waits=wws)
                _, wll = env.split(o, woh[:, gi * C:(gi + 1) * C],
                                   wol[:, gi * C:(gi + 1) * C], wstg[:], waits=[wl_])
                spl_tok.append(wll)
            wsplit = spl_tok[-1]
            for J in range(NJ):
                w1 = env.dma(o, ohJ_h[:].rearrange("p (g t) -> p g t", g=8),
                             dt["S_oh_hi"][:].rearrange("p (g t) -> p g t", g=NG)
                             [:, half * 8:half * 8 + 8, J * 512:(J + 1) * 512],
                             waits=env.mile.get("S_oh", []) + [env.tok("Tn")])
                w2 = env.dma(o, ohJ_l[:].rearrange("p (g t) -> p g t", g=8),
                             dt["S_oh_lo"][:].rearrange("p (g t) -> p g t", g=NG)
                             [:, half * 8:half * 8 + 8, J * 512:(J + 1) * 512])
                for ti in range(4):
                    tau = J * 4 + ti
                    for gi in range(8):
                        for et in range(4):
                            env.mm3(o, P[et][:],
                                    ohJ_h[:, gi * 512 + ti * 128: gi * 512 + ti * 128 + 128],
                                    ohJ_l[:, gi * 512 + ti * 128: gi * 512 + ti * 128 + 128],
                                    woh[:, gi * C + et * 512: gi * C + et * 512 + 512],
                                    wol[:, gi * C + et * 512: gi * C + et * 512 + 512],
                                    start=(gi == 0), stop=(gi == 7),
                                    waits=([w1, w2, wsplit] + ([ev_prev] if ev_prev else []))
                                    if (gi == 0 and et == 0) else (),
                                    inc=(gi == 7 and et == 3))
                    wmm = env.tok("Tn")
                    if half == 0:
                        for et in range(4):
                            env.ven(o, lambda e, et=et: e.tensor_copy(
                                aow[:, et * 512:(et + 1) * 512], P[et][:]),
                                waits=([wmm] + env.qtok("G")) if et == 0 else ())
                        wev = env.tok("Vv")
                        env.dma(o, dt["S_ao"][tau * 128:(tau + 1) * 128, :], aow[:],
                                waits=[wev], eng="gpsimd")
                    else:
                        wld = env.dma(o, aop[:], dt["S_ao"][tau * 128:(tau + 1) * 128, :],
                                      waits=env.mile.get("e1_half0", []) + [ev_prev])
                        for et in range(4):
                            env.ven(o, lambda e, et=et: e.tensor_tensor(
                                aow[:, et * 512:(et + 1) * 512], P[et][:],
                                aop[:, et * 512:(et + 1) * 512], OP.add),
                                waits=([wmm, wld] + env.qtok("G")) if et == 0 else ())
                        wev = env.tok("Vv")
                        env.dma(o, dt["S_ao"][tau * 128:(tau + 1) * 128, :], aow[:],
                                waits=[wev], eng="gpsimd")
                    ev_prev = env.tok("Vv")
            if half == 0:
                env.mile["e1_half0"] = env.qtok("G")
        env.mile["S_ao"] = env.qtok("G")
        env.fence(o)
        o.emit(blk)
      _free(scE)

    if "E2" in phases:
      scE = []
      NH3 = (15, 15, 14)
      wdh = sb("e2wdh", [128, 15 * C], BF16, scE)
      wdl = sb("e2wdl", [128, 15 * C], BF16, scE)
      wstg = sb("e2w", [128, C], F32, scE)
      prJ_h = sb("e2prh", [128, 15 * 512], BF16, scE)
      prJ_l = sb("e2prl", [128, 15 * 512], BF16, scE)
      zw = sb("e2zw", [128, C], F32, scE)
      zp = sb("e2zp", [128, C], F32, scE)
      aot = sb("e2ao", [128, C], F32, scE)
      xt_ = sb("e2x", [128, C], F32, scE)
      ow = sb("e2ow", [128, C], F32, scE)
      with nc.Block("e2") as blk:
        o = Ops()
        ev_prev = None
        fbase = 0
        for third in range(3):
            nf3 = NH3[third]
            spl_tok = []
            for fi in range(nf3):
                f = fbase + fi
                wws = [spl_tok[fi - 1]] if fi >= 1 else (
                    [env.tok("Tn")] if third > 0 else [])
                wl_ = env.dma(o, wstg[:], dt["w_down"][f * 128:(f + 1) * 128, :], waits=wws)
                _, wll = env.split(o, wdh[:, fi * C:(fi + 1) * C],
                                   wdl[:, fi * C:(fi + 1) * C], wstg[:], waits=[wl_])
                spl_tok.append(wll)
            wsplit = spl_tok[-1]
            for J in range(NJ):
                w1 = env.dma(o, prJ_h[:, 0:nf3 * 512].rearrange("p (g t) -> p g t", g=nf3),
                             dt["S_pr_hi"][:].rearrange("p (g t) -> p g t", g=NF)
                             [:, fbase:fbase + nf3, J * 512:(J + 1) * 512],
                             waits=env.mile.get("S_pr", []) + [env.tok("Tn")])
                w2 = env.dma(o, prJ_l[:, 0:nf3 * 512].rearrange("p (g t) -> p g t", g=nf3),
                             dt["S_pr_lo"][:].rearrange("p (g t) -> p g t", g=NF)
                             [:, fbase:fbase + nf3, J * 512:(J + 1) * 512])
                for ti in range(4):
                    tau = J * 4 + ti
                    for fi in range(nf3):
                        for et in range(4):
                            env.mm3(o, P[et][:],
                                    prJ_h[:, fi * 512 + ti * 128: fi * 512 + ti * 128 + 128],
                                    prJ_l[:, fi * 512 + ti * 128: fi * 512 + ti * 128 + 128],
                                    wdh[:, fi * C + et * 512: fi * C + et * 512 + 512],
                                    wdl[:, fi * C + et * 512: fi * C + et * 512 + 512],
                                    start=(fi == 0), stop=(fi == nf3 - 1),
                                    waits=([w1, w2, wsplit] + ([ev_prev] if ev_prev else []))
                                    if (fi == 0 and et == 0) else (),
                                    inc=(fi == nf3 - 1 and et == 3))
                    wmm = env.tok("Tn")
                    if third < 2:
                        wza = env.dma(o, zp[:], dt["S_za"][tau * 128:(tau + 1) * 128, :],
                                      waits=(env.mile.get("S_za", []) if third == 0
                                             else env.mile.get("e2_prev", [])) + [ev_prev])
                        for et in range(4):
                            env.ven(o, lambda e, et=et: e.tensor_tensor(
                                zw[:, et * 512:(et + 1) * 512], P[et][:],
                                zp[:, et * 512:(et + 1) * 512], OP.add),
                                waits=([wmm, wza] + env.qtok("G")) if et == 0 else ())
                        wev = env.tok("Vv")
                        env.dma(o, dt["S_za"][tau * 128:(tau + 1) * 128, :], zw[:],
                                waits=[wev], eng="gpsimd")
                    else:
                        wza = env.dma(o, zp[:], dt["S_za"][tau * 128:(tau + 1) * 128, :],
                                      waits=env.mile.get("e2_prev", []) + [ev_prev])
                        wao = env.dma(o, aot[:], dt["S_ao"][tau * 128:(tau + 1) * 128, :],
                                      waits=env.mile.get("S_ao", []))
                        wxl = env.dma(o, xt_[:], dt["x"][tau * 128:(tau + 1) * 128, :])
                        for et in range(4):
                            env.ven(o, lambda e, et=et: e.tensor_tensor(
                                zw[:, et * 512:(et + 1) * 512], P[et][:],
                                zp[:, et * 512:(et + 1) * 512], OP.add),
                                waits=([wmm, wza] + env.qtok("G")) if et == 0 else ())
                        env.ven(o, lambda e, tau=tau: e.tensor_scalar(
                            zw[:], zw[:], st["rt"][:, tau:tau + 1], None, OP.mult))
                        env.ven(o, lambda e: e.tensor_tensor(zw[:], zw[:], aot[:], OP.add),
                                waits=[wao])
                        env.ven(o, lambda e, tau=tau: e.tensor_scalar(
                            xt_[:], xt_[:], st["u"][:, tau:tau + 1], st["w"][:, tau:tau + 1],
                            OP.mult, OP.add), waits=[wxl])
                        wfin = env.ven(o, lambda e: e.tensor_tensor(ow[:], zw[:], xt_[:],
                                                                    OP.add))
                        env.dma(o, out[tau * 128:(tau + 1) * 128, :], ow[:],
                                waits=[wfin], eng="gpsimd")
                    ev_prev = env.tok("Vv")
            env.mile["e2_prev"] = env.qtok("G")
            fbase += nf3
        o.em("sync", lambda e: e.nop(), waits=env.qtok("G") + env.qtok("L") + env.qtok("A") + env.qtok("X"))
        env.fence(o)
        o.emit(blk)
      _free(scE)


def kernel(**inputs):
    consts = make_consts()
    nc = build_nc()
    def arr(k):
        return np.ascontiguousarray(np.asarray(inputs[k], dtype=np.float32))
    x = arr("L0_attn_x")
    base = {
        "Wq": arr("L0_attn_Wq"), "Wk": arr("L0_attn_Wk"), "Wv": arr("L0_attn_Wv"),
        "Wo": arr("L0_attn_Wo"),
        "head_scales": arr("L0_attn_head_scales").reshape(1, H),
        "w_router": arr("L0_ffn_w_router"),
        "b_router": arr("L0_ffn_b_router").reshape(1, 1),
        "norm1_w": arr("L0_ffn_norm1_w"), "wv": arr("L0_ffn_wv"), "bv": arr("L0_ffn_bv"),
        "wo": arr("L0_ffn_wo"), "norm2_w": arr("L0_ffn_norm2_w"),
        "w_gate": arr("L0_ffn_w_gate"), "w_up": arr("L0_ffn_w_up"),
        "w_down": arr("L0_ffn_w_down"),
    }
    base.update(consts)
    in_maps = [dict(base, x=np.ascontiguousarray(x[i])) for i in range(8)]
    res = run_bass_kernel_spmd(nc, in_maps, core_ids=list(range(8)))
    return np.stack([r["out"] for r in res.results], axis=0)
